# version 10
# baseline (speedup 1.0000x reference)
import numpy as np
from contextlib import ExitStack

import concourse.bass as bass
import concourse.mybir as mybir
from concourse.bass_utils import run_bass_kernel_spmd
from concourse.bass_core_rust import IndirectOffsetOnAxis

F32 = mybir.dt.float32
BF16 = mybir.dt.bfloat16
I32 = mybir.dt.int32
U32 = mybir.dt.uint32
AF = mybir.ActivationFunctionType
ALU = mybir.AluOpType
AX = mybir.AxisListType

T = 2048
D = 1024
NH = 16
NKV = 4
HDim = 64
HID = 4096
NB = 8
TCH = T // 128
FCH = D // 128
LN1E4 = float(np.log(10000.0))
PI = float(np.pi)
MAGIC = 12582912.0
INV2PI = float(1.0 / (2 * np.pi))
TWOPI = float(2 * np.pi)

QPERM = [0, 4, 1, 5, 2, 6, 3, 7, 8, 12, 9, 13, 10, 14, 11, 15]

ENGS = ("sync", "scalar", "vector", "tensor", "gpsimd")


class Ph:
    def __init__(self, nc):
        self.nc = nc
        self.q = {e: [] for e in ENGS}

    def on(self, eng):
        def deco(fn):
            self.q[eng].append(fn)
            return None
        return deco

    def run(self):
        with self.nc.Block() as blk:
            for e in ENGS:
                fns = self.q[e]
                if fns:
                    getattr(blk, e)(lambda engine, fns=fns: [f(engine) for f in fns])


class Sem:
    def __init__(self, ctx, nc, name):
        self.h = ctx.enter_context(nc.semaphore(name))
        self.n = 0

    def inc(self, k=1):
        self.n += k
        return self.n


class BSem:

    def __init__(self, ctx, nc, name, n=2):
        self.s = [Sem(ctx, nc, f"{name}_{i}") for i in range(n)]
        self.n_ = n

    def slot(self, i):
        return self.s[i % self.n_]


def build(debug=False, stop_after=None):
    nc = bass.Bass("TRN2", target_bir_lowering=False)
    ap = lambda t: t.ap()

    x_in = ap(nc.dram_tensor("x", [T, D], F32, kind="ExternalInput"))
    W_qk = ap(nc.dram_tensor("W_qk", [2 * D, D], F32, kind="ExternalInput"))
    b_qk = ap(nc.dram_tensor("b_qk", [2 * D], F32, kind="ExternalInput"))
    lepe_w = ap(nc.dram_tensor("lepe_w", [D, 1, 3], F32, kind="ExternalInput"))
    lepe_b = ap(nc.dram_tensor("lepe_b", [D], F32, kind="ExternalInput"))
    wq = ap(nc.dram_tensor("wq", [D, D], F32, kind="ExternalInput"))
    wk = ap(nc.dram_tensor("wk", [NKV * HDim, D], F32, kind="ExternalInput"))
    wv = ap(nc.dram_tensor("wv", [NKV * HDim, D], F32, kind="ExternalInput"))
    wo = ap(nc.dram_tensor("wo", [D, D], F32, kind="ExternalInput"))
    q_gain = ap(nc.dram_tensor("q_gain", [NH], F32, kind="ExternalInput"))
    k_gain = ap(nc.dram_tensor("k_gain", [NKV], F32, kind="ExternalInput"))
    fc_s1 = ap(nc.dram_tensor("fc_s1", [NB, D, 64], F32, kind="ExternalInput"))
    fc_s2 = ap(nc.dram_tensor("fc_s2", [NB, HID, 64], F32, kind="ExternalInput"))
    fc_mix = ap(nc.dram_tensor("fc_mix", [NB, HID, D], F32, kind="ExternalInput"))
    fc_logits = ap(nc.dram_tensor("fc_logits", [NB], F32, kind="ExternalInput"))
    pj_s1 = ap(nc.dram_tensor("pj_s1", [NB, HID, 64], F32, kind="ExternalInput"))
    pj_s2 = ap(nc.dram_tensor("pj_s2", [NB, D, 64], F32, kind="ExternalInput"))
    pj_mix = ap(nc.dram_tensor("pj_mix", [NB, D, HID], F32, kind="ExternalInput"))
    pj_logits = ap(nc.dram_tensor("pj_logits", [NB], F32, kind="ExternalInput"))

    x_out = ap(nc.dram_tensor("y", [T, D], F32, kind="ExternalOutput"))

    dscr = lambda name, shape, dt=BF16: ap(nc.dram_tensor(name, shape, dt, kind="Internal"))
    s_x = dscr("s_x", [T, D], F32)
    sp_tm = dscr("sp_tm", [T, D])
    sp_tm2 = dscr("sp_tm2", [T, D])
    sp_hn = dscr("sp_hn", [T, D])
    sp_fm1 = dscr("sp_fm1", [D, T])
    sp_fm2 = dscr("sp_fm2", [D, T])
    a_wq = dscr("a_wq", [D, D])
    a_wk = dscr("a_wk", [NKV * HDim, D])
    a_wv = dscr("a_wv", [NKV * HDim, D])
    a_wo = dscr("a_wo", [D, D])
    a_Wqk = dscr("a_Wqk", [2 * D, D])
    a_fc = [dscr(f"a_fc{k}", [HID, D]) for k in range(2)]
    a_pj = [dscr(f"a_pj{k}", [D, HID]) for k in range(2)]
    b_fc = [dscr(f"b_fc{k}", [D, HID]) for k in range(2)]
    b_pj = [dscr(f"b_pj{k}", [HID, D]) for k in range(2)]
    sp_h2 = dscr("sp_h2", [HID, T])
    sp_kr = dscr("sp_kr", [D, T])

    _inv = 10000.0 ** (-2.0 * np.arange(32, dtype=np.float64) / 64.0)
    _ang = np.outer(_inv, np.arange(T, dtype=np.float64))
    _cosf = np.cos(_ang)
    _sinf = np.sin(_ang)
    _c2 = np.empty((128, T), np.float64)
    _s2 = np.empty((128, T), np.float64)
    for _r0 in (0, 64):
        _c2[_r0:_r0 + 32] = _cosf
        _c2[_r0 + 32:_r0 + 64] = _cosf
        _s2[_r0:_r0 + 32] = _sinf
        _s2[_r0 + 32:_r0 + 64] = -_sinf
    cos2_c = ap(nc.inline_tensor(np.ascontiguousarray(_c2, dtype=np.float32),
                                 name="cos2_c"))
    sin2_c = ap(nc.inline_tensor(np.ascontiguousarray(_s2, dtype=np.float32),
                                 name="sin2_c"))
    tabd = ap(nc.dram_tensor("tabd", [32, 128], I32, kind="Internal"))
    qg_d = ap(nc.dram_tensor("qg_d", [16], F32, kind="Internal"))
    kg_d = ap(nc.dram_tensor("kg_d", [4], F32, kind="Internal"))

    dbg = {}
    if debug:
        dbg["xf"] = ap(nc.dram_tensor("dbg_xf", [T, D], BF16, kind="ExternalOutput"))
        dbg["bsel"] = ap(nc.dram_tensor("dbg_bsel", [1, 16], F32, kind="ExternalOutput"))
        dbg["bfc0"] = ap(nc.dram_tensor("dbg_bfc0", [D, 256], BF16, kind="ExternalOutput"))
        dbg["att"] = ap(nc.dram_tensor("dbg_att", [D, T], BF16, kind="ExternalOutput"))
        dbg["sx"] = ap(nc.dram_tensor("dbg_sx", [T, D], F32, kind="ExternalOutput"))
        dbg["hn"] = ap(nc.dram_tensor("dbg_hn", [T, D], BF16, kind="ExternalOutput"))
        dbg["kr"] = ap(nc.dram_tensor("dbg_kr", [128, 2 * T], BF16, kind="ExternalOutput"))
        dbg["qr"] = ap(nc.dram_tensor("dbg_qr", [128, T], BF16, kind="ExternalOutput"))
        dbg["y"] = ap(nc.dram_tensor("dbg_y", [128, 8 * T], BF16, kind="ExternalOutput"))
        dbg["vo"] = ap(nc.dram_tensor("dbg_vo", [128, 16 * 320], BF16, kind="ExternalOutput"))
        dbg["qn"] = ap(nc.dram_tensor("dbg_qn", [128, T], BF16, kind="ExternalOutput"))
        dbg["awq"] = ap(nc.dram_tensor("dbg_awq", [D, D], BF16, kind="ExternalOutput"))
        dbg["awk"] = ap(nc.dram_tensor("dbg_awk", [256, D], BF16, kind="ExternalOutput"))
        dbg["h2"] = ap(nc.dram_tensor("dbg_h2", [128, T], BF16, kind="ExternalOutput"))
        dbg["pj"] = ap(nc.dram_tensor("dbg_pj", [D, T], BF16, kind="ExternalOutput"))
        dbg["bpj0"] = ap(nc.dram_tensor("dbg_bpj0", [128, 512], BF16, kind="ExternalOutput"))
        dbg["xa"] = ap(nc.dram_tensor("dbg_xa", [T, D], BF16, kind="ExternalOutput"))
        dbg["mac"] = ap(nc.dram_tensor("dbg_mac", [D, T], BF16, kind="ExternalOutput"))

    ctx = ExitStack()
    with ctx:
        E = ctx.enter_context
        S = lambda name: Sem(ctx, nc, name)

        from contextlib import contextmanager

        _sem_uid = [0]
        _nm_uid = [0]

        def mk_tag(base):
            _nm_uid[0] += 1
            return f"{base}{_nm_uid[0]}"

        @contextmanager
        def sem_scope():
            handles = []

            def S2(name):
                s = Sem.__new__(Sem)
                _sem_uid[0] += 1
                s.h = nc.alloc_semaphore(name=f"{name}.{_sem_uid[0]}")
                s.n = 0
                handles.append(s.h)
                return s

            def B2(name, n=2):
                b = BSem.__new__(BSem)
                b.s = [S2(f"{name}_{i}") for i in range(n)]
                b.n_ = n
                return b

            try:
                yield S2, B2
            finally:
                nc.clear_and_free_semaphores(handles)

        ones_f = E(nc.sbuf_tensor("ones_f", [1, 128], F32))
        ones_bf = E(nc.sbuf_tensor("ones_bf", [128, 1], BF16))
        small = E(nc.sbuf_tensor("small", [1, 64], F32))
        lgrow = E(nc.sbuf_tensor("lgrow", [1, 16], F32))
        maxr = E(nc.sbuf_tensor("maxr", [1, 16], F32))
        idxr = E(nc.sbuf_tensor("idxr", [1, 16], U32))
        coefr = E(nc.sbuf_tensor("coefr", [1, 8], F32))
        coefc = E(nc.sbuf_tensor("coefc", [128, 4], F32))
        negc = E(nc.sbuf_tensor("negc", [128, NH], F32))
        qg8 = E(nc.sbuf_tensor("qg8", [1, NH], F32))
        kgr = E(nc.sbuf_tensor("kgr", [1, NKV], F32))
        bqk_c = E(nc.sbuf_tensor("bqk_c", [128, 16], F32))
        bqk1_c = E(nc.sbuf_tensor("bqk1_c", [128, 16], F32))
        tri = E(nc.sbuf_tensor("tri", [128, 128], BF16))
        cosm = E(nc.sbuf_tensor("cosm", [128, FCH], F32))
        sinm = E(nc.sbuf_tensor("sinm", [128, FCH], F32))
        lwc = [E(nc.sbuf_tensor(f"lw{j}", [128, FCH], F32)) for j in range(3)]
        lbc = E(nc.sbuf_tensor("lb", [128, FCH], F32))
        cst = E(nc.sbuf_tensor("cst", [128, 4], F32))
        iv = E(nc.sbuf_tensor("iv", [128, 12], I32))
        ivs = E(nc.sbuf_tensor("ivs", [128, 4], I32))
        qgc = E(nc.sbuf_tensor("qgc", [2, 8], F32))
        qgt = E(nc.sbuf_tensor("qgt", [1, 24], F32))
        kgc = E(nc.sbuf_tensor("kgc", [2, 2], F32))
        ind2 = E(nc.sbuf_tensor("ind2", [2, 128], BF16))
        ones2 = E(nc.sbuf_tensor("ones2", [128, 2], BF16))
        onesr_bf = E(nc.sbuf_tensor("onesr_bf", [1, 128], BF16))
        cos2 = E(nc.sbuf_tensor("cos2", [128, T], BF16))
        sin2 = E(nc.sbuf_tensor("sin2", [128, T], BF16))

        with ExitStack() as c0:
            E0 = c0.enter_context
            itmp = E0(nc.sbuf_tensor("itmp", [128, T + 8], F32))
            itmp2 = E0(nc.sbuf_tensor("itmp2", [32, T], F32))
            itab = E0(nc.sbuf_tensor("itab", [1, 32 * 128], I32))
            cosf = E0(nc.sbuf_tensor("cosf", [32, T], F32))
            sinf = E0(nc.sbuf_tensor("sinf", [32, T], F32))
            pb = [E0(nc.psum_tensor(f"pb{i}", [128, 512], F32)) for i in range(2)]

            p = Ph(nc)
            sl = S("a_l"); sv = S("a_v"); sc = S("a_s"); sg = S("a_g"); st = S("a_t")
            slg = S("a_lg"); sll = S("a_ll"); slog = S("a_log"); sgd = S("a_gd"); slp = S("a_lp"); std_ = S("a_td")

            @p.on("sync")
            def _(s):
                s.dma_start(out=lgrow[0:1, 0:8], in_=fc_logits[None, :]).then_inc(slog.h, 16)
                s.dma_start(out=lgrow[0:1, 8:16], in_=pj_logits[None, :]).then_inc(slog.h, 16)
                s.dma_start(out=small[0:1, 0:16], in_=q_gain[None, :]).then_inc(sl.h, 16)
                s.dma_start(out=kgr[0:1, 0:4], in_=k_gain[None, :]).then_inc(sl.h, 16)
            sl.inc(16 * 2)
            slog.inc(16 * 2)
            sl_small0 = sl.n

            @p.on("scalar")
            def _(s):
                srcb = lepe_b.rearrange("(c p) -> c p", c=FCH)
                for c in range(FCH):
                    s.dma_start(out=lbc[:, c:c + 1], in_=srcb[c][:, None]).then_inc(sll.h, 16)
            sll.inc(16 * 8)
            sll_all = sll.n

            @p.on("gpsimd")
            def _(g):
                g.memset(ones_f[0:1, :], 1.0)
                g.memset(ones_bf[:, :], 1.0)
                g.memset(cst[:, 0:1], PI / 2.0)
                g.memset(cst[:, 1:2], 1e-5)
                g.memset(cst[:, 2:3], 1e-6)
                g.memset(tri[:, :], 1.0)
                g.memset(onesr_bf[0:1, :], 1.0)
                g.memset(ones2[:, 0:2], 0.0)
                g.memset(ind2[0:2, :], 0.0)
                g.memset(itmp[0:1, 0:1], 1.0)
                g.engine_nop().then_inc(sg.h, 1)
                g.wait_ge(sg.h, 1)
                g.memset(ones2[0:64, 0:1], 1.0)
                g.memset(ones2[64:128, 1:2], 1.0)
                g.memset(ind2[0:1, 0:64], 1.0)
                g.affine_select(tri[:, :], tri[:, :], [[1, 128]], ALU.is_ge, 0.0,
                                base=0, channel_multiplier=-1)
                g.iota(itmp[0:32, 0:1], [[0, 1]], base=0, channel_multiplier=1,
                       allow_small_or_imprecise_dtypes=True)
                g.iota(itmp[0:1, 8:8 + T], [[1, T]], base=0, channel_multiplier=0,
                       allow_small_or_imprecise_dtypes=True)
                for c in range(FCH):
                    for (r0, bb) in ((0, 64 * c), (32, 64 * c), (64, 64 * c + 32), (96, 64 * c + 32)):
                        g.iota(cosm[r0:r0 + 32, c:c + 1], [[0, 1]], base=bb, channel_multiplier=1,
                               allow_small_or_imprecise_dtypes=True)
                g.iota(ivs[0:32, 0:1], [[0, 1]], base=0, channel_multiplier=2)
                g.iota(ivs[32:64, 0:1], [[0, 1]], base=1, channel_multiplier=2)
                g.iota(ivs[64:96, 0:1], [[0, 1]], base=64, channel_multiplier=2)
                g.iota(ivs[96:128, 0:1], [[0, 1]], base=65, channel_multiplier=2)
                g.iota(ivs[0:64, 1:2], [[0, 1]], base=0, channel_multiplier=1)
                g.iota(ivs[64:128, 1:2], [[0, 1]], base=256, channel_multiplier=1)
                g.iota(ivs[:, 2:3], [[0, 1]], base=0, channel_multiplier=3)
                R = lambda r, c0, c1: itab[0:1, 128 * r + c0:128 * r + c1]
                for b in range(8):
                    g.iota(R(b, 0, 128), [[1, 128]], base=HID * b, channel_multiplier=0)
                    g.iota(R(8 + b, 0, 128), [[1, 128]], base=D * b, channel_multiplier=0)
                for s_ in range(8):
                    r = 16 + s_
                    for a in range(8):
                        if s_ > 0:
                            g.iota(R(r, 16 * a, 16 * a + s_), [[64, s_]],
                                   base=(16 - s_) * 64 + a, channel_multiplier=0)
                        g.iota(R(r, 16 * a + s_, 16 * a + 16), [[64, 16 - s_]], base=a, channel_multiplier=0)
                for s_ in range(8):
                    r = 24 + s_
                    for cc in range(2):
                        if s_ > 0:
                            g.iota(R(r, 64 * cc, 64 * cc + s_), [[64, s_]],
                                   base=(64 - s_) * 64 + cc, channel_multiplier=0)
                        g.iota(R(r, 64 * cc + s_, 64 * cc + 64), [[64, 64 - s_]], base=cc, channel_multiplier=0)
                g.engine_nop().then_inc(sg.h, 1)
                g.wait_ge(sg.h, 2)
                g.dma_start(out=tabd[:, :], in_=itab[0:1, :]).then_inc(std_.h, 16)
                for c in range(16):
                    g.indirect_dma_start(bqk_c[:, c:c + 1], None, b_qk[:, None],
                                         IndirectOffsetOnAxis(ivs[:, 0:1], 0),
                                         element_offset=128 * c).then_inc(slg.h, 16)
                lw_f = lepe_w.rearrange("d one k -> (d one k)")
                for j in range(3):
                    for c in range(FCH):
                        g.indirect_dma_start(lwc[j][:, c:c + 1], None, lw_f[:, None],
                                             IndirectOffsetOnAxis(ivs[:, 2:3], 0),
                                             element_offset=384 * c + j).then_inc(slp.h, 16)
            sg.inc(2)
            sg_iota = sg.n
            std_.inc(16)
            sg_tabs = std_.n
            sg_p0a = std_
            slg.inc(16 * 16)
            slg_all = slg.n
            slp.inc(16 * 24)
            slp_all = slp.n

            @p.on("vector")
            def _(v):
                v.wait_ge(slog.h, 32)
                v.max(out=maxr[0:1, 0:8], in_=lgrow[0:1, 0:8]).then_inc(sv.h, 1)
                v.max(out=maxr[0:1, 8:16], in_=lgrow[0:1, 8:16]).then_inc(sv.h, 1)
                v.wait_ge(sv.h, 2)
                v.max_index(idxr[0:1, 0:8], maxr[0:1, 0:8], lgrow[0:1, 0:8]).then_inc(sv.h, 1)
                v.max_index(idxr[0:1, 8:16], maxr[0:1, 8:16], lgrow[0:1, 8:16]).then_inc(sv.h, 1)
                v.wait_ge(sv.h, 4)
                v.tensor_tensor(coefr[0:1, 4:5], maxr[0:1, 1:2], maxr[0:1, 0:1], ALU.subtract)
                v.tensor_tensor(coefr[0:1, 5:6], maxr[0:1, 9:10], maxr[0:1, 8:9],
                                ALU.subtract).then_inc(sv.h, 1)
            sv.inc(5)
            sv_topk = sv.n

            @p.on("scalar")
            def _(s):
                s.wait_ge(sv.h, sv_topk)
                s.activation(coefr[0:1, 6:8], coefr[0:1, 4:6], AF.Exp).then_inc(sc.h, 1)
            sc_exp = sc.inc()

            @p.on("vector")
            def _(v):
                v.wait_ge(sc.h, sc_exp)
                v.tensor_single_scalar(coefr[0:1, 2:4], coefr[0:1, 6:8], 1.0, ALU.add).then_inc(sv.h, 1)
                v.wait_ge(sv.h, sv_topk + 1)
                v.reciprocal(coefr[0:1, 0:2], coefr[0:1, 2:4]).then_inc(sv.h, 1)
                v.wait_ge(sv.h, sv_topk + 2)
                v.tensor_scalar(coefr[0:1, 4:6], coefr[0:1, 0:2], -1.0, 1.0,
                                ALU.mult, ALU.add).then_inc(sv.h, 1)
                v.wait_ge(sv.h, sv_topk + 3)
                v.tensor_copy(maxr[0:1, 12:13], coefr[0:1, 0:1])
                v.tensor_copy(maxr[0:1, 13:14], coefr[0:1, 4:5])
                v.tensor_scalar_mul(maxr[0:1, 14:15], coefr[0:1, 1:2], 0.5)
                v.tensor_scalar_mul(maxr[0:1, 15:16], coefr[0:1, 5:6], 0.5).then_inc(sv.h, 1)
            sv.inc(4)
            sv_coef = sv.n

            @p.on("vector")
            def _(v):
                v.wait_ge(sl.h, 32)
                for i, h in enumerate(QPERM):
                    v.tensor_scalar_mul(qg8[0:1, i:i + 1], small[0:1, h:h + 1], 0.125)
                    v.tensor_tensor(small[0:1, 40 + i:41 + i], small[0:1, h:h + 1],
                                    kgr[0:1, h // 4:h // 4 + 1], ALU.mult)
                v.engine_nop().then_inc(sv.h, 1)
                v.wait_ge(sv.h, sv_coef + 1)
                v.tensor_scalar_mul(small[0:1, 24:40], small[0:1, 40:56], -1.0).then_inc(sv.h, 1)
                v.wait_ge(sv.h, sv_coef + 2)
                v.tensor_tensor(small[0:1, 40:56], small[0:1, 40:56], small[0:1, 24:40],
                                ALU.max).then_inc(sv.h, 1)
                v.wait_ge(sv.h, sv_coef + 3)
                v.tensor_scalar_mul(small[0:1, 40:56], small[0:1, 40:56], -8.0).then_inc(sv.h, 1)
                v.wait_ge(slg.h, 16 * 16)
                v.tensor_single_scalar(bqk1_c[:, :], bqk_c[:, :], 1.0, ALU.add).then_inc(sv.h, 1)
            sv.inc(5)
            sv_gain = sv.n

            _sgbase = sgd.n

            @p.on("vector")
            def _(v):
                v.wait_ge(sv.h, sv_gain)
                v.tensor_copy(qgt[0:1, 0:8], qg8[0:1, 0::2])
                v.tensor_copy(qgt[0:1, 8:16], qg8[0:1, 1::2])
                v.tensor_copy(qgt[0:1, 16:18], kgr[0:1, 0:4:2])
                v.tensor_copy(qgt[0:1, 18:20], kgr[0:1, 1:4:2]).then_inc(sv.h, 1)
            sv_deint = sv.inc()

            @p.on("gpsimd")
            def _(g, _sgbase=_sgbase):
                g.wait_ge(sv.h, sv_deint)
                g.dma_start(out=qg_d[None, :], in_=qgt[0:1, 0:16]).then_inc(sgd.h, 16)
                g.dma_start(out=kg_d[None, :], in_=qgt[0:1, 16:20]).then_inc(sgd.h, 16)
                g.wait_ge(sgd.h, _sgbase + 32)
                g.dma_start(out=qgc[:, :],
                            in_=qg_d.rearrange("(r c) -> r c", r=2)).then_inc(sgd.h, 16)
                g.dma_start(out=kgc[:, :],
                            in_=kg_d.rearrange("(r c) -> r c", r=2)).then_inc(sgd.h, 16)
                g.dma_start(out=ind2[1:2, 64:128], in_=ind2[0:1, 0:64]).then_inc(sgd.h, 16)
            sv_gain_post = sgd.inc(16 * 5)

            @p.on("gpsimd")
            def _(g):
                g.wait_ge(sgd.h, sv_gain_post)
                g.wait_ge(std_.h, sg_tabs)
                g.wait_ge(sl.h, 32)
                g.wait_ge(slg.h, slg_all)
                g.wait_ge(slp.h, slp_all)
                g.wait_ge(sll.h, sll_all)


            @p.on("tensor")
            def _(t):
                t.wait_ge(sg.h, 1)
                t.wait_ge(sv.h, sv_coef)
                t.matmul(pb[0][:, 0:4], ones_f[0:1, :], maxr[0:1, 12:16],
                         start=True, stop=True).then_inc(st.h, 1)
                t.wait_ge(sv.h, sv_gain)
                t.matmul(pb[1][:, 0:16], ones_f[0:1, :], small[0:1, 40:56],
                         start=True, stop=True).then_inc(st.h, 1)
            st.inc(2)

            @p.on("vector")
            def _(v):
                v.wait_ge(st.h, 1)
                v.tensor_copy(coefc[:, :], pb[0][:, 0:4]).then_inc(sv.h, 1)
                v.wait_ge(st.h, 2)
                v.tensor_copy(negc[:, :], pb[1][:, 0:16]).then_inc(sv.h, 1)
            sv.inc(2)

            @p.on("scalar")
            def _(s):
                s.wait_ge(sg.h, sg_iota)
                s.activation(itmp[0:32, 0:1], itmp[0:32, 0:1], AF.Exp,
                             scale=-LN1E4 / 32.0).then_inc(sc.h, 1)
                s.activation(sinm[:, :], cosm[:, :], AF.Exp,
                             scale=-LN1E4 / 512.0).then_inc(sc.h, 1)
            sc.inc(2)
            sc_inv = sc.n

            p.run()
            if stop_after == "p0a":
                return _finish_stub(nc, ctx, S, x_out, x_in, dbg, sp_tm, b_fc, idxr, E, debug)

            p = Ph(nc)
            sbD = S("b_d"); s2 = S("b_2")

            @p.on("gpsimd")
            def _(g):
                g.dma_start(out=cos2[:, :], in_=cos2_c[:, :]).then_inc(sbD.h, 16)
                g.dma_start(out=sin2[:, :], in_=sin2_c[:, :]).then_inc(sbD.h, 16)
                g.wait_ge(sbD.h, 32)

            @p.on("scalar")
            def _(s):
                s.activation(cosm[:, :], sinm[:, :], AF.Sin,
                             bias=cst[:, 0:1]).then_inc(s2.h, 1)
                s.wait_ge(s2.h, 1)
                s.activation(sinm[:, :], sinm[:, :], AF.Sin).then_inc(s2.h, 1)
                s.wait_ge(s2.h, 2)
            p.run()
            if stop_after == "p0b":
                return _finish_stub(nc, ctx, S, x_out, x_in, dbg, sp_tm, b_fc, idxr, E, debug)

        with ExitStack() as c0:
            E0 = c0.enter_context
            stg = [E0(nc.sbuf_tensor(f"stg{i}", [128, HID], F32)) for i in range(2)]
            stgb = [E0(nc.sbuf_tensor(f"stgb{i}", [128, HID], BF16)) for i in range(2)]
            trb = [E0(nc.sbuf_tensor(f"trb{i}", [128, HID], BF16, is_transpose_dest=True))
                   for i in range(2)]

            p = Ph(nc)
            sC = S("c_c")
            bLg = BSem(ctx, nc, "c_lg")
            bLp = BSem(ctx, nc, "c_lp")
            bW = BSem(ctx, nc, "c_w")

            wqk_f = W_qk
            wq_f = wq
            cast_jobs = []
            for rc in range(16):
                cast_jobs.append(("gather", (wqk_f, 0, 128 * rc * D), a_Wqk, rc, False))
            for rc in range(8):
                pa = QPERM[2 * rc]
                cast_jobs.append(("gather", (wq_f, 1, 64 * pa * D), a_wq, rc, False))
            for rc in range(2):
                cast_jobs.append(("plain", wk[128 * rc:128 * (rc + 1), :], a_wk, rc, False))
            for rc in range(2):
                cast_jobs.append(("plain", wv[128 * rc:128 * (rc + 1), :], a_wv, rc, False))
            for rc in range(8):
                cast_jobs.append(("plain", wo[128 * rc:128 * (rc + 1), :], a_wo, rc, True))

            for jid, (mode, info, dst, rc, colperm) in enumerate(cast_jobs):
                buf = jid % 2
                sLs = (bLg if mode == "gather" else bLp).slot(jid)
                sWs = bW.slot(jid)
                if mode == "gather":
                    @p.on("gpsimd")
                    def _(g, info=info, buf=buf, jid=jid, sLs=sLs):
                        if jid >= 2:
                            g.wait_ge(sC.h, jid - 1)
                        (srcf, ivc, eo) = info
                        g.indirect_dma_start(stg[buf][:, 0:D], None, srcf,
                                             IndirectOffsetOnAxis(ivs[:, ivc:ivc + 1], 0),
                                             element_offset=eo).then_inc(sLs.h, 16)
                else:
                    @p.on("sync")
                    def _(s, info=info, buf=buf, jid=jid, sLs=sLs):
                        if jid >= 2:
                            s.wait_ge(sC.h, jid - 1)
                        s.dma_start(out=stg[buf][:, 0:D], in_=info).then_inc(sLs.h, 16)
                wL = sLs.inc(16)

                @p.on("vector")
                def _(v, buf=buf, wL=wL, jid=jid, colperm=colperm, sLs=sLs,
                      wW=(sWs.n if jid >= 2 else None), sWs=sWs):
                    v.wait_ge(sLs.h, wL)
                    if wW is not None:
                        v.wait_ge(sWs.h, wW)
                    if colperm:
                        sv_ = stg[buf][:, 0:D].rearrange(
                            "p (half pair jh d) -> p half jh pair d",
                            half=2, pair=2, jh=4, d=64)
                        for half in range(2):
                            for jh in range(4):
                                op = v.tensor_copy(
                                    stgb[buf][:, 128 * (half * 4 + jh):
                                              128 * (half * 4 + jh) + 128],
                                    sv_[:, half, jh, :, :])
                        op.then_inc(sC.h, 1)
                    else:
                        v.tensor_copy(stgb[buf][:, 0:D], stg[buf][:, 0:D]).then_inc(sC.h, 1)
                wC = sC.inc()

                @p.on("scalar")
                def _(s, dst=dst, rc=rc, buf=buf, wC=wC, sWs=sWs):
                    s.wait_ge(sC.h, wC)
                    s.dma_start(out=dst[128 * rc:128 * (rc + 1), :],
                                in_=stgb[buf][:, 0:D]).then_inc(sWs.h, 16)
                sWs.inc(16)

            @p.on("scalar")
            def _(s, t0=bW.s[0].n, t1=bW.s[1].n):
                s.wait_ge(bW.s[0].h, t0)
                s.wait_ge(bW.s[1].h, t1)
            p.run()

            if stop_after == "p0c":
                return _finish_stub(nc, ctx, S, x_out, x_in, dbg, sp_tm, b_fc, idxr, E, debug)
            p = Ph(nc)
            sI = S("d_i"); sC = S("d_c")
            bL = BSem(ctx, nc, "d_l")
            bW = BSem(ctx, nc, "d_w")
            bT = BSem(ctx, nc, "d_t")
            bB = BSem(ctx, nc, "d_b")

            @p.on("sync")
            def _(s):
                s.wait_ge(sg_p0a.h, sg_tabs)
                with ExitStack() as rs:
                    rr = [rs.enter_context(s.register(f"ri{i}")) for i in range(4)]
                    s.reg_load(rr[0], idxr[0:1, 0:1])
                    s.reg_load(rr[1], idxr[0:1, 1:2])
                    s.reg_load(rr[2], idxr[0:1, 8:9])
                    s.reg_load(rr[3], idxr[0:1, 9:10])
                    snp = [s.snap(r, min_val=0, max_val=7) for r in rr]
                    for k in range(2):
                        s.dma_start(out=iv[:, k:k + 1],
                                    in_=tabd[bass.ds(snp[k], 1), :]).then_inc(sI.h, 16)
                        s.dma_start(out=iv[:, 2 + k:3 + k],
                                    in_=tabd[8:][bass.ds(snp[k], 1), :]).then_inc(sI.h, 16)
                        s.dma_start(out=iv[:, 4 + k:5 + k],
                                    in_=tabd[8:][bass.ds(snp[2 + k], 1), :]).then_inc(sI.h, 16)
                        s.dma_start(out=iv[:, 6 + k:7 + k],
                                    in_=tabd[bass.ds(snp[2 + k], 1), :]).then_inc(sI.h, 16)
                        s.dma_start(out=iv[:, 8 + k:9 + k],
                                    in_=tabd[16:][bass.ds(snp[k], 1), :]).then_inc(sI.h, 16)
                        s.dma_start(out=iv[:, 10 + k:11 + k],
                                    in_=tabd[24:][bass.ds(snp[2 + k], 1), :]).then_inc(sI.h, 16)
            sI.inc(16 * 12)
            iv_done = sI.n

            fc_mix_f = fc_mix.rearrange("b o i -> (b o) i")
            pj_mix_f = pj_mix.rearrange("b o i -> (b o) i")
            mix_jobs = [(fc_mix_f, 0, a_fc[0], 32, D), (fc_mix_f, 1, a_fc[1], 32, D),
                        (pj_mix_f, 4, a_pj[0], 8, HID), (pj_mix_f, 5, a_pj[1], 8, HID)]

            cnt = 0
            for (srcf, ivc, dst, nrc, cols) in mix_jobs:
                for rc in range(nrc):
                    buf = cnt % 2
                    sLs = bL.slot(cnt)
                    sWs = bW.slot(cnt)

                    @p.on("gpsimd")
                    def _(g, srcf=srcf, ivc=ivc, rc=rc, buf=buf, cols=cols, cnt=cnt,
                          sLs=sLs):
                        if cnt == 0:
                            g.wait_ge(sI.h, iv_done)
                        if cnt >= 2:
                            g.wait_ge(sC.h, cnt - 1)
                        g.indirect_dma_start(
                            stg[buf][:, 0:cols], None, srcf,
                            IndirectOffsetOnAxis(iv[:, ivc:ivc + 1], 0),
                            element_offset=128 * rc * cols).then_inc(sLs.h, 16)
                    wL = sLs.inc(16)

                    @p.on("vector")
                    def _(v, buf=buf, wL=wL, cols=cols, cnt=cnt, sLs=sLs, sWs=sWs,
                          wW=(sWs.n if cnt >= 2 else None)):
                        v.wait_ge(sLs.h, wL)
                        if wW is not None:
                            v.wait_ge(sWs.h, wW)
                        v.tensor_copy(stgb[buf][:, 0:cols], stg[buf][:, 0:cols]).then_inc(sC.h, 1)
                    wC = sC.inc()

                    @p.on("scalar")
                    def _(s, dst=dst, rc=rc, buf=buf, wC=wC, cols=cols, sWs=sWs):
                        s.wait_ge(sC.h, wC)
                        s.dma_start(out=dst[128 * rc:128 * (rc + 1), 0:cols],
                                    in_=stgb[buf][:, 0:cols]).then_inc(sWs.h, 16)
                    sWs.inc(16)
                    cnt += 1
            mix_w_tot = (bW.s[0].n, bW.s[1].n)

            tcnt = 0
            jobs_t = [("fc", k, ic) for k in range(2) for ic in range(FCH)] + \
                     [("pj", k, ic) for k in range(2) for ic in range(HID // 128)]
            for (kind, k, ic) in jobs_t:
                buf = tcnt % 2
                sTs = bT.slot(tcnt)
                sBs = bB.slot(tcnt)
                ntr = 8 if kind == "fc" else 2
                src_t = a_fc[k] if kind == "fc" else a_pj[k]
                dst_t = b_fc[k] if kind == "fc" else b_pj[k]
                ivcol = (8 + k) if kind == "fc" else (10 + k)
                eoff = 8 * ic * HID if kind == "fc" else 2 * ic * D
                width = HID if kind == "fc" else D

                @p.on("sync")
                def _(s, buf=buf, ntr=ntr, src_t=src_t, ic=ic, sTs=sTs,
                      warv=(sBs.n if tcnt >= 2 else None), sBs=sBs,
                      first=(tcnt == 0), mt=mix_w_tot):
                    if first:
                        s.wait_ge(bW.s[0].h, mt[0])
                        s.wait_ge(bW.s[1].h, mt[1])
                    if warv is not None:
                        s.wait_ge(sBs.h, warv)
                    for og in range(ntr):
                        s.dma_start(out=trb[buf][:, 512 * og:512 * (og + 1)],
                                    in_=src_t[512 * og:512 * (og + 1),
                                              128 * ic:128 * (ic + 1)],
                                    transpose=True).then_inc(sTs.h, 16)
                wT = sTs.inc(16 * ntr)

                @p.on("gpsimd")
                def _(g, dst_t=dst_t, ivcol=ivcol, eoff=eoff, buf=buf, wT=wT,
                      width=width, sTs=sTs, sBs=sBs):
                    g.wait_ge(sTs.h, wT)
                    g.indirect_dma_start(
                        dst_t, IndirectOffsetOnAxis(iv[:, ivcol:ivcol + 1], 0),
                        trb[buf][:, 0:width], None,
                        element_offset=eoff).then_inc(sBs.h, 16)
                sBs.inc(16)
                tcnt += 1

            @p.on("gpsimd")
            def _(g, t0=bB.s[0].n, t1=bB.s[1].n):
                g.wait_ge(bB.s[0].h, t0)
                g.wait_ge(bB.s[1].h, t1)
            p.run()

        def emit_s12(tag, E1, which):
            if which == "fc":
                n1, n2 = 8, 32
                s1_f = fc_s1.rearrange("b r c -> (b r) c")
                s2_f = fc_s2.rearrange("b r c -> (b r) c")
                iv1, iv2, cf0 = (2, 3), (0, 1), 0
            else:
                n1, n2 = 32, 8
                s1_f = pj_s1.rearrange("b r c -> (b r) c")
                s2_f = pj_s2.rearrange("b r c -> (b r) c")
                iv1, iv2, cf0 = (6, 7), (4, 5), 2
            s1t = [E1(nc.sbuf_tensor(f"s1{tag}{k}", [128, n1 * 128], BF16))
                   for k in range(2)]
            s2t = [E1(nc.sbuf_tensor(f"s2{tag}{k}", [128, n2 * 128], BF16))
                   for k in range(2)]
            with ExitStack() as c1:
                E2 = c1.enter_context
                S2, B2 = E2(sem_scope())
                stg = [E2(nc.sbuf_tensor(f"sg{tag}{i}", [128, 64], F32)) for i in range(2)]
                p = Ph(nc)
                sC = S2("e_c"); sG = S2("e_g"); sF = S2("e_f")
                bL = B2("e_l")

                s_jobs = [(s1_f, iv1[0], s1t[0], n1, None),
                          (s1_f, iv1[1], s1t[1], n1, None),
                          (s2_f, iv2[0], s2t[0], n2, cf0),
                          (s2_f, iv2[1], s2t[1], n2, cf0 + 1)]

                @p.on("vector")
                def _(v):
                    for k in range(2):
                        v.memset(s1t[k][:, :], 0.0)
                        v.memset(s2t[k][:, :], 0.0)
                    v.engine_nop().then_inc(sG.h, 1)

                cnt = 0
                for (srcf, ivc, dstt, npair, cfc) in s_jobs:
                    for pr in range(npair):
                        buf = cnt % 2
                        sLs = bL.slot(cnt)

                        @p.on("gpsimd")
                        def _(g, srcf=srcf, ivc=ivc, pr=pr, buf=buf, cnt=cnt, sLs=sLs):
                            if cnt >= 2:
                                g.wait_ge(sC.h, cnt - 1)
                            g.indirect_dma_start(
                                stg[buf][:, 0:64], None, srcf,
                                IndirectOffsetOnAxis(iv[:, ivc:ivc + 1], 0),
                                element_offset=128 * pr * 64).then_inc(sLs.h, 16)
                        wL = sLs.inc(16)

                        @p.on("vector")
                        def _(v, dstt=dstt, pr=pr, buf=buf, wL=wL, first=(cnt == 0),
                              sLs=sLs):
                            v.wait_ge(sLs.h, wL)
                            if first:
                                v.wait_ge(sG.h, 1)
                            v.tensor_copy(dstt[0:64, 128 * pr:128 * pr + 64],
                                          stg[buf][0:64, 0:64])
                            v.tensor_copy(dstt[64:128, 128 * pr + 64:128 * pr + 128],
                                          stg[buf][64:128, 0:64]).then_inc(sC.h, 1)
                        sC.inc()
                        cnt += 1
                    if cfc is not None:
                        @p.on("vector")
                        def _(v, dstt=dstt, cfc=cfc, tot=sC.n):
                            v.wait_ge(sC.h, tot)
                            v.tensor_scalar(dstt[:, :], dstt[:, :],
                                            coefc[:, cfc:cfc + 1], None,
                                            ALU.mult).then_inc(sF.h, 1)
                        sF.inc()

                @p.on("gpsimd")
                def _(g, tot=sC.n, ftot=sF.n):
                    g.wait_ge(sC.h, tot)
                    g.wait_ge(sF.h, ftot)
                p.run()
            return s1t, s2t

        def emit_banks(tag, hn_tm, pj_out_fm):
            NOC = HID // 128

            with ExitStack() as c1:
                E1 = c1.enter_context
                s1fc, s2fc = emit_s12(tag + "f", E1, "fc")
                S2, B2 = E1(sem_scope())
                hnf = E1(nc.sbuf_tensor(f"bk{tag}_hn", [128, 8 * T], BF16))
                hbf = [E1(nc.sbuf_tensor(f"bk{tag}_hb{k}", [128, 8 * T], BF16))
                       for k in range(2)]
                btile = [E1(nc.sbuf_tensor(f"bk{tag}_w{i}", [128, 8 * 128], BF16))
                         for i in range(2)]
                msb = [E1(nc.sbuf_tensor(f"bk{tag}_m{k}", [128, 4 * 512], BF16))
                       for k in range(2)]
                ub = [E1(nc.sbuf_tensor(f"bk{tag}_u{i}", [128, 512], F32))
                      for i in range(2)]
                h2b = [E1(nc.sbuf_tensor(f"bk{tag}_2{i}", [128, 512], BF16))
                       for i in range(2)]
                PM = [E1(nc.psum_tensor(f"bk{tag}_pm{i}", [128, 512], F32))
                      for i in range(4)]
                PB = [E1(nc.psum_tensor(f"bk{tag}_pb{i}", [128, 512], F32))
                      for i in range(2)]

                p = Ph(nc)
                sT = S2("bkt"); sA = S2("bka"); sv = S2("bkv"); sG = S2("bkg")
                bX = B2("bkx"); bW = B2("bkw"); bS = B2("bkh")

                @p.on("sync")
                def _(s):
                    for fc in range(8):
                        for tg in range(4):
                            s.dma_start(
                                out=hnf[:, fc * T + 512 * tg:fc * T + 512 * (tg + 1)],
                                in_=hn_tm[512 * tg:512 * (tg + 1),
                                          128 * fc:128 * (fc + 1)],
                                transpose=True).then_inc(bX.s[0].h, 16)
                bX.s[0].inc(16 * 32)
                hn_done = bX.s[0].n

                pm_free = [None] * 4
                cnt = 0
                for k in range(2):
                    for pr in range(8):
                        for tcb in range(4):
                            bank = cnt % 4

                            @p.on("tensor")
                            def _(t, k=k, pr=pr, tcb=tcb, bank=bank,
                                  first=(cnt == 0), war=pm_free[bank]):
                                if first:
                                    t.wait_ge(bX.s[0].h, hn_done)
                                if war is not None:
                                    t.wait_ge(war[0].h, war[1])
                                t.matmul(PM[bank][:, :],
                                         s1fc[k][:, 128 * pr:128 * (pr + 1)],
                                         hnf[:, pr * T + 512 * tcb:
                                             pr * T + 512 * (tcb + 1)],
                                         start=True, stop=True).then_inc(sT.h, 1)
                            mm = sT.inc()

                            eng = "scalar"
                            esem = sA

                            @p.on(eng)
                            def _(e, k=k, pr=pr, tcb=tcb, bank=bank, mm=mm,
                                  esem=esem, sc=(eng == "scalar")):
                                e.wait_ge(sT.h, mm)
                                dst = hbf[k][:, pr * T + 512 * tcb:
                                             pr * T + 512 * (tcb + 1)]
                                if sc:
                                    e.activation(dst, PM[bank][:, :],
                                                 AF.Copy).then_inc(esem.h, 1)
                                else:
                                    e.tensor_copy(dst, PM[bank][:, :]).then_inc(esem.h, 1)
                            pm_free[bank] = (esem, esem.inc())
                            cnt += 1
                bd1_allA = sA.n
                bd1_allG = sG.n

                bfc_v = [b_fc[kk].rearrange("(i p) c -> p i c", p=128)
                         for kk in range(2)]
                btile_free = [None, None]
                msb_free = {}
                pb_free = [None, None]
                ub_free = [None, None]
                h2b_free = [None, None]
                ld = 0
                ucnt = 0
                for oc in range(NOC):
                    for k in range(2):
                        buf = ld % 2
                        sWs = bW.slot(ld)

                        @p.on("sync")
                        def _(s, oc=oc, k=k, buf=buf, war=btile_free[buf], sWs=sWs):
                            if war is not None:
                                s.wait_ge(sT.h, war)
                            s.dma_start(
                                out=btile[buf][:, :].rearrange("p (i c) -> p i c",
                                                               c=128),
                                in_=bfc_v[k][:, :, 128 * oc:128 * (oc + 1)]
                            ).then_inc(sWs.h, 16)
                        wW = sWs.inc(16)

                        for tcb in range(4):
                            bank = (4 * ld + tcb) % 4

                            @p.on("tensor")
                            def _(t, k=k, tcb=tcb, buf=buf, bank=bank, wW=wW,
                                  sWs=sWs, war=pm_free[bank],
                                  first=(ld == 0 and tcb == 0)):
                                if first:
                                    t.wait_ge(sA.h, bd1_allA)
                                    t.wait_ge(sG.h, bd1_allG)
                                if tcb == 0:
                                    t.wait_ge(sWs.h, wW)
                                if war is not None:
                                    t.wait_ge(war[0].h, war[1])
                                for ic in range(8):
                                    mmx = t.matmul(
                                        PM[bank][:, :],
                                        btile[buf][:, 128 * ic:128 * (ic + 1)],
                                        hbf[k][:, ic * T + 512 * tcb:
                                               ic * T + 512 * (tcb + 1)],
                                        start=(ic == 0), stop=(ic == 7))
                                mmx.then_inc(sT.h, 1)
                            mm = sT.inc()

                            @p.on("scalar")
                            def _(s, k=k, tcb=tcb, bank=bank, mm=mm,
                                  warm=msb_free.get((k, tcb))):
                                s.wait_ge(sT.h, mm)
                                if warm is not None:
                                    s.wait_ge(sT.h, warm)
                                s.activation(msb[k][:, 512 * tcb:512 * (tcb + 1)],
                                             PM[bank][:, :],
                                             AF.Copy).then_inc(sA.h, 1)
                            pm_free[bank] = (sA, sA.inc())
                        btile_free[buf] = sT.n
                        ld += 1
                    mix_done = sA.n
                    for tcb in range(4):
                        par = ucnt % 2

                        @p.on("tensor")
                        def _(t, oc=oc, tcb=tcb, par=par, md=mix_done,
                              war=pb_free[par]):
                            t.wait_ge(sA.h, md)
                            if war is not None:
                                t.wait_ge(sv.h, war)
                            t.matmul(PB[par][:, :],
                                     s2fc[0][:, 128 * oc:128 * (oc + 1)],
                                     msb[0][:, 512 * tcb:512 * (tcb + 1)],
                                     start=True, stop=False)
                            t.matmul(PB[par][:, :],
                                     s2fc[1][:, 128 * oc:128 * (oc + 1)],
                                     msb[1][:, 512 * tcb:512 * (tcb + 1)],
                                     start=False, stop=True).then_inc(sT.h, 1)
                        for k in range(2):
                            msb_free[(k, tcb)] = sT.n + 1
                        bd2_mm = sT.inc()

                        @p.on("vector")
                        def _(v, par=par, bd2_mm=bd2_mm, war=ub_free[par]):
                            v.wait_ge(sT.h, bd2_mm)
                            if war is not None:
                                v.wait_ge(sA.h, war)
                            v.tensor_single_scalar(ub[par][:, :], PB[par][:, :],
                                                   0.0, ALU.max).then_inc(sv.h, 1)
                        pb_free[par] = sv.inc()

                        @p.on("scalar")
                        def _(s, par=par, rl=pb_free[par], war=h2b_free[par]):
                            s.wait_ge(sv.h, rl)
                            if war is not None:
                                s.wait_ge(war[0].h, war[1])
                            s.activation(h2b[par][:, :], ub[par][:, :],
                                         AF.Square).then_inc(sA.h, 1)
                        ub_free[par] = sA.inc()

                        sSs = bS.slot(ucnt)

                        @p.on("gpsimd")
                        def _(g, oc=oc, tcb=tcb, par=par, sq=ub_free[par], sSs=sSs):
                            g.wait_ge(sA.h, sq)
                            g.dma_start(out=sp_h2[128 * oc:128 * (oc + 1),
                                                  512 * tcb:512 * (tcb + 1)],
                                        in_=h2b[par][:, :]).then_inc(sSs.h, 16)
                        h2b_free[par] = (sSs, sSs.inc(16))
                        ucnt += 1

                @p.on("gpsimd")
                def _(g, t0=bS.s[0].n, t1=bS.s[1].n):
                    g.wait_ge(bS.s[0].h, t0)
                    g.wait_ge(bS.s[1].h, t1)
                p.run()

            with ExitStack() as c1:
                E1 = c1.enter_context
                s1pj, s2pj = emit_s12(tag + "p", E1, "pj")
                S2, B2 = E1(sem_scope())
                h2c = E1(nc.sbuf_tensor(f"bk{tag}_hc", [128, NOC * 512], BF16))
                hbp = [E1(nc.sbuf_tensor(f"bk{tag}_hp{k}", [128, NOC * 512], BF16))
                       for k in range(2)]
                ptile = [E1(nc.sbuf_tensor(f"bk{tag}_v{i}", [128, NOC * 128], BF16))
                         for i in range(2)]
                mpj = [E1(nc.sbuf_tensor(f"bk{tag}_q{k}", [128, 8 * 512], BF16))
                       for k in range(2)]
                pjb = [E1(nc.sbuf_tensor(f"bk{tag}_j{i}", [128, 512], BF16))
                       for i in range(2)]
                PA = [E1(nc.psum_tensor(f"bk{tag}_pa{i}", [128, 512], F32))
                      for i in range(4)]
                PX = [E1(nc.psum_tensor(f"bk{tag}_px{i}", [128, 512], F32))
                      for i in range(2)]
                PB = [E1(nc.psum_tensor(f"bk{tag}_pc{i}", [128, 512], F32))
                      for i in range(2)]

                p = Ph(nc)
                sT = S2("bpt"); sA = S2("bpa"); sG = S2("bpg")
                bH = B2("bph"); bW = B2("bpw"); bS = B2("bps")

                pa_free = [None] * 4
                px_free = [None, None]
                pb_free = [None, None]
                ptile_free = [None, None]
                pjb_free = [None, None]
                h2c_free = None
                hbp_free = None
                mpj_free = None
                ld = 0
                for tcb in range(4):
                    sHs = bH.slot(tcb)

                    @p.on("sync")
                    def _(s, tcb=tcb, war=h2c_free, sHs=sHs):
                        if war is not None:
                            s.wait_ge(sT.h, war)
                        for ic in range(NOC):
                            s.dma_start(out=h2c[:, 512 * ic:512 * (ic + 1)],
                                        in_=sp_h2[128 * ic:128 * (ic + 1),
                                                  512 * tcb:512 * (tcb + 1)]
                                        ).then_inc(sHs.h, 16)
                    wH = sHs.inc(16 * NOC)

                    bd1k = []
                    bcnt = 2 * NOC * tcb
                    for k in range(2):
                        for pr in range(NOC):
                            bank = bcnt % 4

                            @p.on("tensor")
                            def _(t, k=k, pr=pr, bank=bank, wH=wH, sHs=sHs,
                                  first=(k == 0 and pr == 0), war=pa_free[bank]):
                                if first:
                                    t.wait_ge(sHs.h, wH)
                                if war is not None:
                                    t.wait_ge(war[0].h, war[1])
                                t.matmul(PA[bank][:, :],
                                         s1pj[k][:, 128 * pr:128 * (pr + 1)],
                                         h2c[:, 512 * pr:512 * (pr + 1)],
                                         start=True, stop=True).then_inc(sT.h, 1)
                            mm = sT.inc()

                            eng = "scalar"
                            esem = sA

                            @p.on(eng)
                            def _(e, k=k, pr=pr, bank=bank, mm=mm, war=hbp_free,
                                  esem=esem, sc=(eng == "scalar")):
                                e.wait_ge(sT.h, mm)
                                if war is not None:
                                    e.wait_ge(sT.h, war)
                                dst = hbp[k][:, 512 * pr:512 * (pr + 1)]
                                if sc:
                                    e.activation(dst, PA[bank][:, :],
                                                 AF.Copy).then_inc(esem.h, 1)
                                else:
                                    e.tensor_copy(dst, PA[bank][:, :]).then_inc(esem.h, 1)
                            pa_free[bank] = (esem, esem.inc())
                            bcnt += 1
                        bd1k.append((sA.n, sG.n))
                    h2c_free = sT.n
                    hbp_free = None

                    bpj_v = [b_pj[kk].rearrange("(i p) c -> p i c", p=128)
                             for kk in range(2)]
                    for k in range(2):
                        for oc in range(8):
                            buf = ld % 2
                            sWs = bW.slot(ld)

                            @p.on("sync")
                            def _(s, k=k, oc=oc, buf=buf, war=ptile_free[buf],
                                  sWs=sWs):
                                if war is not None:
                                    s.wait_ge(sT.h, war)
                                s.dma_start(
                                    out=ptile[buf][:, :].rearrange(
                                        "p (i c) -> p i c", c=128),
                                    in_=bpj_v[k][:, :, 128 * oc:128 * (oc + 1)]
                                ).then_inc(sWs.h, 16)
                            wW = sWs.inc(16)

                            bank = ld % 2

                            @p.on("tensor")
                            def _(t, k=k, oc=oc, buf=buf, bank=bank, wW=wW,
                                  sWs=sWs, war=px_free[bank], bd1=bd1k[k],
                                  first=(oc == 0)):
                                if first:
                                    t.wait_ge(sA.h, bd1[0])
                                    t.wait_ge(sG.h, bd1[1])
                                t.wait_ge(sWs.h, wW)
                                if war is not None:
                                    t.wait_ge(sA.h, war)
                                for ic in range(NOC):
                                    mmx = t.matmul(
                                        PX[bank][:, :],
                                        ptile[buf][:, 128 * ic:128 * (ic + 1)],
                                        hbp[k][:, 512 * ic:512 * (ic + 1)],
                                        start=(ic == 0), stop=(ic == NOC - 1))
                                mmx.then_inc(sT.h, 1)
                            mm = sT.inc()

                            @p.on("scalar")
                            def _(s, k=k, oc=oc, bank=bank, mm=mm, war=mpj_free):
                                s.wait_ge(sT.h, mm)
                                if war is not None:
                                    s.wait_ge(sT.h, war)
                                s.activation(mpj[k][:, 512 * oc:512 * (oc + 1)],
                                             PX[bank][:, :],
                                             AF.Copy).then_inc(sA.h, 1)
                            px_free[bank] = sA.inc()
                            ptile_free[buf] = sT.n
                            ld += 1
                    hbp_free = sT.n
                    mpj_free = None
                    mix_all = sA.n

                    for pr in range(8):
                        par = (8 * tcb + pr) % 2

                        @p.on("tensor")
                        def _(t, pr=pr, par=par, ma=mix_all, war=pb_free[par]):
                            t.wait_ge(sA.h, ma)
                            if war is not None:
                                t.wait_ge(sA.h, war)
                            t.matmul(PB[par][:, :],
                                     s2pj[0][:, 128 * pr:128 * (pr + 1)],
                                     mpj[0][:, 512 * pr:512 * (pr + 1)],
                                     start=True, stop=False)
                            t.matmul(PB[par][:, :],
                                     s2pj[1][:, 128 * pr:128 * (pr + 1)],
                                     mpj[1][:, 512 * pr:512 * (pr + 1)],
                                     start=False, stop=True).then_inc(sT.h, 1)
                        bd2_mm = sT.inc()

                        @p.on("scalar")
                        def _(s, par=par, bd2_mm=bd2_mm, war=pjb_free[par]):
                            s.wait_ge(sT.h, bd2_mm)
                            if war is not None:
                                s.wait_ge(war[0].h, war[1])
                            s.activation(pjb[par][:, :], PB[par][:, :],
                                         AF.Copy).then_inc(sA.h, 1)
                        pb_free[par] = sA.inc()

                        sSs = bS.slot(8 * tcb + pr)

                        @p.on("gpsimd")
                        def _(g, pr=pr, tcb=tcb, par=par, ev=pb_free[par], sSs=sSs):
                            g.wait_ge(sA.h, ev)
                            g.dma_start(out=pj_out_fm[128 * pr:128 * (pr + 1),
                                                      512 * tcb:512 * (tcb + 1)],
                                        in_=pjb[par][:, :]).then_inc(sSs.h, 16)
                        pjb_free[par] = (sSs, sSs.inc(16))
                    mpj_free = sT.n

                @p.on("gpsimd")
                def _(g, t0=bS.s[0].n, t1=bS.s[1].n):
                    g.wait_ge(bS.s[0].h, t0)
                    g.wait_ge(bS.s[1].h, t1)
                p.run()

        def emit_linattn(xa_tm, mac_fm):
            with ExitStack() as c1:
                E1 = c1.enter_context
                S2, B2 = c1.enter_context(sem_scope())
                xv = E1(nc.sbuf_tensor("la_xv", [128, 8 * T], BF16))
                wqkT = E1(nc.sbuf_tensor("la_wt", [128, 8 * 2 * D], BF16))
                sms = E1(nc.sbuf_tensor("la_sm", [128, FCH], F32))
                ksums = E1(nc.sbuf_tensor("la_ks", [128, 32], F32))
                kwa = E1(nc.sbuf_tensor("la_ka", [128, 8], F32))
                kwb = E1(nc.sbuf_tensor("la_kb", [128, 8], F32))
                kme8 = E1(nc.sbuf_tensor("la_km", [128, 8], F32))
                kmbs = E1(nc.sbuf_tensor("la_kc", [128, 16], BF16))
                zb = [E1(nc.sbuf_tensor(f"la_z{i}", [128, 512], F32)) for i in range(2)]
                eb = [E1(nc.sbuf_tensor(f"la_e{i}", [128, 512], F32)) for i in range(2)]
                rb = [E1(nc.sbuf_tensor(f"la_r{i}", [128, 512], F32)) for i in range(2)]
                qb = [E1(nc.sbuf_tensor(f"la_q{i}", [128, 512], BF16)) for i in range(2)]
                swp = [E1(nc.sbuf_tensor(f"la_s{i}", [128, 512], BF16)) for i in range(2)]
                rw1 = [E1(nc.sbuf_tensor(f"la_1{i}", [128, 512], F32)) for i in range(2)]
                rw2 = [E1(nc.sbuf_tensor(f"la_2{i}", [128, 512], F32)) for i in range(2)]
                qrb = [E1(nc.sbuf_tensor(f"la_o{i}", [128, 512], BF16)) for i in range(2)]
                xat = [E1(nc.sbuf_tensor(f"la_x{i}", [128, D], BF16)) for i in range(2)]
                krt = [E1(nc.sbuf_tensor(f"la_k{i}", [128, D], BF16,
                                         is_transpose_dest=True)) for i in range(2)]
                kvs = E1(nc.sbuf_tensor("la_kv", [128, 512], BF16))
                kvacc = E1(nc.sbuf_tensor("la_kf", [128, 512], F32))
                zf = E1(nc.sbuf_tensor("la_zf", [2, 512], F32))
                zf2 = E1(nc.sbuf_tensor("la_z2", [2, 512], F32))
                zrb = E1(nc.sbuf_tensor("la_zr", [2, 512], BF16))
                peb = E1(nc.sbuf_tensor("la_pe", [128, T], F32))
                pebt = E1(nc.sbuf_tensor("la_pt", [128, T], F32))
                azf = [E1(nc.sbuf_tensor(f"la_a{i}", [128, 512], F32)) for i in range(2)]
                ob = [E1(nc.sbuf_tensor(f"la_b{i}", [128, 512], BF16)) for i in range(2)]
                PK = [E1(nc.psum_tensor(f"la_pk{i}", [128, 512], F32)) for i in range(2)]
                PKV = [E1(nc.psum_tensor(f"la_pkv{i}", [128, 512], F32))
                       for i in range(2)]
                PD = E1(nc.psum_tensor("la_pd", [2, 512], F32))
                PZ = E1(nc.psum_tensor("la_pz", [128, 512], F32))
                PAT = [E1(nc.psum_tensor(f"la_pa{i}", [128, 512], F32)) for i in range(2)]

                p = Ph(nc)
                sT = S2("la_t"); sA = S2("la_a"); sv = S2("la_v"); sP = S2("la_p")
                bX = B2("la_x"); bK = B2("la_kr"); bL = B2("la_l"); bO = B2("la_o")

                pk_free = [None, None]
                zbrb_free = [None, None]
                qb_free = [None, None]
                qrb_war = [None, None]
                pd_free = [None]
                pz_free = [None]
                pat_free = [None, None]
                azf_free = [None, None]
                ob_free = [None, None]
                krb_sp = {}
                rope_read = {}

                @p.on("vector")
                def _(v):
                    v.memset(cst[:, 3:4], 0.0)
                    v.tensor_scalar_mul(sms[0:32, :], sinm[0:32, :], -1.0)
                    v.tensor_copy(sms[32:64, :], sinm[32:64, :])
                    v.tensor_scalar_mul(sms[64:96, :], sinm[64:96, :], -1.0)
                    v.tensor_copy(sms[96:128, :], sinm[96:128, :]).then_inc(sv.h, 1)
                prep_done = sv.inc()

                @p.on("sync")
                def _(s):
                    for fc in range(8):
                        for tg in range(4):
                            s.dma_start(
                                out=xv[:, fc * T + 512 * tg:fc * T + 512 * (tg + 1)],
                                in_=xa_tm[512 * tg:512 * (tg + 1),
                                          128 * fc:128 * (fc + 1)],
                                transpose=True).then_inc(bX.s[0].h, 16)
                    for ic in range(8):
                        for og in range(4):
                            s.dma_start(
                                out=wqkT[:, ic * 2 * D + 512 * og:
                                         ic * 2 * D + 512 * (og + 1)],
                                in_=a_Wqk[512 * og:512 * (og + 1),
                                          128 * ic:128 * (ic + 1)],
                                transpose=True).then_inc(bX.s[0].h, 16)
                bX.s[0].inc(16 * 64)
                tp_done = bX.s[0].n

                def proj_mm(oc, tcb, bank, extra_wait):
                    @p.on("tensor")
                    def _(t, oc=oc, tcb=tcb, bank=bank, extra_wait=extra_wait,
                          war=pk_free[bank]):
                        for (sem, val) in extra_wait:
                            t.wait_ge(sem.h, val)
                        if war is not None:
                            t.wait_ge(sv.h, war)
                        for ic in range(8):
                            mmx = t.matmul(
                                PK[bank][:, :],
                                wqkT[:, ic * 2 * D + 128 * oc:
                                     ic * 2 * D + 128 * (oc + 1)],
                                xv[:, ic * T + 512 * tcb:ic * T + 512 * (tcb + 1)],
                                start=(ic == 0), stop=(ic == 7))
                        mmx.then_inc(sT.h, 1)
                    return sT.inc()

                def elu(oc, bank, buf, mm, accum_col, war_q):
                    @p.on("vector")
                    def _(v, oc=oc, bank=bank, buf=buf, mm=mm,
                          war=zbrb_free[buf], prep=prep_done):
                        v.wait_ge(sT.h, mm)
                        v.wait_ge(sv.h, prep)
                        if war is not None:
                            v.wait_ge(sA.h, war)
                        v.tensor_scalar(zb[buf][:, :], PK[bank][:, :],
                                        bqk_c[:, oc:oc + 1], cst[:, 3:4],
                                        ALU.add, ALU.min)
                        v.tensor_scalar(rb[buf][:, :], PK[bank][:, :],
                                        bqk_c[:, oc:oc + 1], cst[:, 3:4],
                                        ALU.add, ALU.max).then_inc(sv.h, 1)
                    zr_done = sv.inc()
                    pk_free[bank] = zr_done

                    @p.on("scalar")
                    def _(s, buf=buf, zr_done=zr_done):
                        s.wait_ge(sv.h, zr_done)
                        s.activation(eb[buf][:, :], zb[buf][:, :],
                                     AF.Exp).then_inc(sA.h, 1)
                    e_done = sA.inc()
                    zbrb_free[buf] = e_done

                    @p.on("vector")
                    def _(v, buf=buf, e_done=e_done, accum_col=accum_col,
                          war_q=war_q):
                        v.wait_ge(sA.h, e_done)
                        if war_q is not None:
                            for wsem, wval in war_q:
                                v.wait_ge(wsem.h, wval)
                        kw = {}
                        if accum_col is not None:
                            kw = dict(accum_out=ksums[:, accum_col:accum_col + 1])
                        v.scalar_tensor_tensor(qb[buf][:, :], eb[buf][:, :], 1.0,
                                               rb[buf][:, :], ALU.mult, ALU.add,
                                               **kw).then_inc(sv.h, 1)
                    return sv.inc()

                def rope_la(col, buf, src_done, dst, war_d):
                    @p.on("gpsimd")
                    def _(g, col=col, buf=buf, src_done=src_done, base=sP.n,
                          dst=dst, war_d=war_d):
                        g.wait_ge(sv.h, src_done)
                        g.tensor_copy(swp[buf][0:32, :], qb[buf][32:64, :])
                        g.tensor_copy(swp[buf][32:64, :], qb[buf][0:32, :])
                        g.tensor_copy(swp[buf][64:96, :], qb[buf][96:128, :])
                        g.tensor_copy(swp[buf][96:128, :],
                                      qb[buf][64:96, :]).then_inc(sP.h, 1)
                        g.wait_ge(sP.h, base + 1)
                        g.tensor_scalar(rw1[buf][:, :], qb[buf][:, :],
                                        cosm[:, col:col + 1], None, ALU.mult)
                        g.tensor_scalar(rw2[buf][:, :], swp[buf][:, :],
                                        sms[:, col:col + 1], None,
                                        ALU.mult).then_inc(sP.h, 1)
                        g.wait_ge(sP.h, base + 2)
                        if war_d is not None:
                            g.wait_ge(war_d[0].h, war_d[1])
                        g.tensor_tensor(dst[:, :], rw1[buf][:, :], rw2[buf][:, :],
                                        ALU.add).then_inc(sP.h, 1)
                    return sP.inc(3)

                cnt = 0
                for kc in range(8):
                    for tcb in range(4):
                        bank = cnt % 2
                        buf = cnt % 2
                        ew = [(bX.s[0], tp_done)] if cnt == 0 else []
                        mm = proj_mm(8 + kc, tcb, bank, ew)
                        rq = rope_read.get(cnt - 2)
                        q_done = elu(8 + kc, bank, buf, mm, cnt,
                                     [rq] if rq else None)
                        sps = bK.slot(cnt)
                        kr_done = rope_la(kc, buf, q_done, qrb[buf],
                                          krb_sp.get(cnt - 2))
                        rope_read[cnt] = (sP, kr_done)

                        @p.on("gpsimd")
                        def _(g, kc=kc, tcb=tcb, buf=buf, kr_done=kr_done, sps=sps):
                            g.wait_ge(sP.h, kr_done)
                            g.dma_start(out=sp_kr[128 * kc:128 * (kc + 1),
                                                  512 * tcb:512 * (tcb + 1)],
                                        in_=qrb[buf][:, :]).then_inc(sps.h, 16)
                        krb_sp[cnt] = (sps, sps.inc(16))
                        cnt += 1
                kr_sp_tot = (bK.s[0].n, bK.s[1].n)
                qrb_war[0] = krb_sp[30]
                qrb_war[1] = krb_sp[31]
                zrb_free = [None]

                @p.on("vector")
                def _(v, base=sv.n):
                    v.wait_ge(sv.h, base)
                    v.tensor_tensor(kwa[:, :], ksums[:, 0:32:4], ksums[:, 1:32:4],
                                    ALU.add)
                    v.tensor_tensor(kwb[:, :], ksums[:, 2:32:4], ksums[:, 3:32:4],
                                    ALU.add).then_inc(sv.h, 1)
                    v.wait_ge(sv.h, base + 1)
                    v.tensor_tensor(kme8[:, :], kwa[:, :], kwb[:, :],
                                    ALU.add).then_inc(sv.h, 1)
                    v.wait_ge(sv.h, base + 2)
                    v.tensor_scalar_mul(kme8[:, :], kme8[:, :],
                                        1.0 / T).then_inc(sv.h, 1)
                    v.wait_ge(sv.h, base + 3)
                    for kc in range(8):
                        v.tensor_scalar(kmbs[:, 2 * kc:2 * kc + 2], ones2[:, 0:2],
                                        kme8[:, kc:kc + 1], None, ALU.mult)
                    v.engine_nop().then_inc(sv.h, 1)
                km_done = sv.inc(4)

                @p.on("vector")
                def _(v):
                    v.memset(kvacc[:, :], 0.0).then_inc(sv.h, 1)
                kv0_done = sv.inc()

                ld_free = [None, None]
                pkv_free = [None, None]
                prev_add = kv0_done
                for tc in range(16):
                    buf = tc % 2
                    bank = tc % 2
                    sls = bL.slot(tc)

                    @p.on("sync")
                    def _(s, tc=tc, buf=buf, war=ld_free[buf], sls=sls,
                          first=(tc == 0), kt=kr_sp_tot):
                        if first:
                            s.wait_ge(bK.s[0].h, kt[0])
                            s.wait_ge(bK.s[1].h, kt[1])
                        if war is not None:
                            s.wait_ge(sT.h, war)
                        s.dma_start(out=xat[buf][:, :],
                                    in_=xa_tm[128 * tc:128 * (tc + 1), :]
                                    ).then_inc(sls.h, 16)
                        for fh in range(2):
                            s.dma_start(out=krt[buf][:, 512 * fh:512 * (fh + 1)],
                                        in_=sp_kr[512 * fh:512 * (fh + 1),
                                                  128 * tc:128 * (tc + 1)],
                                        transpose=True).then_inc(sls.h, 16)
                    wl = sls.inc(16 * 3)

                    @p.on("tensor")
                    def _(t, tc=tc, buf=buf, bank=bank, wl=wl, sls=sls,
                          war=pkv_free[bank]):
                        t.wait_ge(sls.h, wl)
                        if war is not None:
                            t.wait_ge(sv.h, war)
                        for h in range(16):
                            mmx = t.matmul(PKV[bank][64 * (h % 2):64 * (h % 2) + 64,
                                               64 * (h // 2):64 * (h // 2) + 64],
                                           krt[buf][:, 64 * h:64 * (h + 1)],
                                           xat[buf][:, 64 * h:64 * (h + 1)],
                                           start=True, stop=True,
                                           tile_position=(0, 64 * (h % 2)))
                        mmx.then_inc(sT.h, 1)
                    ld_free[buf] = sT.inc()

                    @p.on("vector")
                    def _(v, bank=bank, mm=ld_free[buf], pa=prev_add):
                        v.wait_ge(sT.h, mm)
                        v.wait_ge(sv.h, pa)
                        v.tensor_tensor(kvacc[:, :], kvacc[:, :],
                                        PKV[bank][:, :], ALU.add).then_inc(sv.h, 1)
                    prev_add = sv.inc()
                    pkv_free[bank] = prev_add
                kv_sum = prev_add

                @p.on("scalar")
                def _(s, kv_sum=kv_sum):
                    s.wait_ge(sv.h, kv_sum)
                    s.activation(kvs[:, :], kvacc[:, :], AF.Copy,
                                 scale=1.0 / T).then_inc(sA.h, 1)
                kv_done = sA.inc()

                last_o = None
                for rc in range(8):
                    @p.on("gpsimd")
                    def _(g, rc=rc, base=sP.n, tp=tp_done, lo=last_o):
                        if rc == 0:
                            g.wait_ge(bX.s[0].h, tp)
                        if lo is not None:
                            g.wait_ge(sv.h, lo)
                        g.tensor_scalar(peb[:, :], xv[:, rc * T:(rc + 1) * T],
                                        lwc[1][:, rc:rc + 1], lbc[:, rc:rc + 1],
                                        ALU.mult, ALU.add)
                        g.memset(pebt[:, 0:1], 0.0)
                        g.tensor_scalar(pebt[:, 1:T], xv[:, rc * T:rc * T + T - 1],
                                        lwc[0][:, rc:rc + 1], None,
                                        ALU.mult).then_inc(sP.h, 1)
                        g.wait_ge(sP.h, base + 1)
                        g.tensor_tensor(peb[:, :], peb[:, :], pebt[:, :],
                                        ALU.add).then_inc(sP.h, 1)
                        g.wait_ge(sP.h, base + 2)
                        g.memset(pebt[:, T - 1:T], 0.0)
                        g.tensor_scalar(pebt[:, 0:T - 1],
                                        xv[:, rc * T + 1:rc * T + T],
                                        lwc[2][:, rc:rc + 1], None,
                                        ALU.mult).then_inc(sP.h, 1)
                        g.wait_ge(sP.h, base + 3)
                        g.tensor_tensor(peb[:, :], peb[:, :], pebt[:, :],
                                        ALU.add).then_inc(sP.h, 1)
                    pe_done = sP.inc(4)

                    for tcb in range(4):
                        bank = cnt % 2
                        buf = cnt % 2
                        mm = proj_mm(rc, tcb, bank, [])
                        wl = []
                        if qb_free[buf] is not None:
                            wl.append((sT, qb_free[buf]))
                        rq = rope_read.get(cnt - 2)
                        if rq is not None:
                            wl.append(rq)
                        q_done = elu(rc, bank, buf, mm, None, wl or None)

                        @p.on("tensor")
                        def _(t, rc=rc, buf=buf, q_done=q_done, war=pd_free[0],
                              km=km_done, first=(cnt == 32)):
                            t.wait_ge(sv.h, q_done)
                            if first:
                                t.wait_ge(sv.h, km)
                            if war is not None:
                                t.wait_ge(sA.h, war)
                            t.matmul(PD[0:2, :], kmbs[:, 2 * rc:2 * rc + 2],
                                     qb[buf][:, :], start=True,
                                     stop=True).then_inc(sT.h, 1)
                        den_mm = sT.inc()
                        qb_free[buf] = den_mm

                        @p.on("scalar")
                        def _(s, den_mm=den_mm, base=sA.n, warz=zrb_free[0]):
                            s.wait_ge(sT.h, den_mm)
                            s.activation(zf[0:2, :], PD[0:2, :], AF.Ln,
                                         bias=cst[0:2, 2:3]).then_inc(sA.h, 1)
                            s.wait_ge(sA.h, base + 1)
                            if warz is not None:
                                s.wait_ge(sT.h, warz)
                            s.activation(zrb[0:2, :], zf[0:2, :], AF.Exp,
                                         scale=-1.0).then_inc(sA.h, 1)
                        z_done = sA.inc(2)
                        pd_free[0] = z_done - 1

                        @p.on("tensor")
                        def _(t, z_done=z_done, war=pz_free[0]):
                            t.wait_ge(sA.h, z_done)
                            if war is not None:
                                t.wait_ge(sv.h, war)
                            t.matmul(PZ[:, :], ind2[0:2, :], zrb[0:2, :],
                                     start=True, stop=True).then_inc(sT.h, 1)
                        zb_mm = sT.inc()
                        zrb_free[0] = zb_mm

                        qr_done = rope_la(rc, buf, q_done, qrb[buf],
                                          qrb_war[buf])
                        rope_read[cnt] = (sP, qr_done)

                        @p.on("tensor")
                        def _(t, rc=rc, buf=buf, bank2=buf, qr_done=qr_done,
                              kv=kv_done, first=(cnt == 32),
                              war=pat_free[buf]):
                            t.wait_ge(sP.h, qr_done)
                            if first:
                                t.wait_ge(sA.h, kv)
                            if war is not None:
                                t.wait_ge(sA.h, war)
                            for hh in range(2):
                                mmx = t.matmul(
                                    PAT[bank2][64 * hh:64 * hh + 64, :],
                                    kvs[64 * hh:64 * hh + 64,
                                        64 * rc:64 * rc + 64],
                                    qrb[buf][64 * hh:64 * hh + 64, :],
                                    start=True, stop=True,
                                    tile_position=(64 * hh, 64 * hh))
                            mmx.then_inc(sT.h, 1)
                        at_mm = sT.inc()

                        @p.on("scalar")
                        def _(s, buf=buf, bank2=buf, at_mm=at_mm,
                              war=azf_free[buf]):
                            s.wait_ge(sT.h, at_mm)
                            if war is not None:
                                s.wait_ge(sv.h, war)
                            s.activation(azf[buf][:, :], PAT[bank2][:, :],
                                         AF.Copy).then_inc(sA.h, 1)
                        ev_done = sA.inc()
                        pat_free[buf] = ev_done

                        sos = bO.slot(cnt - 32)

                        @p.on("vector")
                        def _(v, rc=rc, tcb=tcb, buf=buf, ev_done=ev_done,
                              zb_mm=zb_mm, base=sv.n, wo=ob_free[buf],
                              wpe=pe_done):
                            v.wait_ge(sA.h, ev_done)
                            v.wait_ge(sT.h, zb_mm)
                            v.tensor_tensor(azf[buf][:, :], azf[buf][:, :],
                                            PZ[:, :], ALU.mult).then_inc(sv.h, 1)
                            v.wait_ge(sv.h, base + 1)
                            if wo is not None:
                                v.wait_ge(wo[0].h, wo[1])
                            v.wait_ge(sP.h, wpe)
                            v.tensor_tensor(ob[buf][:, :], azf[buf][:, :],
                                            peb[:, 512 * tcb:512 * (tcb + 1)],
                                            ALU.add).then_inc(sv.h, 1)
                        o_done = sv.inc(2)
                        last_o = o_done
                        pz_free[0] = o_done - 1
                        azf_free[buf] = o_done
                        qrb_war[buf] = (sT, at_mm)

                        @p.on("gpsimd")
                        def _(g, rc=rc, tcb=tcb, buf=buf, o_done=o_done, sos=sos):
                            g.wait_ge(sv.h, o_done)
                            g.dma_start(out=mac_fm[128 * rc:128 * (rc + 1),
                                                   512 * tcb:512 * (tcb + 1)],
                                        in_=ob[buf][:, :]).then_inc(sos.h, 16)
                        ob_free[buf] = (sos, sos.inc(16))
                        cnt += 1

                @p.on("gpsimd")
                def _(g, t0=bO.s[0].n, t1=bO.s[1].n):
                    g.wait_ge(bO.s[0].h, t0)
                    g.wait_ge(bO.s[1].h, t1)
                p.run()

        def emit_ffn_attention(src_tm, x1_fm):
            ft = mk_tag("fa")
            with ExitStack() as c1:
                E1 = c1.enter_context
                wqT = E1(nc.sbuf_tensor(f"{ft}wqT", [128, 8 * D], BF16))
                wkT = E1(nc.sbuf_tensor(f"{ft}wkT", [128, 8 * 256], BF16))
                wvT = E1(nc.sbuf_tensor(f"{ft}wvT", [128, 8 * 256], BF16))
                woT = E1(nc.sbuf_tensor(f"{ft}woT", [128, 8 * D], BF16))
                xf = E1(nc.sbuf_tensor(f"{ft}xf", [128, 8 * T], BF16))
                kr = E1(nc.sbuf_tensor(f"{ft}kr", [128, 2 * T], BF16))
                kr_view = [kr[:, 0:T], kr[:, T:2 * T]]
                vo = E1(nc.sbuf_tensor(f"{ft}vo", [128, 16 * 320], BF16, is_transpose_dest=True))
                qn = E1(nc.sbuf_tensor(f"{ft}qn", [128, T], BF16))
                qr = E1(nc.sbuf_tensor(f"{ft}qr", [128, T], BF16))
                kn = E1(nc.sbuf_tensor(f"{ft}kn", [128, T], BF16))
                sq = E1(nc.sbuf_tensor(f"{ft}sqb", [128, 512], BF16))
                swp = E1(nc.sbuf_tensor(f"{ft}swp", [128, T], BF16))
                krw2 = [E1(nc.sbuf_tensor(f"{ft}krw{i}", [128, T], BF16)) for i in range(2)]
                rw4 = E1(nc.sbuf_tensor(f"{ft}rw", [128, 4 * 1024], F32))
                rwb4 = E1(nc.sbuf_tensor(f"{ft}rwb", [2, 4 * 512], BF16))
                pt = [E1(nc.sbuf_tensor(f"{ft}pt{i}", [128, 512], BF16)) for i in range(5)]
                ysb = E1(nc.sbuf_tensor(f"{ft}ysb", [128, 8 * T], BF16))
                yt = E1(nc.sbuf_tensor(f"{ft}yt", [128, 512], F32))
                x1b = [E1(nc.sbuf_tensor(f"{ft}x1b{i}", [128, 512], BF16)) for i in range(3)]
                rlr = E1(nc.sbuf_tensor(f"{ft}rlr", [1, 512], F32))
                rlb = E1(nc.sbuf_tensor(f"{ft}rlb", [1, 512], BF16))
                vtmp = [E1(nc.sbuf_tensor(f"{ft}vtmp{i}", [128, 512], BF16)) for i in range(2)]
                PP = E1(nc.psum_tensor(f"{ft}PP", [128, 512], F32))
                SS = [E1(nc.psum_tensor(f"{ft}SS{i}", [128, 512], F32)) for i in range(4)]
                YY = [E1(nc.psum_tensor(f"{ft}YY{i}", [128, 512], F32)) for i in range(2)]
                PRL = E1(nc.psum_tensor(f"{ft}PRL", [128, 512], F32))
                pbanks = [PP] + SS
                pfree = {i: None for i in range(5)}

                S2, B2 = c1.enter_context(sem_scope())
                p = Ph(nc)
                sW = S2("fa_w"); sX = S2("fa_x"); sV = S2("fa_v"); sVb = S2("fa_vb")
                sT_ = S2("fa_t"); sA = S2("fa_a"); sD = S2("fa_d")
                sDbg = S2("fa_dbg")
                bO = B2("fa_o", 3)

                @p.on("sync")
                def _(s):
                    for ic in range(8):
                        for og in range(2):
                            s.dma_start(out=wqT[:, ic * D + 512 * og:ic * D + 512 * (og + 1)],
                                        in_=a_wq[512 * og:512 * (og + 1), 128 * ic:128 * (ic + 1)],
                                        transpose=True).then_inc(sW.h, 16)
                            s.dma_start(out=woT[:, ic * D + 512 * og:ic * D + 512 * (og + 1)],
                                        in_=a_wo[512 * og:512 * (og + 1), 128 * ic:128 * (ic + 1)],
                                        transpose=True).then_inc(sW.h, 16)
                        s.dma_start(out=wkT[:, ic * 256:(ic + 1) * 256],
                                    in_=a_wk[:, 128 * ic:128 * (ic + 1)],
                                    transpose=True).then_inc(sW.h, 16)
                        s.dma_start(out=wvT[:, ic * 256:(ic + 1) * 256],
                                    in_=a_wv[:, 128 * ic:128 * (ic + 1)],
                                    transpose=True).then_inc(sW.h, 16)
                    for fc in range(8):
                        for tg in range(4):
                            s.dma_start(out=xf[:, fc * T + 512 * tg:fc * T + 512 * (tg + 1)],
                                        in_=src_tm[512 * tg:512 * (tg + 1),
                                                   128 * fc:128 * (fc + 1)],
                                        transpose=True).then_inc(sX.h, 16)
                sW.inc(16 * 48)
                sX.inc(16 * 32)
                w_done = sW.n
                x_done = sX.n

                prcount = [0]
                rwb_free = [None] * 4
                rw_free = [None] * 4
                pl_free = [None]

                def proj_norm(kind, cc, tcb, wT, wT_stride, gc_t, dst, dst_off,
                              vtmp_war=None):
                    myid = prcount[0]; prcount[0] += 1
                    bi = myid % 5
                    bank = pbanks[bi]
                    rs = myid % 4
                    t0 = 512 * tcb

                    @p.on("tensor")
                    def _(t, cc=cc, t0=t0, wT=wT, wT_stride=wT_stride, bank=bank,
                          war=pfree[bi], first=(myid == 0)):
                        if first:
                            t.wait_ge(sW.h, w_done)
                            t.wait_ge(sX.h, x_done)
                        if war is not None:
                            t.wait_ge(war[0].h, war[1])
                        for ic in range(8):
                            mm = t.matmul(bank[:, :], wT[:, ic * wT_stride + 128 * cc:
                                                         ic * wT_stride + 128 * (cc + 1)],
                                          xf[:, ic * T + t0:ic * T + t0 + 512],
                                          start=(ic == 0), stop=(ic == 7))
                        mm.then_inc(sT_.h, 1)
                    mm_done = sT_.inc()

                    @p.on("scalar")
                    def _(s, mm_done=mm_done, dst=dst, dst_off=dst_off, kind=kind,
                          bank=bank, vtmp_war=vtmp_war):
                        s.wait_ge(sT_.h, mm_done)
                        if vtmp_war is not None:
                            wl = vtmp_war if isinstance(vtmp_war, list) else [vtmp_war]
                            for wsem, wval in wl:
                                s.wait_ge(wsem.h, wval)
                        if kind == "v":
                            s.activation(dst[:, dst_off:dst_off + 512], bank[:, :],
                                         AF.Copy).then_inc(sA.h, 2)
                        else:
                            s.activation(sq[:, :], bank[:, :], AF.Square).then_inc(sA.h, 1)
                            s.activation(dst[:, dst_off:dst_off + 512], bank[:, :],
                                         AF.Copy).then_inc(sA.h, 1)
                    ev_done = sA.inc(2)
                    if kind == "v":
                        pfree[bi] = (sA, ev_done)
                        return ev_done

                    @p.on("tensor")
                    def _(t, ev_done=ev_done, bank=bank):
                        t.wait_ge(sA.h, ev_done)
                        t.matmul(bank[0:2, :], ones2[:, 0:2], sq[:, :],
                                 start=True, stop=True).then_inc(sT_.h, 1)
                    st_done = sT_.inc()

                    @p.on("scalar")
                    def _(s, st_done=st_done, base=sA.n, bank=bank, rs=rs,
                          rww=rw_free[rs]):
                        s.wait_ge(sT_.h, st_done)
                        if rww is not None:
                            s.wait_ge(sD.h, rww)
                        s.activation(rw4[0:2, rs * 1024:rs * 1024 + 512], bank[0:2, :],
                                     AF.Ln, bias=cst[0:2, 2:3],
                                     scale=1.0 / 64.0).then_inc(sA.h, 1)
                        s.wait_ge(sA.h, base + 1)
                        s.activation(rw4[0:2, rs * 1024 + 512:rs * 1024 + 1024],
                                     rw4[0:2, rs * 1024:rs * 1024 + 512], AF.Exp,
                                     scale=-0.5).then_inc(sA.h, 1)
                    ln_done = sA.inc(2)
                    pfree[bi] = (sA, ln_done - 1)

                    @p.on("vector")
                    def _(v, ln_done=ln_done, gc_t=gc_t, cc=cc, rs=rs,
                          rbw=rwb_free[rs]):
                        v.wait_ge(sA.h, ln_done)
                        if rbw is not None:
                            v.wait_ge(sT_.h, rbw)
                        v.tensor_scalar(rwb4[0:2, rs * 512:rs * 512 + 512],
                                        rw4[0:2, rs * 1024 + 512:rs * 1024 + 1024],
                                        gc_t[0:2, cc:cc + 1], None,
                                        ALU.mult).then_inc(sD.h, 1)
                    g_done = sD.inc()
                    rw_free[rs] = g_done

                    @p.on("tensor")
                    def _(t, g_done=g_done, rs=rs, plw=pl_free[0]):
                        t.wait_ge(sD.h, g_done)
                        if plw is not None:
                            t.wait_ge(sD.h, plw)
                        t.matmul(PRL[:, :], ind2[0:2, :],
                                 rwb4[0:2, rs * 512:rs * 512 + 512],
                                 start=True, stop=True).then_inc(sT_.h, 1)
                    bc_done = sT_.inc()
                    rwb_free[rs] = bc_done
                    return bc_done

                def rope_chunk(srcb, dstb, wval, width=T, war_t=None, war_dbg=None):
                    @p.on("vector")
                    def _(v, srcb=srcb, dstb=dstb, wval=wval, base=sD.n, width=width,
                          war_t=war_t, war_dbg=war_dbg):
                        v.wait_ge(sD.h, wval)
                        if war_t is not None:
                            v.wait_ge(sT_.h, war_t)
                        if war_dbg is not None:
                            v.wait_ge(sDbg.h, war_dbg)
                        v.tensor_copy(swp[0:32, 0:width], srcb[32:64, 0:width])
                        v.tensor_copy(swp[32:64, 0:width], srcb[0:32, 0:width])
                        v.tensor_copy(swp[64:96, 0:width], srcb[96:128, 0:width])
                        v.tensor_copy(swp[96:128, 0:width], srcb[64:96, 0:width]).then_inc(sD.h, 1)
                        v.wait_ge(sD.h, base + 1)
                        v.tensor_tensor(krw2[0][:, 0:width], srcb[:, 0:width],
                                        cos2[:, 0:width], ALU.mult)
                        v.tensor_tensor(krw2[1][:, 0:width], swp[:, 0:width],
                                        sin2[:, 0:width], ALU.mult).then_inc(sD.h, 1)
                        v.wait_ge(sD.h, base + 2)
                        v.tensor_tensor(dstb[:, 0:width], krw2[0][:, 0:width],
                                        krw2[1][:, 0:width], ALU.add).then_inc(sD.h, 1)
                    return sD.inc(3)

                rope_done = None
                for kc in range(2):
                    for tcb in range(4):
                        kn_war = (sD, rope_done) if (kc == 1 and tcb == 0) else None
                        bc_done = proj_norm("k", kc, tcb, wkT, 256, kgc, kn, 512 * tcb,
                                            vtmp_war=kn_war)

                        @p.on("vector")
                        def _(v, bc_done=bc_done, tcb=tcb):
                            v.wait_ge(sT_.h, bc_done)
                            v.tensor_tensor(kn[:, 512 * tcb:512 * (tcb + 1)],
                                            kn[:, 512 * tcb:512 * (tcb + 1)], PRL[:, :],
                                            ALU.mult).then_inc(sD.h, 1)
                        nm_done = sD.inc()
                    rope_done = rope_chunk(kn, kr_view[kc], nm_done)
                vcnt = 0
                for kc in range(2):
                    for tcb in range(4):
                        vbuf = vcnt % 2
                        sVs = sV if vbuf == 0 else sVb
                        vtmp_war = (sVs, sVs.n) if vcnt >= 2 else None
                        ev_done = proj_norm("v", kc, tcb, wvT, 256, None,
                                            vtmp[vbuf], 0, vtmp_war=vtmp_war)

                        @p.on("gpsimd")
                        def _(g, kc=kc, tcb=tcb, ev_done=ev_done, vbuf=vbuf, sVs=sVs):
                            g.wait_ge(sA.h, ev_done)
                            g.dma_start(out=sp_fm2[128 * kc:128 * (kc + 1),
                                                   512 * tcb:512 * (tcb + 1)],
                                        in_=vtmp[vbuf][:, 0:512]).then_inc(sVs.h, 16)
                        sVs.inc(16)
                        vcnt += 1
                v_sp_done = sV.n
                v_sp_doneb = sVb.n

                @p.on("vector")
                def _(v):
                    v.memset(vo[:, :], 1.0).then_inc(sD.h, 1)
                vo_ones = sD.inc()

                sVo = S2("fa_vo")

                @p.on("sync")
                def _(s):
                    s.wait_ge(sV.h, v_sp_done)
                    s.wait_ge(sVb.h, v_sp_doneb)
                    s.wait_ge(sD.h, vo_ones)
                    for kt in range(16):
                        for kv in range(4):
                            s.dma_start(
                                out=vo[:, kt * 320 + kv * 80:kt * 320 + kv * 80 + 64],
                                in_=sp_fm2[64 * kv:64 * (kv + 1), 128 * kt:128 * (kt + 1)],
                                transpose=True).then_inc(sVo.h, 16)
                sVo.inc(16 * 64)
                vo_done = sVo.n

                sG2 = S2("fa_g")
                slab_free = [None] * 5
                yy_free = [None, None]
                rlb_free = [None]
                yt_free = [None]
                prev_q_ready = None
                prev_attn_t = None
                dbg_q_done = None
                pending_tail = [None]

                def flush_tail():
                    if pending_tail[0] is not None:
                        pending_tail[0]()
                        pending_tail[0] = None

                def emit_av(kt, hh, slab, wv, first, last, kvh):
                    @p.on("tensor")
                    def _(t, kt=kt, hh=hh, slab=slab, wv=wv, first=first,
                          last=last, kvh=kvh, off=max(0, 0),
                          yyw=(yy_free[hh] if first else None)):
                        t.wait_ge(wv[0].h, wv[1])
                        if yyw is not None:
                            t.wait_ge(sA.h, yyw[0])
                            t.wait_ge(sA.h, yyw[1])
                        t.matmul(YY[hh][0:65, wv[2]:512],
                                 vo[:, kt * 320 + kvh * 80:
                                    kt * 320 + kvh * 80 + 65],
                                 pt[slab][:, wv[2]:512],
                                 start=first, stop=last).then_inc(sT_.h, 1)
                    slab_free[slab] = sT_.inc()

                for c in range(8):
                    ha, hb = QPERM[2 * c], QPERM[2 * c + 1]
                    kvh2 = (ha // 4, hb // 4)
                    flush_tail()
                    for tcb in range(4):
                        qn_war = []
                        if tcb == 0 and prev_q_ready is not None:
                            qn_war.append((sD, prev_q_ready))
                        if tcb == 0 and dbg_q_done is not None:
                            qn_war.append((sDbg, dbg_q_done))
                        bc_done = proj_norm("q", c, tcb, wqT, D, qgc, qn, 512 * tcb,
                                            vtmp_war=qn_war or None)

                        @p.on("vector")
                        def _(v, bc_done=bc_done, tcb=tcb):
                            v.wait_ge(sT_.h, bc_done)
                            v.tensor_tensor(qn[:, 512 * tcb:512 * (tcb + 1)],
                                            qn[:, 512 * tcb:512 * (tcb + 1)],
                                            PRL[:, :], ALU.mult).then_inc(sD.h, 1)
                        nm_done = sD.inc()
                    rope_chunk(qn, qr, nm_done, war_t=prev_attn_t,
                               war_dbg=dbg_q_done)
                    dbg_q_done = None
                    q_ready = sD.n
                    prev_q_ready = q_ready
                    if debug and c == 0:
                        @p.on("gpsimd")
                        def _(g, q_ready=q_ready):
                            g.wait_ge(sD.h, q_ready)
                            g.dma_start(out=dbg["qr"][:, :], in_=qr[:, :]).then_inc(sDbg.h, 16)
                            g.dma_start(out=dbg["qn"][:, :], in_=qn[:, :]).then_inc(sDbg.h, 16)
                        sDbg.inc(32)
                        dbg_q_done = sDbg.n

                    for qc in range(4):
                        q0 = 512 * qc
                        ktmax = 4 * (qc + 1)
                        pend = []
                        ucnt = 0
                        for kt in range(ktmax):
                            k0 = 128 * kt
                            off = max(0, k0 - q0)
                            diag = (k0 >= q0)
                            for hh in range(2):
                                kvh = kvh2[hh]
                                if ucnt == 3:
                                    flush_tail()
                                slab = ucnt % 5
                                sbi = slab

                                @p.on("tensor")
                                def _(t, kvh=kvh, kt=kt, q0=q0, hh=hh, slab=slab,
                                      q_ready=q_ready, war=pfree[sbi],
                                      first=(kt == 0)):
                                    t.wait_ge(sD.h, q_ready)
                                    if first:
                                        t.wait_ge(sVo.h, vo_done)
                                    if war is not None:
                                        t.wait_ge(war[0].h, war[1])
                                    t.matmul(pbanks[slab][:, :],
                                             kr[64 * hh:64 * hh + 64,
                                                (kvh // 2) * T + 128 * kt:
                                                (kvh // 2) * T + 128 * (kt + 1)],
                                             qr[64 * hh:64 * hh + 64, q0:q0 + 512],
                                             start=True, stop=True,
                                             tile_position=(64 * hh, 0)).then_inc(sT_.h, 1)
                                sc_done = sT_.inc()

                                @p.on("scalar")
                                def _(s, sc_done=sc_done, hh=hh, c=c, off=off,
                                      slab=slab, war=slab_free[slab]):
                                    s.wait_ge(sT_.h, sc_done)
                                    if war is not None:
                                        s.wait_ge(sT_.h, war)
                                    s.activation(pt[slab][:, off:512],
                                                 pbanks[slab][:, off:512], AF.Exp,
                                                 bias=negc[:, 2 * c + hh:2 * c + hh + 1]
                                                 ).then_inc(sA.h, 1)
                                exp_done = sA.inc()
                                pfree[sbi] = (sA, exp_done)

                                if diag:
                                    @p.on("gpsimd")
                                    def _(g, exp_done=exp_done, slab=slab, off=off):
                                        g.wait_ge(sA.h, exp_done)
                                        g.tensor_tensor(pt[slab][:, off:off + 128],
                                                        pt[slab][:, off:off + 128],
                                                        tri[:, :], ALU.mult).then_inc(sG2.h, 1)
                                    wv = (sG2, sG2.inc(), off)
                                else:
                                    wv = (sA, exp_done, off)

                                pend.append((kt, hh, slab, wv, kt == 0,
                                             kt == ktmax - 1, kvh))
                                if len(pend) > 3:
                                    emit_av(*pend.pop(0))
                                ucnt += 1
                        if pending_tail[0] is not None:
                            flush_tail()
                        for x in pend:
                            emit_av(*x)
                        av_done = sT_.n

                        def mk_tail(av_done=av_done, c=c, qc=qc):
                            def tail():
                                for hh in range(2):
                                    @p.on("scalar")
                                    def _(s, av_done=av_done, hh=hh, base=sA.n,
                                          rbw=rlb_free[0]):
                                        s.wait_ge(sT_.h, av_done)
                                        if rbw is not None:
                                            s.wait_ge(sT_.h, rbw)
                                        s.activation(rlr[0:1, :], YY[hh][64:65, :],
                                                     AF.Ln).then_inc(sA.h, 1)
                                        s.wait_ge(sA.h, base + 1)
                                        s.activation(rlb[0:1, :], rlr[0:1, :],
                                                     AF.Exp, scale=-1.0).then_inc(sA.h, 1)
                                    rl_done = sA.inc(2)

                                    @p.on("tensor")
                                    def _(t, rl_done=rl_done, plw=pl_free[0]):
                                        t.wait_ge(sA.h, rl_done)
                                        if plw is not None:
                                            t.wait_ge(sD.h, plw)
                                        t.matmul(PRL[0:64, :], onesr_bf[0:1, 0:64],
                                                 rlb[0:1, :], start=True,
                                                 stop=True).then_inc(sT_.h, 1)
                                    plb_done = sT_.inc()
                                    rlb_free[0] = plb_done

                                    @p.on("scalar")
                                    def _(s, av_done=av_done, hh=hh, ytw=yt_free[0]):
                                        s.wait_ge(sT_.h, av_done)
                                        if ytw is not None:
                                            s.wait_ge(sD.h, ytw)
                                        s.activation(yt[0:64, :], YY[hh][0:64, :],
                                                     AF.Copy).then_inc(sA.h, 1)
                                    yev_done = sA.inc()

                                    @p.on("vector")
                                    def _(v, plb_done=plb_done, yev_done=yev_done,
                                          hh=hh, c=c, qc=qc):
                                        v.wait_ge(sT_.h, plb_done)
                                        v.wait_ge(sA.h, yev_done)
                                        v.tensor_tensor(
                                            ysb[64 * hh:64 * hh + 64,
                                                c * T + 512 * qc:
                                                c * T + 512 * (qc + 1)],
                                            yt[0:64, :], PRL[0:64, :],
                                            ALU.mult).then_inc(sD.h, 1)
                                    yfin = sD.inc()
                                    pl_free[0] = yfin
                                    yt_free[0] = yfin
                                    yy_free[hh] = (yev_done, rl_done)
                            return tail
                        pending_tail[0] = mk_tail()
                    prev_attn_t = sT_.n
                flush_tail()
                y_all_done = sD.n

                if debug:
                    @p.on("gpsimd")
                    def _(g, y_all=y_all_done):
                        g.wait_ge(sD.h, y_all)
                        g.dma_start(out=dbg["kr"][:, :], in_=kr[:, :]).then_inc(sDbg.h, 16)
                        g.dma_start(out=dbg["y"][:, :], in_=ysb[:, :]).then_inc(sDbg.h, 16)
                        g.dma_start(out=dbg["vo"][:, :], in_=vo[:, :]).then_inc(sDbg.h, 16)
                    sDbg.inc(48)

                x1cnt = [0]
                for oc in range(8):
                    for tcb in range(4):
                        myid = x1cnt[0]; x1cnt[0] += 1
                        buf = myid % 3
                        bi = myid % 5
                        bank = pbanks[bi]

                        @p.on("tensor")
                        def _(t, oc=oc, tcb=tcb, y_all=y_all_done, bank=bank,
                              warp=pfree[bi]):
                            t.wait_ge(sD.h, y_all)
                            if warp is not None:
                                t.wait_ge(warp[0].h, warp[1])
                            for ic in range(8):
                                mm = t.matmul(bank[:, :],
                                              woT[:, ic * D + 128 * oc:ic * D + 128 * (oc + 1)],
                                              ysb[:, ic * T + 512 * tcb:ic * T + 512 * (tcb + 1)],
                                              start=(ic == 0), stop=(ic == 7))
                            mm.then_inc(sT_.h, 1)
                        wo_done = sT_.inc()

                        sOs = bO.slot(myid)

                        @p.on("vector")
                        def _(v, wo_done=wo_done, buf=buf, sOs=sOs, bank=bank,
                              war=(sOs.n if myid >= 3 else None)):
                            v.wait_ge(sT_.h, wo_done)
                            if war is not None:
                                v.wait_ge(sOs.h, war)
                            v.tensor_copy(x1b[buf][:, :], bank[:, :]).then_inc(sD.h, 1)
                        x1_done = sD.inc()
                        pfree[bi] = (sD, x1_done)

                        @p.on("gpsimd")
                        def _(g, x1_done=x1_done, oc=oc, tcb=tcb, buf=buf, sOs=sOs):
                            g.wait_ge(sD.h, x1_done)
                            g.dma_start(out=x1_fm[128 * oc:128 * (oc + 1),
                                                  512 * tcb:512 * (tcb + 1)],
                                        in_=x1b[buf][:, :]).then_inc(sOs.h, 16)
                        sOs.inc(16)

                @p.on("gpsimd")
                def _(g, t0=bO.s[0].n, t1=bO.s[1].n, t2=bO.s[2].n, td=sDbg.n):
                    g.wait_ge(bO.s[0].h, t0)
                    g.wait_ge(bO.s[1].h, t1)
                    g.wait_ge(bO.s[2].h, t2)
                    if td:
                        g.wait_ge(sDbg.h, td)
                p.run()

        def emit_merge(res_in, term_fm, res_out, scale, xf_tm=None, hn_out=None,
                       ln_out=None, attin_out=None):
            mt = mk_tag("mg")
            with ExitStack() as c1:
                E1 = c1.enter_context
                xr = [E1(nc.sbuf_tensor(f"{mt}_x{i}", [128, D], F32)) for i in range(2)]
                tt_ = [E1(nc.sbuf_tensor(f"{mt}_t{i}", [128, D], BF16, is_transpose_dest=True))
                       for i in range(2)]
                xfb = [E1(nc.sbuf_tensor(f"{mt}_f{i}", [128, D], BF16)) for i in range(2)]
                x1f = E1(nc.sbuf_tensor(f"{mt}_1", [128, D], F32))
                sxn = [E1(nc.sbuf_tensor(f"{mt}_o{i}", [128, D], F32)) for i in range(2)]
                ob = [E1(nc.sbuf_tensor(f"{mt}_b{i}", [128, D], BF16)) for i in range(2)]
                ob2 = [E1(nc.sbuf_tensor(f"{mt}_c{i}", [128, D], BF16)) for i in range(2)]
                stt = [E1(nc.sbuf_tensor(f"{mt}_s{i}", [128, 12], F32)) for i in range(2)]
                sqs = E1(nc.sbuf_tensor(f"{mt}_q", [128, D], F32))

                S2, B2 = c1.enter_context(sem_scope())
                p = Ph(nc)
                sv = S2("mg_v"); sa = S2("mg_a")
                bl = B2("mg_l")
                bo = B2("mg_o")
                apply_done = []
                store_done = []
                n_store = (1 + (1 if hn_out else 0) + (1 if ln_out else 0)
                           + (1 if attin_out else 0))
                x1f_free = [None]
                for tc in range(TCH):
                    buf = tc % 2
                    war_load = apply_done[tc - 2] if tc >= 2 else None
                    sls = bl.slot(tc)
                    sos = bo.slot(tc)
                    war_store = sos.n if tc >= 2 else None

                    @p.on("sync")
                    def _(s, tc=tc, buf=buf, war=war_load, sls=sls):
                        if war is not None:
                            s.wait_ge(sv.h, war)
                        s.dma_start(out=xr[buf][:, :],
                                    in_=res_in[128 * tc:128 * (tc + 1), :]).then_inc(sls.h, 16)
                        for fg in range(2):
                            s.dma_start(out=tt_[buf][:, 512 * fg:512 * (fg + 1)],
                                        in_=term_fm[512 * fg:512 * (fg + 1),
                                                    128 * tc:128 * (tc + 1)],
                                        transpose=True).then_inc(sls.h, 16)
                        if xf_tm is not None:
                            s.dma_start(out=xfb[buf][:, :],
                                        in_=xf_tm[128 * tc:128 * (tc + 1), :]).then_inc(sls.h, 16)
                    wl = sls.inc(16 * (4 if xf_tm is not None else 3))

                    base = sv.n

                    @p.on("vector")
                    def _(v, buf=buf, wl=wl, base=base, war=war_store, sls=sls, sos=sos,
                          x1w=x1f_free[0]):
                        v.wait_ge(sls.h, wl)
                        if war is not None:
                            v.wait_ge(sos.h, war)
                        if xf_tm is not None:
                            if x1w is not None:
                                v.wait_ge(sv.h, x1w)
                            v.tensor_tensor(x1f[:, :], xfb[buf][:, :], tt_[buf][:, :],
                                            ALU.add).then_inc(sv.h, 1)
                            v.wait_ge(sv.h, base + 1)
                            v.scalar_tensor_tensor(sxn[buf][:, :], x1f[:, :], scale,
                                                   xr[buf][:, :], ALU.mult,
                                                   ALU.add).then_inc(sv.h, 1)
                            v.scalar_tensor_tensor(sqs[:, :], x1f[:, :], 1.0, x1f[:, :],
                                                   ALU.mult, ALU.mult,
                                                   accum_out=stt[buf][:, 0:1]).then_inc(sv.h, 1)
                        else:
                            v.scalar_tensor_tensor(sxn[buf][:, :], tt_[buf][:, :], scale,
                                                   xr[buf][:, :], ALU.mult,
                                                   ALU.add).then_inc(sv.h, 1)
                            if ln_out is not None:
                                v.wait_ge(sv.h, base + 1)
                                v.reduce_sum(stt[buf][:, 1:2], sxn[buf][:, :],
                                             AX.X).then_inc(sv.h, 1)
                                v.scalar_tensor_tensor(sqs[:, :], sxn[buf][:, :], 1.0,
                                                       sxn[buf][:, :], ALU.mult, ALU.mult,
                                                       accum_out=stt[buf][:, 0:1]
                                                       ).then_inc(sv.h, 1)
                    nv = 3 if (xf_tm is not None or ln_out is not None) else 1
                    sv.inc(nv)
                    base2 = sv.n

                    if xf_tm is not None:
                        @p.on("vector")
                        def _(v, buf=buf, base2=base2):
                            v.wait_ge(sv.h, base2)
                            v.tensor_scalar_mul(stt[buf][:, 2:3], stt[buf][:, 0:1],
                                                1.0 / D).then_inc(sv.h, 1)
                        sv.inc(1)
                        b3 = sv.n

                        @p.on("scalar")
                        def _(s, buf=buf, b3=b3, ba=sa.n):
                            s.wait_ge(sv.h, b3)
                            s.activation(stt[buf][:, 3:4], stt[buf][:, 2:3], AF.Ln,
                                         bias=cst[:, 2:3]).then_inc(sa.h, 1)
                            s.wait_ge(sa.h, ba + 1)
                            s.activation(stt[buf][:, 4:5], stt[buf][:, 3:4], AF.Exp,
                                         scale=-0.5).then_inc(sa.h, 1)
                        sa.inc(2)
                        b4 = sa.n

                        @p.on("vector")
                        def _(v, buf=buf, b4=b4):
                            v.wait_ge(sa.h, b4)
                            v.tensor_scalar(ob[buf][:, :], x1f[:, :], stt[buf][:, 4:5],
                                            None, ALU.mult).then_inc(sv.h, 1)
                        sv.inc(1)
                        x1f_free[0] = sv.n
                    elif ln_out is not None:
                        @p.on("vector")
                        def _(v, buf=buf, base2=base2):
                            v.wait_ge(sv.h, base2)
                            v.tensor_scalar_mul(stt[buf][:, 2:3], stt[buf][:, 1:2], 1.0 / D)
                            v.tensor_scalar_mul(stt[buf][:, 3:4], stt[buf][:, 0:1],
                                                1.0 / D).then_inc(sv.h, 1)
                            v.wait_ge(sv.h, base2 + 1)
                            v.scalar_tensor_tensor(stt[buf][:, 4:5], stt[buf][:, 2:3],
                                                   -1.0, stt[buf][:, 2:3], ALU.mult,
                                                   ALU.mult).then_inc(sv.h, 1)
                            v.wait_ge(sv.h, base2 + 2)
                            v.tensor_tensor(stt[buf][:, 4:5], stt[buf][:, 3:4],
                                            stt[buf][:, 4:5], ALU.add).then_inc(sv.h, 1)
                        sv.inc(3)
                        b3 = sv.n

                        @p.on("scalar")
                        def _(s, buf=buf, b3=b3, ba=sa.n, dual=(attin_out is not None)):
                            s.wait_ge(sv.h, b3)
                            s.activation(stt[buf][:, 5:6], stt[buf][:, 4:5], AF.Ln,
                                         bias=cst[:, 1:2]).then_inc(sa.h, 1)
                            s.wait_ge(sa.h, ba + 1)
                            s.activation(stt[buf][:, 6:7], stt[buf][:, 5:6], AF.Exp,
                                         scale=-0.5).then_inc(sa.h, 1)
                            if dual:
                                s.activation(stt[buf][:, 7:8], stt[buf][:, 5:6], AF.Exp,
                                             scale=-1.0).then_inc(sa.h, 1)
                        sa.inc(3 if attin_out is not None else 2)
                        b4 = sa.n

                        @p.on("vector")
                        def _(v, buf=buf, b4=b4, bb=sv.n, dual=(attin_out is not None)):
                            v.wait_ge(sa.h, b4)
                            v.scalar_tensor_tensor(stt[buf][:, 8:9], stt[buf][:, 2:3],
                                                   -1.0, stt[buf][:, 6:7], ALU.mult,
                                                   ALU.mult).then_inc(sv.h, 1)
                            v.wait_ge(sv.h, bb + 1)
                            v.tensor_scalar(ob[buf][:, :], sxn[buf][:, :], stt[buf][:, 6:7],
                                            stt[buf][:, 8:9], ALU.mult,
                                            ALU.add).then_inc(sv.h, 1)
                            if dual:
                                v.tensor_scalar(stt[buf][:, 9:10], stt[buf][:, 4:5],
                                                stt[buf][:, 7:8], cst[:, 2:3], ALU.mult,
                                                ALU.add).then_inc(sv.h, 1)
                        sv.inc(3 if attin_out is not None else 2)
                        b5 = sv.n

                        if attin_out is not None:
                            @p.on("scalar")
                            def _(s, buf=buf, b5=b5, ba=sa.n):
                                s.wait_ge(sv.h, b5)
                                s.activation(stt[buf][:, 10:11], stt[buf][:, 9:10], AF.Ln
                                             ).then_inc(sa.h, 1)
                                s.wait_ge(sa.h, ba + 1)
                                s.activation(stt[buf][:, 11:12], stt[buf][:, 10:11], AF.Exp,
                                             scale=-0.5).then_inc(sa.h, 1)
                            sa.inc(2)
                            b6 = sa.n

                            @p.on("vector")
                            def _(v, buf=buf, b6=b6, bb=sv.n):
                                v.wait_ge(sa.h, b6)
                                v.tensor_scalar(ob2[buf][:, :], ob[buf][:, :],
                                                stt[buf][:, 11:12], None,
                                                ALU.mult).then_inc(sv.h, 1)
                            sv.inc(1)

                    apply_done.append(sv.n)
                    wv2 = sv.n

                    @p.on("gpsimd")
                    def _(g, tc=tc, buf=buf, wv2=wv2, sos=sos):
                        g.wait_ge(sv.h, wv2)
                        g.dma_start(out=res_out[128 * tc:128 * (tc + 1), :],
                                    in_=sxn[buf][:, :]).then_inc(sos.h, 16)
                        if hn_out is not None:
                            g.dma_start(out=hn_out[128 * tc:128 * (tc + 1), :],
                                        in_=ob[buf][:, :]).then_inc(sos.h, 16)
                        if ln_out is not None:
                            g.dma_start(out=ln_out[128 * tc:128 * (tc + 1), :],
                                        in_=ob[buf][:, :]).then_inc(sos.h, 16)
                        if attin_out is not None:
                            g.dma_start(out=attin_out[128 * tc:128 * (tc + 1), :],
                                        in_=ob2[buf][:, :]).then_inc(sos.h, 16)
                    sos.inc(16 * n_store)

                @p.on("gpsimd")
                def _(g, t0=bo.s[0].n, t1=bo.s[1].n):
                    g.wait_ge(bo.s[0].h, t0)
                    g.wait_ge(bo.s[1].h, t1)
                p.run()

        def emit_ln_dual(src_f32, dst_ln, dst_attin):
            mt = mk_tag("ld")
            with ExitStack() as c1:
                E1 = c1.enter_context
                xc = [E1(nc.sbuf_tensor(f"{mt}_x{i}", [128, D], F32)) for i in range(2)]
                ob = [E1(nc.sbuf_tensor(f"{mt}_o{i}", [128, D], BF16)) for i in range(2)]
                ob2 = [E1(nc.sbuf_tensor(f"{mt}_c{i}", [128, D], BF16)) for i in range(2)]
                stt = [E1(nc.sbuf_tensor(f"{mt}_s{i}", [128, 12], F32)) for i in range(2)]
                sqs = E1(nc.sbuf_tensor(f"{mt}_q", [128, D], F32))

                S2, B2 = c1.enter_context(sem_scope())
                p = Ph(nc)
                sv = S2("ld_v"); sa = S2("ld_a")
                bl = B2("ld_l")
                bo = B2("ld_o")
                apply_done = []
                for tc in range(TCH):
                    buf = tc % 2
                    war_load = apply_done[tc - 2] if tc >= 2 else None
                    sls = bl.slot(tc)
                    sos = bo.slot(tc)
                    war_store = sos.n if tc >= 2 else None

                    @p.on("sync")
                    def _(s, tc=tc, buf=buf, war=war_load, sls=sls):
                        if war is not None:
                            s.wait_ge(sv.h, war)
                        s.dma_start(out=xc[buf][:, :],
                                    in_=src_f32[128 * tc:128 * (tc + 1), :]).then_inc(sls.h, 16)
                    wl = sls.inc(16)
                    base = sv.n

                    @p.on("vector")
                    def _(v, buf=buf, wl=wl, base=base, war=war_store, sls=sls, sos=sos):
                        v.wait_ge(sls.h, wl)
                        if war is not None:
                            v.wait_ge(sos.h, war)
                        v.reduce_sum(stt[buf][:, 1:2], xc[buf][:, :], AX.X).then_inc(sv.h, 1)
                        v.scalar_tensor_tensor(sqs[:, :], xc[buf][:, :], 1.0, xc[buf][:, :],
                                               ALU.mult, ALU.mult,
                                               accum_out=stt[buf][:, 0:1]).then_inc(sv.h, 1)
                        v.wait_ge(sv.h, base + 2)
                        v.tensor_scalar_mul(stt[buf][:, 2:3], stt[buf][:, 1:2], 1.0 / D)
                        v.tensor_scalar_mul(stt[buf][:, 3:4], stt[buf][:, 0:1],
                                            1.0 / D).then_inc(sv.h, 1)
                        v.wait_ge(sv.h, base + 3)
                        v.scalar_tensor_tensor(stt[buf][:, 4:5], stt[buf][:, 2:3], -1.0,
                                               stt[buf][:, 2:3], ALU.mult,
                                               ALU.mult).then_inc(sv.h, 1)
                        v.wait_ge(sv.h, base + 4)
                        v.tensor_tensor(stt[buf][:, 4:5], stt[buf][:, 3:4], stt[buf][:, 4:5],
                                        ALU.add).then_inc(sv.h, 1)
                    sv.inc(5)
                    b3 = sv.n

                    @p.on("scalar")
                    def _(s, buf=buf, b3=b3, ba=sa.n):
                        s.wait_ge(sv.h, b3)
                        s.activation(stt[buf][:, 5:6], stt[buf][:, 4:5], AF.Ln,
                                     bias=cst[:, 1:2]).then_inc(sa.h, 1)
                        s.wait_ge(sa.h, ba + 1)
                        s.activation(stt[buf][:, 6:7], stt[buf][:, 5:6], AF.Exp,
                                     scale=-0.5).then_inc(sa.h, 1)
                        s.activation(stt[buf][:, 7:8], stt[buf][:, 5:6], AF.Exp,
                                     scale=-1.0).then_inc(sa.h, 1)
                    sa.inc(3)
                    b4 = sa.n

                    @p.on("vector")
                    def _(v, buf=buf, b4=b4, bb=sv.n):
                        v.wait_ge(sa.h, b4)
                        v.scalar_tensor_tensor(stt[buf][:, 8:9], stt[buf][:, 2:3], -1.0,
                                               stt[buf][:, 6:7], ALU.mult,
                                               ALU.mult).then_inc(sv.h, 1)
                        v.tensor_scalar(stt[buf][:, 9:10], stt[buf][:, 4:5],
                                        stt[buf][:, 7:8], cst[:, 2:3], ALU.mult,
                                        ALU.add).then_inc(sv.h, 1)
                        v.wait_ge(sv.h, bb + 2)
                        v.tensor_scalar(ob[buf][:, :], xc[buf][:, :], stt[buf][:, 6:7],
                                        stt[buf][:, 8:9], ALU.mult, ALU.add).then_inc(sv.h, 1)
                    sv.inc(3)
                    b5 = sv.n

                    @p.on("scalar")
                    def _(s, buf=buf, b5=b5, ba=sa.n):
                        s.wait_ge(sv.h, b5)
                        s.activation(stt[buf][:, 10:11], stt[buf][:, 9:10],
                                     AF.Ln).then_inc(sa.h, 1)
                        s.wait_ge(sa.h, ba + 1)
                        s.activation(stt[buf][:, 11:12], stt[buf][:, 10:11], AF.Exp,
                                     scale=-0.5).then_inc(sa.h, 1)
                    sa.inc(2)
                    b6 = sa.n

                    @p.on("vector")
                    def _(v, buf=buf, b6=b6):
                        v.wait_ge(sa.h, b6)
                        v.tensor_scalar(ob2[buf][:, :], ob[buf][:, :], stt[buf][:, 11:12],
                                        None, ALU.mult).then_inc(sv.h, 1)
                    sv.inc(1)
                    apply_done.append(sv.n)
                    wv2 = sv.n

                    @p.on("gpsimd")
                    def _(g, tc=tc, buf=buf, wv2=wv2, sos=sos):
                        g.wait_ge(sv.h, wv2)
                        g.dma_start(out=dst_ln[128 * tc:128 * (tc + 1), :],
                                    in_=ob[buf][:, :]).then_inc(sos.h, 16)
                        g.dma_start(out=dst_attin[128 * tc:128 * (tc + 1), :],
                                    in_=ob2[buf][:, :]).then_inc(sos.h, 16)
                    sos.inc(32)

                @p.on("gpsimd")
                def _(g, t0=bo.s[0].n, t1=bo.s[1].n):
                    g.wait_ge(bo.s[0].h, t0)
                    g.wait_ge(bo.s[1].h, t1)
                p.run()

        full_chain = False
        if stop_after != "p0":
            emit_ln_dual(x_in, sp_tm2, sp_tm)
            if stop_after != "p1":
                emit_ffn_attention(sp_tm, sp_fm1)
                emit_merge(x_in, sp_fm1, s_x, 0.5, xf_tm=sp_tm2, hn_out=sp_hn)
                if stop_after != "p2":
                    emit_banks("b1", sp_hn, sp_fm2)
                    emit_merge(s_x, sp_fm2, s_x, 1.0, ln_out=sp_tm2)
                    if stop_after != "p3":
                        emit_linattn(sp_tm2, sp_fm1)
                        emit_merge(s_x, sp_fm1, s_x, 1.0, ln_out=sp_tm2,
                                   attin_out=sp_tm)
                        if stop_after != "p4":
                            emit_ffn_attention(sp_tm, sp_fm1)
                            emit_merge(s_x, sp_fm1, s_x, 0.5, xf_tm=sp_tm2,
                                       hn_out=sp_hn)
                            emit_banks("b2", sp_hn, sp_fm2)
                            emit_merge(s_x, sp_fm2, x_out, 1.0)
                            full_chain = True

        if debug:
            p = Ph(nc)
            sd = S("dbg")
            dbg_row = E(nc.sbuf_tensor("dbg_row", [1, 16], F32))

            @p.on("vector")
            def _(v):
                v.tensor_copy(dbg_row[0:1, 0:16], idxr[0:1, 0:16]).then_inc(sd.h, 1)

            @p.on("sync")
            def _(s):
                s.wait_ge(sd.h, 1)
                s.dma_start(out=dbg["xf"][:, :], in_=sp_tm[:, :]).then_inc(sd.h, 16)
                s.dma_start(out=dbg["bfc0"][:, :], in_=b_fc[0][:, 0:256]).then_inc(sd.h, 16)
                s.dma_start(out=dbg["bsel"][:, :], in_=dbg_row[0:1, :]).then_inc(sd.h, 16)
                s.dma_start(out=dbg["att"][:, :], in_=sp_fm1[:, :]).then_inc(sd.h, 16)
                s.dma_start(out=dbg["sx"][:, :], in_=s_x[:, :]).then_inc(sd.h, 16)
                s.dma_start(out=dbg["hn"][:, :], in_=sp_hn[:, :]).then_inc(sd.h, 16)
                s.dma_start(out=dbg["awq"][:, :], in_=a_wq[:, :]).then_inc(sd.h, 16)
                s.dma_start(out=dbg["awk"][:, :], in_=a_wk[:, :]).then_inc(sd.h, 16)
                s.dma_start(out=dbg["h2"][:, :], in_=sp_h2[0:128, :]).then_inc(sd.h, 16)
                s.dma_start(out=dbg["pj"][:, :], in_=sp_fm2[:, :]).then_inc(sd.h, 16)
                s.dma_start(out=dbg["bpj0"][:, :],
                            in_=b_pj[0][0:128, 0:512]).then_inc(sd.h, 16)
                s.dma_start(out=dbg["xa"][:, :], in_=sp_tm2[:, :]).then_inc(sd.h, 16)
                s.dma_start(out=dbg["mac"][:, :], in_=sp_fm1[:, :]).then_inc(sd.h, 16)
                s.wait_ge(sd.h, 209)
            p.run()

        if not full_chain:
            p = Ph(nc)
            sf = S("fin")

            @p.on("sync")
            def _(s):
                s.dma_start(out=x_out[:, :], in_=x_in[:, :]).then_inc(sf.h, 16)
                s.wait_ge(sf.h, 16)
            p.run()

    return nc


def _finish_stub(nc, ctx, S, x_out, x_in, dbg, sp_tm, b_fc, idxr, E, debug):
    p = Ph(nc)
    sf = S("stub")

    @p.on("sync")
    def _(s):
        s.dma_start(out=x_out[:, :], in_=x_in[:, :]).then_inc(sf.h, 16)
        s.wait_ge(sf.h, 16)
    p.run()
    return nc


def emit_norm_pass(nc, ctx, S, E, src_f32, dst_bf, epscol, kind="ln"):
    p = Ph(nc)
    sl = S(f"n_{id(p) % 9999}_l")
    sv = S(f"n_{id(p) % 9999}_v")
    sa = S(f"n_{id(p) % 9999}_a")
    so = S(f"n_{id(p) % 9999}_o")
    with ExitStack() as c0:
        E0 = c0.enter_context
        xc = [E0(nc.sbuf_tensor(f"nx{i}_{id(p) % 9999}", [128, D], F32)) for i in range(2)]
        ob = [E0(nc.sbuf_tensor(f"no{i}_{id(p) % 9999}", [128, D], BF16)) for i in range(2)]
        stt = [E0(nc.sbuf_tensor(f"ns{i}_{id(p) % 9999}", [128, 8], F32)) for i in range(2)]
        sq = E0(nc.sbuf_tensor(f"nq_{id(p) % 9999}", [128, D], F32))

        apply_done = []
        store_done = []

        for tc in range(TCH):
            buf = tc % 2
            war_load = apply_done[tc - 2] if tc >= 2 else None
            war_store = store_done[tc - 2] if tc >= 2 else None

            @p.on("sync")
            def _(s, tc=tc, buf=buf, war=war_load):
                if war is not None:
                    s.wait_ge(sv.h, war)
                s.dma_start(out=xc[buf][:, :],
                            in_=src_f32[128 * tc:128 * (tc + 1), :]).then_inc(sl.h, 16)
            sl.inc(16)
            wl = sl.n

            base = sv.n

            @p.on("vector")
            def _(v, tc=tc, buf=buf, wl=wl, base=base, war=war_store):
                v.wait_ge(sl.h, wl)
                if war is not None:
                    v.wait_ge(so.h, war)
                if kind == "ln":
                    v.reduce_sum(stt[buf][:, 0:1], xc[buf][:, :], AX.X).then_inc(sv.h, 1)
                v.scalar_tensor_tensor(sq[:, :], xc[buf][:, :], 1.0, xc[buf][:, :],
                                       ALU.mult, ALU.mult,
                                       accum_out=stt[buf][:, 1:2]).then_inc(sv.h, 1)
            nred = 2 if kind == "ln" else 1
            sv.inc(nred)
            base2 = sv.n

            @p.on("vector")
            def _(v, buf=buf, base2=base2):
                v.wait_ge(sv.h, base2)
                if kind == "ln":
                    v.tensor_scalar_mul(stt[buf][:, 2:3], stt[buf][:, 0:1], 1.0 / D)
                    v.tensor_scalar_mul(stt[buf][:, 3:4], stt[buf][:, 1:2], 1.0 / D).then_inc(sv.h, 1)
                else:
                    v.tensor_scalar_mul(stt[buf][:, 3:4], stt[buf][:, 1:2], 1.0 / D).then_inc(sv.h, 1)
            sv.inc(1)
            base3 = sv.n

            @p.on("vector")
            def _(v, buf=buf, base3=base3):
                v.wait_ge(sv.h, base3)
                if kind == "ln":
                    v.scalar_tensor_tensor(stt[buf][:, 4:5], stt[buf][:, 2:3], -1.0,
                                           stt[buf][:, 2:3], ALU.mult, ALU.mult).then_inc(sv.h, 1)
                else:
                    v.tensor_copy(stt[buf][:, 4:5], stt[buf][:, 3:4]).then_inc(sv.h, 1)
            sv.inc(1)
            base4 = sv.n

            @p.on("vector")
            def _(v, buf=buf, base4=base4, isln=(kind == "ln")):
                v.wait_ge(sv.h, base4)
                if isln:
                    v.tensor_tensor(stt[buf][:, 4:5], stt[buf][:, 3:4], stt[buf][:, 4:5],
                                    ALU.add).then_inc(sv.h, 1)
                else:
                    v.engine_nop().then_inc(sv.h, 1)
            sv.inc(1)
            base5 = sv.n

            @p.on("scalar")
            def _(s, buf=buf, base5=base5, basea=sa.n):
                s.wait_ge(sv.h, base5)
                s.activation(stt[buf][:, 5:6], stt[buf][:, 4:5], AF.Ln,
                             bias=epscol).then_inc(sa.h, 1)
                s.wait_ge(sa.h, basea + 1)
                s.activation(stt[buf][:, 6:7], stt[buf][:, 5:6], AF.Exp, scale=-0.5).then_inc(sa.h, 1)
            sa.inc(2)
            base6 = sa.n

            @p.on("vector")
            def _(v, buf=buf, base6=base6, bs=sv.n, isln=(kind == "ln")):
                v.wait_ge(sa.h, base6)
                if isln:
                    v.scalar_tensor_tensor(stt[buf][:, 7:8], stt[buf][:, 2:3], -1.0,
                                           stt[buf][:, 6:7], ALU.mult, ALU.mult).then_inc(sv.h, 1)
                    v.wait_ge(sv.h, bs + 1)
                    v.tensor_scalar(ob[buf][:, :], xc[buf][:, :], stt[buf][:, 6:7],
                                    stt[buf][:, 7:8], ALU.mult, ALU.add).then_inc(sv.h, 1)
                else:
                    v.engine_nop().then_inc(sv.h, 1)
                    v.tensor_scalar(ob[buf][:, :], xc[buf][:, :], stt[buf][:, 6:7], None,
                                    ALU.mult).then_inc(sv.h, 1)
            sv.inc(2)
            apply_done.append(sv.n)
            wv2 = sv.n

            @p.on("gpsimd")
            def _(g, tc=tc, buf=buf, wv2=wv2):
                g.wait_ge(sv.h, wv2)
                g.dma_start(out=dst_bf[128 * tc:128 * (tc + 1), :],
                            in_=ob[buf][:, :]).then_inc(so.h, 16)
            so.inc(16)
            store_done.append(so.n)

        @p.on("gpsimd")
        def _(g, tot=so.n):
            g.wait_ge(so.h, tot)
        p.run()


_CACHE = {}


def kernel(**inputs):
    if "nc" not in _CACHE:
        _CACHE["nc"] = build(debug=False)
    nc = _CACHE["nc"]
    x = np.ascontiguousarray(np.asarray(inputs["L0_attn_x"], dtype=np.float32))
    shared = {
        "W_qk": inputs["L0_attn_W_qk"], "b_qk": inputs["L0_attn_b_qk"],
        "lepe_w": inputs["L0_attn_lepe_w"], "lepe_b": inputs["L0_attn_lepe_b"],
        "wq": inputs["L0_ffn_wq"], "wk": inputs["L0_ffn_wk"], "wv": inputs["L0_ffn_wv"],
        "wo": inputs["L0_ffn_wo"], "q_gain": inputs["L0_ffn_q_gain"],
        "k_gain": inputs["L0_ffn_k_gain"],
        "fc_s1": inputs["L0_ffn_fc_s1"], "fc_s2": inputs["L0_ffn_fc_s2"],
        "fc_mix": inputs["L0_ffn_fc_mix"], "fc_logits": inputs["L0_ffn_fc_logits"],
        "pj_s1": inputs["L0_ffn_pj_s1"], "pj_s2": inputs["L0_ffn_pj_s2"],
        "pj_mix": inputs["L0_ffn_pj_mix"], "pj_logits": inputs["L0_ffn_pj_logits"],
    }
    shared = {k: np.ascontiguousarray(np.asarray(v, dtype=np.float32)) for k, v in shared.items()}
    in_maps = [dict(shared, x=np.ascontiguousarray(x[c])) for c in range(8)]
    res = run_bass_kernel_spmd(nc, in_maps, core_ids=list(range(8)))
    return np.stack([r["y"] for r in res.results], axis=0)


# version 11
# speedup vs baseline: 1.3734x; 1.3734x over previous
import numpy as np
from contextlib import ExitStack

import concourse.bass as bass
import concourse.mybir as mybir
from concourse.bass_utils import run_bass_kernel_spmd
from concourse.bass_core_rust import IndirectOffsetOnAxis

F32 = mybir.dt.float32
BF16 = mybir.dt.bfloat16
I32 = mybir.dt.int32
U32 = mybir.dt.uint32
AF = mybir.ActivationFunctionType
ALU = mybir.AluOpType
AX = mybir.AxisListType

T = 2048
D = 1024
NH = 16
NKV = 4
HDim = 64
HID = 4096
NB = 8
TCH = T // 128
FCH = D // 128
LN1E4 = float(np.log(10000.0))
PI = float(np.pi)
MAGIC = 12582912.0
INV2PI = float(1.0 / (2 * np.pi))
TWOPI = float(2 * np.pi)

QPERM = [0, 4, 1, 5, 2, 6, 3, 7, 8, 12, 9, 13, 10, 14, 11, 15]

ENGS = ("sync", "scalar", "vector", "tensor", "gpsimd")


class Ph:
    def __init__(self, nc):
        self.nc = nc
        self.q = {e: [] for e in ENGS}

    def on(self, eng):
        def deco(fn):
            self.q[eng].append(fn)
            return None
        return deco

    def run(self):
        with self.nc.Block() as blk:
            for e in ENGS:
                fns = self.q[e]
                if fns:
                    getattr(blk, e)(lambda engine, fns=fns: [f(engine) for f in fns])


class Sem:
    def __init__(self, ctx, nc, name):
        self.h = ctx.enter_context(nc.semaphore(name))
        self.n = 0

    def inc(self, k=1):
        self.n += k
        return self.n


class BSem:

    def __init__(self, ctx, nc, name, n=2):
        self.s = [Sem(ctx, nc, f"{name}_{i}") for i in range(n)]
        self.n_ = n

    def slot(self, i):
        return self.s[i % self.n_]


def build(debug=False, stop_after=None):
    nc = bass.Bass("TRN2", target_bir_lowering=False)
    ap = lambda t: t.ap()

    x_in = ap(nc.dram_tensor("x", [T, D], F32, kind="ExternalInput"))
    W_qk = ap(nc.dram_tensor("W_qk", [2 * D, D], F32, kind="ExternalInput"))
    b_qk = ap(nc.dram_tensor("b_qk", [2 * D], F32, kind="ExternalInput"))
    lepe_w = ap(nc.dram_tensor("lepe_w", [D, 1, 3], F32, kind="ExternalInput"))
    lepe_b = ap(nc.dram_tensor("lepe_b", [D], F32, kind="ExternalInput"))
    wq = ap(nc.dram_tensor("wq", [D, D], F32, kind="ExternalInput"))
    wk = ap(nc.dram_tensor("wk", [NKV * HDim, D], F32, kind="ExternalInput"))
    wv = ap(nc.dram_tensor("wv", [NKV * HDim, D], F32, kind="ExternalInput"))
    wo = ap(nc.dram_tensor("wo", [D, D], F32, kind="ExternalInput"))
    q_gain = ap(nc.dram_tensor("q_gain", [NH], F32, kind="ExternalInput"))
    k_gain = ap(nc.dram_tensor("k_gain", [NKV], F32, kind="ExternalInput"))
    fc_s1 = ap(nc.dram_tensor("fc_s1", [NB, D, 64], F32, kind="ExternalInput"))
    fc_s2 = ap(nc.dram_tensor("fc_s2", [NB, HID, 64], F32, kind="ExternalInput"))
    fc_mix = ap(nc.dram_tensor("fc_mix", [NB, HID, D], F32, kind="ExternalInput"))
    fc_logits = ap(nc.dram_tensor("fc_logits", [NB], F32, kind="ExternalInput"))
    pj_s1 = ap(nc.dram_tensor("pj_s1", [NB, HID, 64], F32, kind="ExternalInput"))
    pj_s2 = ap(nc.dram_tensor("pj_s2", [NB, D, 64], F32, kind="ExternalInput"))
    pj_mix = ap(nc.dram_tensor("pj_mix", [NB, D, HID], F32, kind="ExternalInput"))
    pj_logits = ap(nc.dram_tensor("pj_logits", [NB], F32, kind="ExternalInput"))

    x_out = ap(nc.dram_tensor("y", [T, D], F32, kind="ExternalOutput"))

    dscr = lambda name, shape, dt=BF16: ap(nc.dram_tensor(name, shape, dt, kind="Internal"))
    s_x = dscr("s_x", [T, D], F32)
    sp_tm = dscr("sp_tm", [T, D])
    sp_tm2 = dscr("sp_tm2", [T, D])
    sp_hn = dscr("sp_hn", [T, D])
    sp_fm1 = dscr("sp_fm1", [D, T])
    sp_fm2 = dscr("sp_fm2", [D, T])
    a_wq = dscr("a_wq", [D, D])
    a_wk = dscr("a_wk", [NKV * HDim, D])
    a_wv = dscr("a_wv", [NKV * HDim, D])
    a_wo = dscr("a_wo", [D, D])
    a_Wqk = dscr("a_Wqk", [2 * D, D])
    a_fc = [dscr(f"a_fc{k}", [HID, D]) for k in range(2)]
    a_pj = [dscr(f"a_pj{k}", [D, HID]) for k in range(2)]
    b_fc = [dscr(f"b_fc{k}", [D, HID]) for k in range(2)]
    b_pj = [dscr(f"b_pj{k}", [HID, D]) for k in range(2)]
    sp_h2 = dscr("sp_h2", [HID, T])
    sp_kr = dscr("sp_kr", [D, T])

    _inv = 10000.0 ** (-2.0 * np.arange(32, dtype=np.float64) / 64.0)
    _ang = np.outer(_inv, np.arange(T, dtype=np.float64))
    _cosf = np.cos(_ang)
    _sinf = np.sin(_ang)
    _c2 = np.empty((128, T), np.float64)
    _s2 = np.empty((128, T), np.float64)
    for _r0 in (0, 64):
        _c2[_r0:_r0 + 32] = _cosf
        _c2[_r0 + 32:_r0 + 64] = _cosf
        _s2[_r0:_r0 + 32] = _sinf
        _s2[_r0 + 32:_r0 + 64] = -_sinf
    cos2_c = ap(nc.inline_tensor(np.ascontiguousarray(_c2, dtype=np.float32),
                                 name="cos2_c"))
    sin2_c = ap(nc.inline_tensor(np.ascontiguousarray(_s2, dtype=np.float32),
                                 name="sin2_c"))
    tabd = ap(nc.dram_tensor("tabd", [32, 128], I32, kind="Internal"))
    qg_d = ap(nc.dram_tensor("qg_d", [16], F32, kind="Internal"))
    kg_d = ap(nc.dram_tensor("kg_d", [4], F32, kind="Internal"))

    dbg = {}
    if debug:
        dbg["xf"] = ap(nc.dram_tensor("dbg_xf", [T, D], BF16, kind="ExternalOutput"))
        dbg["bsel"] = ap(nc.dram_tensor("dbg_bsel", [1, 16], F32, kind="ExternalOutput"))
        dbg["bfc0"] = ap(nc.dram_tensor("dbg_bfc0", [D, 256], BF16, kind="ExternalOutput"))
        dbg["att"] = ap(nc.dram_tensor("dbg_att", [D, T], BF16, kind="ExternalOutput"))
        dbg["sx"] = ap(nc.dram_tensor("dbg_sx", [T, D], F32, kind="ExternalOutput"))
        dbg["hn"] = ap(nc.dram_tensor("dbg_hn", [T, D], BF16, kind="ExternalOutput"))
        dbg["kr"] = ap(nc.dram_tensor("dbg_kr", [128, 2 * T], BF16, kind="ExternalOutput"))
        dbg["qr"] = ap(nc.dram_tensor("dbg_qr", [128, T], BF16, kind="ExternalOutput"))
        dbg["y"] = ap(nc.dram_tensor("dbg_y", [128, 8 * T], BF16, kind="ExternalOutput"))
        dbg["vo"] = ap(nc.dram_tensor("dbg_vo", [128, 16 * 320], BF16, kind="ExternalOutput"))
        dbg["qn"] = ap(nc.dram_tensor("dbg_qn", [128, T], BF16, kind="ExternalOutput"))
        dbg["awq"] = ap(nc.dram_tensor("dbg_awq", [D, D], BF16, kind="ExternalOutput"))
        dbg["awk"] = ap(nc.dram_tensor("dbg_awk", [256, D], BF16, kind="ExternalOutput"))
        dbg["h2"] = ap(nc.dram_tensor("dbg_h2", [128, T], BF16, kind="ExternalOutput"))
        dbg["pj"] = ap(nc.dram_tensor("dbg_pj", [D, T], BF16, kind="ExternalOutput"))
        dbg["bpj0"] = ap(nc.dram_tensor("dbg_bpj0", [128, 512], BF16, kind="ExternalOutput"))
        dbg["xa"] = ap(nc.dram_tensor("dbg_xa", [T, D], BF16, kind="ExternalOutput"))
        dbg["mac"] = ap(nc.dram_tensor("dbg_mac", [D, T], BF16, kind="ExternalOutput"))

    ctx = ExitStack()
    with ctx:
        E = ctx.enter_context
        S = lambda name: Sem(ctx, nc, name)

        from contextlib import contextmanager

        _sem_uid = [0]
        _nm_uid = [0]

        def mk_tag(base):
            _nm_uid[0] += 1
            return f"{base}{_nm_uid[0]}"

        @contextmanager
        def sem_scope():
            handles = []

            def S2(name):
                s = Sem.__new__(Sem)
                _sem_uid[0] += 1
                s.h = nc.alloc_semaphore(name=f"{name}.{_sem_uid[0]}")
                s.n = 0
                handles.append(s.h)
                return s

            def B2(name, n=2):
                b = BSem.__new__(BSem)
                b.s = [S2(f"{name}_{i}") for i in range(n)]
                b.n_ = n
                return b

            try:
                yield S2, B2
            finally:
                nc.clear_and_free_semaphores(handles)

        ones_f = E(nc.sbuf_tensor("ones_f", [1, 128], F32))
        ones_bf = E(nc.sbuf_tensor("ones_bf", [128, 1], BF16))
        small = E(nc.sbuf_tensor("small", [1, 64], F32))
        lgrow = E(nc.sbuf_tensor("lgrow", [1, 16], F32))
        maxr = E(nc.sbuf_tensor("maxr", [1, 16], F32))
        idxr = E(nc.sbuf_tensor("idxr", [1, 16], U32))
        coefr = E(nc.sbuf_tensor("coefr", [1, 8], F32))
        coefc = E(nc.sbuf_tensor("coefc", [128, 4], F32))
        negc = E(nc.sbuf_tensor("negc", [128, NH], F32))
        qg8 = E(nc.sbuf_tensor("qg8", [1, NH], F32))
        kgr = E(nc.sbuf_tensor("kgr", [1, NKV], F32))
        bqk_c = E(nc.sbuf_tensor("bqk_c", [128, 16], F32))
        bqk1_c = E(nc.sbuf_tensor("bqk1_c", [128, 16], F32))
        tri = E(nc.sbuf_tensor("tri", [128, 128], BF16))
        cosm = E(nc.sbuf_tensor("cosm", [128, FCH], F32))
        sinm = E(nc.sbuf_tensor("sinm", [128, FCH], F32))
        lwc = [E(nc.sbuf_tensor(f"lw{j}", [128, FCH], F32)) for j in range(3)]
        lbc = E(nc.sbuf_tensor("lb", [128, FCH], F32))
        cst = E(nc.sbuf_tensor("cst", [128, 4], F32))
        iv = E(nc.sbuf_tensor("iv", [128, 12], I32))
        ivs = E(nc.sbuf_tensor("ivs", [128, 4], I32))
        qgc = E(nc.sbuf_tensor("qgc", [2, 8], F32))
        qgt = E(nc.sbuf_tensor("qgt", [1, 24], F32))
        kgc = E(nc.sbuf_tensor("kgc", [2, 2], F32))
        ind2 = E(nc.sbuf_tensor("ind2", [2, 128], BF16))
        ones2 = E(nc.sbuf_tensor("ones2", [128, 2], BF16))
        onesr_bf = E(nc.sbuf_tensor("onesr_bf", [1, 128], BF16))
        cos2 = E(nc.sbuf_tensor("cos2", [128, T], BF16))
        sin2 = E(nc.sbuf_tensor("sin2", [128, T], BF16))

        with ExitStack() as c0:
            E0 = c0.enter_context
            itmp = E0(nc.sbuf_tensor("itmp", [128, T + 8], F32))
            itmp2 = E0(nc.sbuf_tensor("itmp2", [32, T], F32))
            itab = E0(nc.sbuf_tensor("itab", [1, 32 * 128], I32))
            cosf = E0(nc.sbuf_tensor("cosf", [32, T], F32))
            sinf = E0(nc.sbuf_tensor("sinf", [32, T], F32))
            pb = [E0(nc.psum_tensor(f"pb{i}", [128, 512], F32)) for i in range(2)]

            p = Ph(nc)
            sl = S("a_l"); sv = S("a_v"); sc = S("a_s"); sg = S("a_g"); st = S("a_t")
            slg = S("a_lg"); sll = S("a_ll"); slog = S("a_log"); sgd = S("a_gd"); slp = S("a_lp"); std_ = S("a_td")

            @p.on("sync")
            def _(s):
                s.dma_start(out=lgrow[0:1, 0:8], in_=fc_logits[None, :]).then_inc(slog.h, 16)
                s.dma_start(out=lgrow[0:1, 8:16], in_=pj_logits[None, :]).then_inc(slog.h, 16)
                s.dma_start(out=small[0:1, 0:16], in_=q_gain[None, :]).then_inc(sl.h, 16)
                s.dma_start(out=kgr[0:1, 0:4], in_=k_gain[None, :]).then_inc(sl.h, 16)
            sl.inc(16 * 2)
            slog.inc(16 * 2)
            sl_small0 = sl.n

            @p.on("scalar")
            def _(s):
                srcb = lepe_b.rearrange("(c p) -> c p", c=FCH)
                for c in range(FCH):
                    s.dma_start(out=lbc[:, c:c + 1], in_=srcb[c][:, None]).then_inc(sll.h, 16)
            sll.inc(16 * 8)
            sll_all = sll.n

            @p.on("gpsimd")
            def _(g):
                g.memset(ones_f[0:1, :], 1.0)
                g.memset(ones_bf[:, :], 1.0)
                g.memset(cst[:, 0:1], PI / 2.0)
                g.memset(cst[:, 1:2], 1e-5)
                g.memset(cst[:, 2:3], 1e-6)
                g.memset(tri[:, :], 1.0)
                g.memset(onesr_bf[0:1, :], 1.0)
                g.memset(ones2[:, 0:2], 0.0)
                g.memset(ind2[0:2, :], 0.0)
                g.memset(itmp[0:1, 0:1], 1.0)
                g.engine_nop().then_inc(sg.h, 1)
                g.wait_ge(sg.h, 1)
                g.memset(ones2[0:64, 0:1], 1.0)
                g.memset(ones2[64:128, 1:2], 1.0)
                g.memset(ind2[0:1, 0:64], 1.0)
                g.affine_select(tri[:, :], tri[:, :], [[1, 128]], ALU.is_ge, 0.0,
                                base=0, channel_multiplier=-1)
                g.iota(itmp[0:32, 0:1], [[0, 1]], base=0, channel_multiplier=1,
                       allow_small_or_imprecise_dtypes=True)
                g.iota(itmp[0:1, 8:8 + T], [[1, T]], base=0, channel_multiplier=0,
                       allow_small_or_imprecise_dtypes=True)
                for c in range(FCH):
                    for (r0, bb) in ((0, 64 * c), (32, 64 * c), (64, 64 * c + 32), (96, 64 * c + 32)):
                        g.iota(cosm[r0:r0 + 32, c:c + 1], [[0, 1]], base=bb, channel_multiplier=1,
                               allow_small_or_imprecise_dtypes=True)
                g.iota(ivs[0:32, 0:1], [[0, 1]], base=0, channel_multiplier=2)
                g.iota(ivs[32:64, 0:1], [[0, 1]], base=1, channel_multiplier=2)
                g.iota(ivs[64:96, 0:1], [[0, 1]], base=64, channel_multiplier=2)
                g.iota(ivs[96:128, 0:1], [[0, 1]], base=65, channel_multiplier=2)
                g.iota(ivs[0:64, 1:2], [[0, 1]], base=0, channel_multiplier=1)
                g.iota(ivs[64:128, 1:2], [[0, 1]], base=256, channel_multiplier=1)
                g.iota(ivs[:, 2:3], [[0, 1]], base=0, channel_multiplier=3)
                R = lambda r, c0, c1: itab[0:1, 128 * r + c0:128 * r + c1]
                for b in range(8):
                    g.iota(R(b, 0, 128), [[1, 128]], base=HID * b, channel_multiplier=0)
                    g.iota(R(8 + b, 0, 128), [[1, 128]], base=D * b, channel_multiplier=0)
                for s_ in range(8):
                    r = 16 + s_
                    for a in range(8):
                        if s_ > 0:
                            g.iota(R(r, 16 * a, 16 * a + s_), [[64, s_]],
                                   base=(16 - s_) * 64 + a, channel_multiplier=0)
                        g.iota(R(r, 16 * a + s_, 16 * a + 16), [[64, 16 - s_]], base=a, channel_multiplier=0)
                for s_ in range(8):
                    r = 24 + s_
                    for cc in range(2):
                        if s_ > 0:
                            g.iota(R(r, 64 * cc, 64 * cc + s_), [[64, s_]],
                                   base=(64 - s_) * 64 + cc, channel_multiplier=0)
                        g.iota(R(r, 64 * cc + s_, 64 * cc + 64), [[64, 64 - s_]], base=cc, channel_multiplier=0)
                g.engine_nop().then_inc(sg.h, 1)
                g.wait_ge(sg.h, 2)
                g.dma_start(out=tabd[:, :], in_=itab[0:1, :]).then_inc(std_.h, 16)
                for c in range(16):
                    g.indirect_dma_start(bqk_c[:, c:c + 1], None, b_qk[:, None],
                                         IndirectOffsetOnAxis(ivs[:, 0:1], 0),
                                         element_offset=128 * c).then_inc(slg.h, 16)
                lw_f = lepe_w.rearrange("d one k -> (d one k)")
                for j in range(3):
                    for c in range(FCH):
                        g.indirect_dma_start(lwc[j][:, c:c + 1], None, lw_f[:, None],
                                             IndirectOffsetOnAxis(ivs[:, 2:3], 0),
                                             element_offset=384 * c + j).then_inc(slp.h, 16)
            sg.inc(2)
            sg_iota = sg.n
            std_.inc(16)
            sg_tabs = std_.n
            sg_p0a = std_
            slg.inc(16 * 16)
            slg_all = slg.n
            slp.inc(16 * 24)
            slp_all = slp.n

            @p.on("vector")
            def _(v):
                v.wait_ge(slog.h, 32)
                v.max(out=maxr[0:1, 0:8], in_=lgrow[0:1, 0:8]).then_inc(sv.h, 1)
                v.max(out=maxr[0:1, 8:16], in_=lgrow[0:1, 8:16]).then_inc(sv.h, 1)
                v.wait_ge(sv.h, 2)
                v.max_index(idxr[0:1, 0:8], maxr[0:1, 0:8], lgrow[0:1, 0:8]).then_inc(sv.h, 1)
                v.max_index(idxr[0:1, 8:16], maxr[0:1, 8:16], lgrow[0:1, 8:16]).then_inc(sv.h, 1)
                v.wait_ge(sv.h, 4)
                v.tensor_tensor(coefr[0:1, 4:5], maxr[0:1, 1:2], maxr[0:1, 0:1], ALU.subtract)
                v.tensor_tensor(coefr[0:1, 5:6], maxr[0:1, 9:10], maxr[0:1, 8:9],
                                ALU.subtract).then_inc(sv.h, 1)
            sv.inc(5)
            sv_topk = sv.n

            @p.on("scalar")
            def _(s):
                s.wait_ge(sv.h, sv_topk)
                s.activation(coefr[0:1, 6:8], coefr[0:1, 4:6], AF.Exp).then_inc(sc.h, 1)
            sc_exp = sc.inc()

            @p.on("vector")
            def _(v):
                v.wait_ge(sc.h, sc_exp)
                v.tensor_single_scalar(coefr[0:1, 2:4], coefr[0:1, 6:8], 1.0, ALU.add).then_inc(sv.h, 1)
                v.wait_ge(sv.h, sv_topk + 1)
                v.reciprocal(coefr[0:1, 0:2], coefr[0:1, 2:4]).then_inc(sv.h, 1)
                v.wait_ge(sv.h, sv_topk + 2)
                v.tensor_scalar(coefr[0:1, 4:6], coefr[0:1, 0:2], -1.0, 1.0,
                                ALU.mult, ALU.add).then_inc(sv.h, 1)
                v.wait_ge(sv.h, sv_topk + 3)
                v.tensor_copy(maxr[0:1, 12:13], coefr[0:1, 0:1])
                v.tensor_copy(maxr[0:1, 13:14], coefr[0:1, 4:5])
                v.tensor_scalar_mul(maxr[0:1, 14:15], coefr[0:1, 1:2], 0.5)
                v.tensor_scalar_mul(maxr[0:1, 15:16], coefr[0:1, 5:6], 0.5).then_inc(sv.h, 1)
            sv.inc(4)
            sv_coef = sv.n

            @p.on("vector")
            def _(v):
                v.wait_ge(sl.h, 32)
                for i, h in enumerate(QPERM):
                    v.tensor_scalar_mul(qg8[0:1, i:i + 1], small[0:1, h:h + 1], 0.125)
                    v.tensor_tensor(small[0:1, 40 + i:41 + i], small[0:1, h:h + 1],
                                    kgr[0:1, h // 4:h // 4 + 1], ALU.mult)
                v.engine_nop().then_inc(sv.h, 1)
                v.wait_ge(sv.h, sv_coef + 1)
                v.tensor_scalar_mul(small[0:1, 24:40], small[0:1, 40:56], -1.0).then_inc(sv.h, 1)
                v.wait_ge(sv.h, sv_coef + 2)
                v.tensor_tensor(small[0:1, 40:56], small[0:1, 40:56], small[0:1, 24:40],
                                ALU.max).then_inc(sv.h, 1)
                v.wait_ge(sv.h, sv_coef + 3)
                v.tensor_scalar_mul(small[0:1, 40:56], small[0:1, 40:56], -8.0).then_inc(sv.h, 1)
                v.wait_ge(slg.h, 16 * 16)
                v.tensor_single_scalar(bqk1_c[:, :], bqk_c[:, :], 1.0, ALU.add).then_inc(sv.h, 1)
            sv.inc(5)
            sv_gain = sv.n

            _sgbase = sgd.n

            @p.on("vector")
            def _(v):
                v.wait_ge(sv.h, sv_gain)
                v.tensor_copy(qgt[0:1, 0:8], qg8[0:1, 0::2])
                v.tensor_copy(qgt[0:1, 8:16], qg8[0:1, 1::2])
                v.tensor_copy(qgt[0:1, 16:18], kgr[0:1, 0:4:2])
                v.tensor_copy(qgt[0:1, 18:20], kgr[0:1, 1:4:2]).then_inc(sv.h, 1)
            sv_deint = sv.inc()

            @p.on("gpsimd")
            def _(g, _sgbase=_sgbase):
                g.wait_ge(sv.h, sv_deint)
                g.dma_start(out=qg_d[None, :], in_=qgt[0:1, 0:16]).then_inc(sgd.h, 16)
                g.dma_start(out=kg_d[None, :], in_=qgt[0:1, 16:20]).then_inc(sgd.h, 16)
                g.wait_ge(sgd.h, _sgbase + 32)
                g.dma_start(out=qgc[:, :],
                            in_=qg_d.rearrange("(r c) -> r c", r=2)).then_inc(sgd.h, 16)
                g.dma_start(out=kgc[:, :],
                            in_=kg_d.rearrange("(r c) -> r c", r=2)).then_inc(sgd.h, 16)
                g.dma_start(out=ind2[1:2, 64:128], in_=ind2[0:1, 0:64]).then_inc(sgd.h, 16)
            sv_gain_post = sgd.inc(16 * 5)

            @p.on("gpsimd")
            def _(g):
                g.wait_ge(sgd.h, sv_gain_post)
                g.wait_ge(std_.h, sg_tabs)
                g.wait_ge(sl.h, 32)
                g.wait_ge(slg.h, slg_all)
                g.wait_ge(slp.h, slp_all)
                g.wait_ge(sll.h, sll_all)


            @p.on("tensor")
            def _(t):
                t.wait_ge(sg.h, 1)
                t.wait_ge(sv.h, sv_coef)
                t.matmul(pb[0][:, 0:4], ones_f[0:1, :], maxr[0:1, 12:16],
                         start=True, stop=True).then_inc(st.h, 1)
                t.wait_ge(sv.h, sv_gain)
                t.matmul(pb[1][:, 0:16], ones_f[0:1, :], small[0:1, 40:56],
                         start=True, stop=True).then_inc(st.h, 1)
            st.inc(2)

            @p.on("vector")
            def _(v):
                v.wait_ge(st.h, 1)
                v.tensor_copy(coefc[:, :], pb[0][:, 0:4]).then_inc(sv.h, 1)
                v.wait_ge(st.h, 2)
                v.tensor_copy(negc[:, :], pb[1][:, 0:16]).then_inc(sv.h, 1)
            sv.inc(2)

            @p.on("scalar")
            def _(s):
                s.wait_ge(sg.h, sg_iota)
                s.activation(itmp[0:32, 0:1], itmp[0:32, 0:1], AF.Exp,
                             scale=-LN1E4 / 32.0).then_inc(sc.h, 1)
                s.activation(sinm[:, :], cosm[:, :], AF.Exp,
                             scale=-LN1E4 / 512.0).then_inc(sc.h, 1)
            sc.inc(2)
            sc_inv = sc.n

            p.run()
            if stop_after == "p0a":
                return _finish_stub(nc, ctx, S, x_out, x_in, dbg, sp_tm, b_fc, idxr, E, debug)

            p = Ph(nc)
            sbD = S("b_d"); s2 = S("b_2")

            @p.on("gpsimd")
            def _(g):
                g.dma_start(out=cos2[:, :], in_=cos2_c[:, :]).then_inc(sbD.h, 16)
                g.dma_start(out=sin2[:, :], in_=sin2_c[:, :]).then_inc(sbD.h, 16)
                g.wait_ge(sbD.h, 32)

            @p.on("scalar")
            def _(s):
                s.activation(cosm[:, :], sinm[:, :], AF.Sin,
                             bias=cst[:, 0:1]).then_inc(s2.h, 1)
                s.wait_ge(s2.h, 1)
                s.activation(sinm[:, :], sinm[:, :], AF.Sin).then_inc(s2.h, 1)
                s.wait_ge(s2.h, 2)
            p.run()
            if stop_after == "p0b":
                return _finish_stub(nc, ctx, S, x_out, x_in, dbg, sp_tm, b_fc, idxr, E, debug)

        with ExitStack() as c0:
            E0 = c0.enter_context
            stg = [E0(nc.sbuf_tensor(f"stg{i}", [128, HID], F32)) for i in range(2)]
            stgb = [E0(nc.sbuf_tensor(f"stgb{i}", [128, HID], BF16)) for i in range(2)]
            trb = [E0(nc.sbuf_tensor(f"trb{i}", [128, HID], BF16, is_transpose_dest=True))
                   for i in range(2)]

            p = Ph(nc)
            sC = S("c_c")
            bLg = BSem(ctx, nc, "c_lg")
            bLp = BSem(ctx, nc, "c_lp")
            bW = BSem(ctx, nc, "c_w")

            wqk_f = W_qk
            wq_f = wq
            cast_jobs = []
            for rc in range(16):
                cast_jobs.append(("gather", (wqk_f, 0, 128 * rc * D), a_Wqk, rc, False))
            for rc in range(8):
                pa = QPERM[2 * rc]
                cast_jobs.append(("gather", (wq_f, 1, 64 * pa * D), a_wq, rc, False))
            for rc in range(2):
                cast_jobs.append(("plain", wk[128 * rc:128 * (rc + 1), :], a_wk, rc, False))
            for rc in range(2):
                cast_jobs.append(("plain", wv[128 * rc:128 * (rc + 1), :], a_wv, rc, False))
            for rc in range(8):
                cast_jobs.append(("plain", wo[128 * rc:128 * (rc + 1), :], a_wo, rc, True))

            for jid, (mode, info, dst, rc, colperm) in enumerate(cast_jobs):
                buf = jid % 2
                sLs = (bLg if mode == "gather" else bLp).slot(jid)
                sWs = bW.slot(jid)
                if mode == "gather":
                    @p.on("gpsimd")
                    def _(g, info=info, buf=buf, jid=jid, sLs=sLs):
                        if jid >= 2:
                            g.wait_ge(sC.h, jid - 1)
                        (srcf, ivc, eo) = info
                        g.indirect_dma_start(stg[buf][:, 0:D], None, srcf,
                                             IndirectOffsetOnAxis(ivs[:, ivc:ivc + 1], 0),
                                             element_offset=eo).then_inc(sLs.h, 16)
                else:
                    @p.on("sync")
                    def _(s, info=info, buf=buf, jid=jid, sLs=sLs):
                        if jid >= 2:
                            s.wait_ge(sC.h, jid - 1)
                        s.dma_start(out=stg[buf][:, 0:D], in_=info).then_inc(sLs.h, 16)
                wL = sLs.inc(16)

                @p.on("vector")
                def _(v, buf=buf, wL=wL, jid=jid, colperm=colperm, sLs=sLs,
                      wW=(sWs.n if jid >= 2 else None), sWs=sWs):
                    v.wait_ge(sLs.h, wL)
                    if wW is not None:
                        v.wait_ge(sWs.h, wW)
                    if colperm:
                        sv_ = stg[buf][:, 0:D].rearrange(
                            "p (half pair jh d) -> p half jh pair d",
                            half=2, pair=2, jh=4, d=64)
                        for half in range(2):
                            for jh in range(4):
                                op = v.tensor_copy(
                                    stgb[buf][:, 128 * (half * 4 + jh):
                                              128 * (half * 4 + jh) + 128],
                                    sv_[:, half, jh, :, :])
                        op.then_inc(sC.h, 1)
                    else:
                        v.tensor_copy(stgb[buf][:, 0:D], stg[buf][:, 0:D]).then_inc(sC.h, 1)
                wC = sC.inc()

                @p.on("scalar")
                def _(s, dst=dst, rc=rc, buf=buf, wC=wC, sWs=sWs):
                    s.wait_ge(sC.h, wC)
                    s.dma_start(out=dst[128 * rc:128 * (rc + 1), :],
                                in_=stgb[buf][:, 0:D]).then_inc(sWs.h, 16)
                sWs.inc(16)

            @p.on("scalar")
            def _(s, t0=bW.s[0].n, t1=bW.s[1].n):
                s.wait_ge(bW.s[0].h, t0)
                s.wait_ge(bW.s[1].h, t1)
            p.run()

            if stop_after == "p0c":
                return _finish_stub(nc, ctx, S, x_out, x_in, dbg, sp_tm, b_fc, idxr, E, debug)
            p = Ph(nc)
            sI = S("d_i"); sC = S("d_c")
            bL = BSem(ctx, nc, "d_l")
            bW = BSem(ctx, nc, "d_w")
            bT = BSem(ctx, nc, "d_t")
            bB = BSem(ctx, nc, "d_b")

            @p.on("sync")
            def _(s):
                s.wait_ge(sg_p0a.h, sg_tabs)
                with ExitStack() as rs:
                    rr = [rs.enter_context(s.register(f"ri{i}")) for i in range(4)]
                    s.reg_load(rr[0], idxr[0:1, 0:1])
                    s.reg_load(rr[1], idxr[0:1, 1:2])
                    s.reg_load(rr[2], idxr[0:1, 8:9])
                    s.reg_load(rr[3], idxr[0:1, 9:10])
                    snp = [s.snap(r, min_val=0, max_val=7) for r in rr]
                    for k in range(2):
                        s.dma_start(out=iv[:, k:k + 1],
                                    in_=tabd[bass.ds(snp[k], 1), :]).then_inc(sI.h, 16)
                        s.dma_start(out=iv[:, 2 + k:3 + k],
                                    in_=tabd[8:][bass.ds(snp[k], 1), :]).then_inc(sI.h, 16)
                        s.dma_start(out=iv[:, 4 + k:5 + k],
                                    in_=tabd[8:][bass.ds(snp[2 + k], 1), :]).then_inc(sI.h, 16)
                        s.dma_start(out=iv[:, 6 + k:7 + k],
                                    in_=tabd[bass.ds(snp[2 + k], 1), :]).then_inc(sI.h, 16)
                        s.dma_start(out=iv[:, 8 + k:9 + k],
                                    in_=tabd[16:][bass.ds(snp[k], 1), :]).then_inc(sI.h, 16)
                        s.dma_start(out=iv[:, 10 + k:11 + k],
                                    in_=tabd[24:][bass.ds(snp[2 + k], 1), :]).then_inc(sI.h, 16)
            sI.inc(16 * 12)
            iv_done = sI.n

            fc_mix_f = fc_mix.rearrange("b o i -> (b o) i")
            pj_mix_f = pj_mix.rearrange("b o i -> (b o) i")
            mix_jobs = [(fc_mix_f, 0, a_fc[0], 32, D), (fc_mix_f, 1, a_fc[1], 32, D),
                        (pj_mix_f, 4, a_pj[0], 8, HID), (pj_mix_f, 5, a_pj[1], 8, HID)]

            cnt = 0
            for (srcf, ivc, dst, nrc, cols) in mix_jobs:
                for rc in range(nrc):
                    buf = cnt % 2
                    sLs = bL.slot(cnt)
                    sWs = bW.slot(cnt)

                    @p.on("gpsimd")
                    def _(g, srcf=srcf, ivc=ivc, rc=rc, buf=buf, cols=cols, cnt=cnt,
                          sLs=sLs):
                        if cnt == 0:
                            g.wait_ge(sI.h, iv_done)
                        if cnt >= 2:
                            g.wait_ge(sC.h, cnt - 1)
                        g.indirect_dma_start(
                            stg[buf][:, 0:cols], None, srcf,
                            IndirectOffsetOnAxis(iv[:, ivc:ivc + 1], 0),
                            element_offset=128 * rc * cols).then_inc(sLs.h, 16)
                    wL = sLs.inc(16)

                    @p.on("vector")
                    def _(v, buf=buf, wL=wL, cols=cols, cnt=cnt, sLs=sLs, sWs=sWs,
                          wW=(sWs.n if cnt >= 2 else None)):
                        v.wait_ge(sLs.h, wL)
                        if wW is not None:
                            v.wait_ge(sWs.h, wW)
                        v.tensor_copy(stgb[buf][:, 0:cols], stg[buf][:, 0:cols]).then_inc(sC.h, 1)
                    wC = sC.inc()

                    @p.on("scalar")
                    def _(s, dst=dst, rc=rc, buf=buf, wC=wC, cols=cols, sWs=sWs):
                        s.wait_ge(sC.h, wC)
                        s.dma_start(out=dst[128 * rc:128 * (rc + 1), 0:cols],
                                    in_=stgb[buf][:, 0:cols]).then_inc(sWs.h, 16)
                    sWs.inc(16)
                    cnt += 1
            mix_w_tot = (bW.s[0].n, bW.s[1].n)

            tcnt = 0
            jobs_t = [("fc", k, ic) for k in range(2) for ic in range(FCH)] + \
                     [("pj", k, ic) for k in range(2) for ic in range(HID // 128)]
            for (kind, k, ic) in jobs_t:
                buf = tcnt % 2
                sTs = bT.slot(tcnt)
                sBs = bB.slot(tcnt)
                ntr = 8 if kind == "fc" else 2
                src_t = a_fc[k] if kind == "fc" else a_pj[k]
                dst_t = b_fc[k] if kind == "fc" else b_pj[k]
                ivcol = (8 + k) if kind == "fc" else (10 + k)
                eoff = 8 * ic * HID if kind == "fc" else 2 * ic * D
                width = HID if kind == "fc" else D

                @p.on("sync")
                def _(s, buf=buf, ntr=ntr, src_t=src_t, ic=ic, sTs=sTs,
                      warv=(sBs.n if tcnt >= 2 else None), sBs=sBs,
                      first=(tcnt == 0), mt=mix_w_tot):
                    if first:
                        s.wait_ge(bW.s[0].h, mt[0])
                        s.wait_ge(bW.s[1].h, mt[1])
                    if warv is not None:
                        s.wait_ge(sBs.h, warv)
                    for og in range(ntr):
                        s.dma_start(out=trb[buf][:, 512 * og:512 * (og + 1)],
                                    in_=src_t[512 * og:512 * (og + 1),
                                              128 * ic:128 * (ic + 1)],
                                    transpose=True).then_inc(sTs.h, 16)
                wT = sTs.inc(16 * ntr)

                @p.on("gpsimd")
                def _(g, dst_t=dst_t, ivcol=ivcol, eoff=eoff, buf=buf, wT=wT,
                      width=width, sTs=sTs, sBs=sBs):
                    g.wait_ge(sTs.h, wT)
                    g.indirect_dma_start(
                        dst_t, IndirectOffsetOnAxis(iv[:, ivcol:ivcol + 1], 0),
                        trb[buf][:, 0:width], None,
                        element_offset=eoff).then_inc(sBs.h, 16)
                sBs.inc(16)
                tcnt += 1

            @p.on("gpsimd")
            def _(g, t0=bB.s[0].n, t1=bB.s[1].n):
                g.wait_ge(bB.s[0].h, t0)
                g.wait_ge(bB.s[1].h, t1)
            p.run()

        def emit_s12(tag, E1, which):
            if which == "fc":
                n1, n2 = 8, 32
                s1_f = fc_s1.rearrange("b r c -> (b r) c")
                s2_f = fc_s2.rearrange("b r c -> (b r) c")
                iv1, iv2, cf0 = (2, 3), (0, 1), 0
            else:
                n1, n2 = 32, 8
                s1_f = pj_s1.rearrange("b r c -> (b r) c")
                s2_f = pj_s2.rearrange("b r c -> (b r) c")
                iv1, iv2, cf0 = (6, 7), (4, 5), 2
            s1t = [E1(nc.sbuf_tensor(f"s1{tag}{k}", [128, n1 * 128], BF16))
                   for k in range(2)]
            s2t = [E1(nc.sbuf_tensor(f"s2{tag}{k}", [128, n2 * 128], BF16))
                   for k in range(2)]
            with ExitStack() as c1:
                E2 = c1.enter_context
                S2, B2 = E2(sem_scope())
                stg = [E2(nc.sbuf_tensor(f"sg{tag}{i}", [128, 64], F32)) for i in range(2)]
                p = Ph(nc)
                sC = S2("e_c"); sG = S2("e_g"); sF = S2("e_f")
                bL = B2("e_l")

                s_jobs = [(s1_f, iv1[0], s1t[0], n1, None),
                          (s1_f, iv1[1], s1t[1], n1, None),
                          (s2_f, iv2[0], s2t[0], n2, cf0),
                          (s2_f, iv2[1], s2t[1], n2, cf0 + 1)]

                @p.on("vector")
                def _(v):
                    for k in range(2):
                        v.memset(s1t[k][:, :], 0.0)
                        v.memset(s2t[k][:, :], 0.0)
                    v.engine_nop().then_inc(sG.h, 1)

                cnt = 0
                for (srcf, ivc, dstt, npair, cfc) in s_jobs:
                    for pr in range(npair):
                        buf = cnt % 2
                        sLs = bL.slot(cnt)

                        @p.on("gpsimd")
                        def _(g, srcf=srcf, ivc=ivc, pr=pr, buf=buf, cnt=cnt, sLs=sLs):
                            if cnt >= 2:
                                g.wait_ge(sC.h, cnt - 1)
                            g.indirect_dma_start(
                                stg[buf][:, 0:64], None, srcf,
                                IndirectOffsetOnAxis(iv[:, ivc:ivc + 1], 0),
                                element_offset=128 * pr * 64).then_inc(sLs.h, 16)
                        wL = sLs.inc(16)

                        @p.on("vector")
                        def _(v, dstt=dstt, pr=pr, buf=buf, wL=wL, first=(cnt == 0),
                              sLs=sLs):
                            v.wait_ge(sLs.h, wL)
                            if first:
                                v.wait_ge(sG.h, 1)
                            v.tensor_copy(dstt[0:64, 128 * pr:128 * pr + 64],
                                          stg[buf][0:64, 0:64])
                            v.tensor_copy(dstt[64:128, 128 * pr + 64:128 * pr + 128],
                                          stg[buf][64:128, 0:64]).then_inc(sC.h, 1)
                        sC.inc()
                        cnt += 1
                    if cfc is not None:
                        @p.on("vector")
                        def _(v, dstt=dstt, cfc=cfc, tot=sC.n):
                            v.wait_ge(sC.h, tot)
                            v.tensor_scalar(dstt[:, :], dstt[:, :],
                                            coefc[:, cfc:cfc + 1], None,
                                            ALU.mult).then_inc(sF.h, 1)
                        sF.inc()

                @p.on("gpsimd")
                def _(g, tot=sC.n, ftot=sF.n):
                    g.wait_ge(sC.h, tot)
                    g.wait_ge(sF.h, ftot)
                p.run()
            return s1t, s2t

        def emit_banks(tag, hn_tm, pj_out_fm):
            NOC = HID // 128

            with ExitStack() as c1:
                E1 = c1.enter_context
                s1fc, s2fc = emit_s12(tag + "f", E1, "fc")
                S2, B2 = E1(sem_scope())
                hnf = E1(nc.sbuf_tensor(f"bk{tag}_hn", [128, 8 * T], BF16))
                hbf = [E1(nc.sbuf_tensor(f"bk{tag}_hb{k}", [128, 8 * T], BF16))
                       for k in range(2)]
                btile = [E1(nc.sbuf_tensor(f"bk{tag}_w{i}", [128, 8 * 128], BF16))
                         for i in range(2)]
                msb = [E1(nc.sbuf_tensor(f"bk{tag}_m{k}", [128, 4 * 512], BF16))
                       for k in range(2)]
                ub = [E1(nc.sbuf_tensor(f"bk{tag}_u{i}", [128, 512], F32))
                      for i in range(2)]
                h2b = [E1(nc.sbuf_tensor(f"bk{tag}_2{i}", [128, 512], BF16))
                       for i in range(2)]
                PM = [E1(nc.psum_tensor(f"bk{tag}_pm{i}", [128, 512], F32))
                      for i in range(4)]
                PB = [E1(nc.psum_tensor(f"bk{tag}_pb{i}", [128, 512], F32))
                      for i in range(2)]

                p = Ph(nc)
                sT = S2("bkt"); sA = S2("bka"); sv = S2("bkv"); sG = S2("bkg")
                bX = B2("bkx"); bW = B2("bkw"); bS = B2("bkh")

                @p.on("sync")
                def _(s):
                    for fc in range(8):
                        for tg in range(4):
                            s.dma_start(
                                out=hnf[:, fc * T + 512 * tg:fc * T + 512 * (tg + 1)],
                                in_=hn_tm[512 * tg:512 * (tg + 1),
                                          128 * fc:128 * (fc + 1)],
                                transpose=True).then_inc(bX.s[0].h, 16)
                bX.s[0].inc(16 * 32)
                hn_done = bX.s[0].n

                pm_free = [None] * 4
                cnt = 0
                for k in range(2):
                    for pr in range(8):
                        for tcb in range(4):
                            bank = cnt % 4

                            @p.on("tensor")
                            def _(t, k=k, pr=pr, tcb=tcb, bank=bank,
                                  first=(cnt == 0), war=pm_free[bank]):
                                if first:
                                    t.wait_ge(bX.s[0].h, hn_done)
                                if war is not None:
                                    t.wait_ge(war[0].h, war[1])
                                t.matmul(PM[bank][:, :],
                                         s1fc[k][:, 128 * pr:128 * (pr + 1)],
                                         hnf[:, pr * T + 512 * tcb:
                                             pr * T + 512 * (tcb + 1)],
                                         start=True, stop=True).then_inc(sT.h, 1)
                            mm = sT.inc()

                            eng = "scalar"
                            esem = sA

                            @p.on(eng)
                            def _(e, k=k, pr=pr, tcb=tcb, bank=bank, mm=mm,
                                  esem=esem, sc=(eng == "scalar")):
                                e.wait_ge(sT.h, mm)
                                dst = hbf[k][:, pr * T + 512 * tcb:
                                             pr * T + 512 * (tcb + 1)]
                                if sc:
                                    e.activation(dst, PM[bank][:, :],
                                                 AF.Copy).then_inc(esem.h, 1)
                                else:
                                    e.tensor_copy(dst, PM[bank][:, :]).then_inc(esem.h, 1)
                            pm_free[bank] = (esem, esem.inc())
                            cnt += 1
                bd1_allA = sA.n
                bd1_allG = sG.n

                bfc_v = [b_fc[kk].rearrange("(i p) c -> p i c", p=128)
                         for kk in range(2)]
                btile_free = [None, None]
                msb_free = {}
                pb_free = [None, None]
                ub_free = [None, None]
                h2b_free = [None, None]
                ld = 0
                ucnt = 0
                for oc in range(NOC):
                    for k in range(2):
                        buf = ld % 2
                        sWs = bW.slot(ld)

                        @p.on("sync")
                        def _(s, oc=oc, k=k, buf=buf, war=btile_free[buf], sWs=sWs):
                            if war is not None:
                                s.wait_ge(sT.h, war)
                            s.dma_start(
                                out=btile[buf][:, :].rearrange("p (i c) -> p i c",
                                                               c=128),
                                in_=bfc_v[k][:, :, 128 * oc:128 * (oc + 1)]
                            ).then_inc(sWs.h, 16)
                        wW = sWs.inc(16)

                        for tcb in range(4):
                            bank = (4 * ld + tcb) % 4

                            @p.on("tensor")
                            def _(t, k=k, tcb=tcb, buf=buf, bank=bank, wW=wW,
                                  sWs=sWs, war=pm_free[bank],
                                  first=(ld == 0 and tcb == 0)):
                                if first:
                                    t.wait_ge(sA.h, bd1_allA)
                                    t.wait_ge(sG.h, bd1_allG)
                                if tcb == 0:
                                    t.wait_ge(sWs.h, wW)
                                if war is not None:
                                    t.wait_ge(war[0].h, war[1])
                                for ic in range(8):
                                    mmx = t.matmul(
                                        PM[bank][:, :],
                                        btile[buf][:, 128 * ic:128 * (ic + 1)],
                                        hbf[k][:, ic * T + 512 * tcb:
                                               ic * T + 512 * (tcb + 1)],
                                        start=(ic == 0), stop=(ic == 7))
                                mmx.then_inc(sT.h, 1)
                            mm = sT.inc()

                            @p.on("scalar")
                            def _(s, k=k, tcb=tcb, bank=bank, mm=mm,
                                  warm=msb_free.get((k, tcb))):
                                s.wait_ge(sT.h, mm)
                                if warm is not None:
                                    s.wait_ge(sT.h, warm)
                                s.activation(msb[k][:, 512 * tcb:512 * (tcb + 1)],
                                             PM[bank][:, :],
                                             AF.Copy).then_inc(sA.h, 1)
                            pm_free[bank] = (sA, sA.inc())
                        btile_free[buf] = sT.n
                        ld += 1
                    mix_done = sA.n
                    for tcb in range(4):
                        par = ucnt % 2

                        @p.on("tensor")
                        def _(t, oc=oc, tcb=tcb, par=par, md=mix_done,
                              war=pb_free[par]):
                            t.wait_ge(sA.h, md)
                            if war is not None:
                                t.wait_ge(sv.h, war)
                            t.matmul(PB[par][:, :],
                                     s2fc[0][:, 128 * oc:128 * (oc + 1)],
                                     msb[0][:, 512 * tcb:512 * (tcb + 1)],
                                     start=True, stop=False)
                            t.matmul(PB[par][:, :],
                                     s2fc[1][:, 128 * oc:128 * (oc + 1)],
                                     msb[1][:, 512 * tcb:512 * (tcb + 1)],
                                     start=False, stop=True).then_inc(sT.h, 1)
                        for k in range(2):
                            msb_free[(k, tcb)] = sT.n + 1
                        bd2_mm = sT.inc()

                        @p.on("vector")
                        def _(v, par=par, bd2_mm=bd2_mm, war=ub_free[par]):
                            v.wait_ge(sT.h, bd2_mm)
                            if war is not None:
                                v.wait_ge(sA.h, war)
                            v.tensor_single_scalar(ub[par][:, :], PB[par][:, :],
                                                   0.0, ALU.max).then_inc(sv.h, 1)
                        pb_free[par] = sv.inc()

                        @p.on("scalar")
                        def _(s, par=par, rl=pb_free[par], war=h2b_free[par]):
                            s.wait_ge(sv.h, rl)
                            if war is not None:
                                s.wait_ge(war[0].h, war[1])
                            s.activation(h2b[par][:, :], ub[par][:, :],
                                         AF.Square).then_inc(sA.h, 1)
                        ub_free[par] = sA.inc()

                        sSs = bS.slot(ucnt)

                        @p.on("gpsimd")
                        def _(g, oc=oc, tcb=tcb, par=par, sq=ub_free[par], sSs=sSs):
                            g.wait_ge(sA.h, sq)
                            g.dma_start(out=sp_h2[128 * oc:128 * (oc + 1),
                                                  512 * tcb:512 * (tcb + 1)],
                                        in_=h2b[par][:, :]).then_inc(sSs.h, 16)
                        h2b_free[par] = (sSs, sSs.inc(16))
                        ucnt += 1

                @p.on("gpsimd")
                def _(g, t0=bS.s[0].n, t1=bS.s[1].n):
                    g.wait_ge(bS.s[0].h, t0)
                    g.wait_ge(bS.s[1].h, t1)
                p.run()

            with ExitStack() as c1:
                E1 = c1.enter_context
                s1pj, s2pj = emit_s12(tag + "p", E1, "pj")
                S2, B2 = E1(sem_scope())
                h2c = E1(nc.sbuf_tensor(f"bk{tag}_hc", [128, NOC * 512], BF16))
                hbp = [E1(nc.sbuf_tensor(f"bk{tag}_hp{k}", [128, NOC * 512], BF16))
                       for k in range(2)]
                ptile = [E1(nc.sbuf_tensor(f"bk{tag}_v{i}", [128, NOC * 128], BF16))
                         for i in range(2)]
                mpj = [E1(nc.sbuf_tensor(f"bk{tag}_q{k}", [128, 8 * 512], BF16))
                       for k in range(2)]
                pjb = [E1(nc.sbuf_tensor(f"bk{tag}_j{i}", [128, 512], BF16))
                       for i in range(2)]
                PA = [E1(nc.psum_tensor(f"bk{tag}_pa{i}", [128, 512], F32))
                      for i in range(4)]
                PX = [E1(nc.psum_tensor(f"bk{tag}_px{i}", [128, 512], F32))
                      for i in range(2)]
                PB = [E1(nc.psum_tensor(f"bk{tag}_pc{i}", [128, 512], F32))
                      for i in range(2)]

                p = Ph(nc)
                sT = S2("bpt"); sA = S2("bpa"); sG = S2("bpg")
                bH = B2("bph"); bW = B2("bpw"); bS = B2("bps")

                pa_free = [None] * 4
                px_free = [None, None]
                pb_free = [None, None]
                ptile_free = [None, None]
                pjb_free = [None, None]
                h2c_free = None
                hbp_free = None
                mpj_free = None
                ld = 0
                for tcb in range(4):
                    sHs = bH.slot(tcb)

                    @p.on("sync")
                    def _(s, tcb=tcb, war=h2c_free, sHs=sHs):
                        if war is not None:
                            s.wait_ge(sT.h, war)
                        for ic in range(NOC):
                            s.dma_start(out=h2c[:, 512 * ic:512 * (ic + 1)],
                                        in_=sp_h2[128 * ic:128 * (ic + 1),
                                                  512 * tcb:512 * (tcb + 1)]
                                        ).then_inc(sHs.h, 16)
                    wH = sHs.inc(16 * NOC)

                    bd1k = []
                    bcnt = 2 * NOC * tcb
                    for k in range(2):
                        for pr in range(NOC):
                            bank = bcnt % 4

                            @p.on("tensor")
                            def _(t, k=k, pr=pr, bank=bank, wH=wH, sHs=sHs,
                                  first=(k == 0 and pr == 0), war=pa_free[bank]):
                                if first:
                                    t.wait_ge(sHs.h, wH)
                                if war is not None:
                                    t.wait_ge(war[0].h, war[1])
                                t.matmul(PA[bank][:, :],
                                         s1pj[k][:, 128 * pr:128 * (pr + 1)],
                                         h2c[:, 512 * pr:512 * (pr + 1)],
                                         start=True, stop=True).then_inc(sT.h, 1)
                            mm = sT.inc()

                            eng = "scalar"
                            esem = sA

                            @p.on(eng)
                            def _(e, k=k, pr=pr, bank=bank, mm=mm, war=hbp_free,
                                  esem=esem, sc=(eng == "scalar")):
                                e.wait_ge(sT.h, mm)
                                if war is not None:
                                    e.wait_ge(sT.h, war)
                                dst = hbp[k][:, 512 * pr:512 * (pr + 1)]
                                if sc:
                                    e.activation(dst, PA[bank][:, :],
                                                 AF.Copy).then_inc(esem.h, 1)
                                else:
                                    e.tensor_copy(dst, PA[bank][:, :]).then_inc(esem.h, 1)
                            pa_free[bank] = (esem, esem.inc())
                            bcnt += 1
                        bd1k.append((sA.n, sG.n))
                    h2c_free = sT.n
                    hbp_free = None

                    bpj_v = [b_pj[kk].rearrange("(i p) c -> p i c", p=128)
                             for kk in range(2)]
                    for k in range(2):
                        for oc in range(8):
                            buf = ld % 2
                            sWs = bW.slot(ld)

                            @p.on("sync")
                            def _(s, k=k, oc=oc, buf=buf, war=ptile_free[buf],
                                  sWs=sWs):
                                if war is not None:
                                    s.wait_ge(sT.h, war)
                                s.dma_start(
                                    out=ptile[buf][:, :].rearrange(
                                        "p (i c) -> p i c", c=128),
                                    in_=bpj_v[k][:, :, 128 * oc:128 * (oc + 1)]
                                ).then_inc(sWs.h, 16)
                            wW = sWs.inc(16)

                            bank = ld % 2

                            @p.on("tensor")
                            def _(t, k=k, oc=oc, buf=buf, bank=bank, wW=wW,
                                  sWs=sWs, war=px_free[bank], bd1=bd1k[k],
                                  first=(oc == 0)):
                                if first:
                                    t.wait_ge(sA.h, bd1[0])
                                    t.wait_ge(sG.h, bd1[1])
                                t.wait_ge(sWs.h, wW)
                                if war is not None:
                                    t.wait_ge(sA.h, war)
                                for ic in range(NOC):
                                    mmx = t.matmul(
                                        PX[bank][:, :],
                                        ptile[buf][:, 128 * ic:128 * (ic + 1)],
                                        hbp[k][:, 512 * ic:512 * (ic + 1)],
                                        start=(ic == 0), stop=(ic == NOC - 1))
                                mmx.then_inc(sT.h, 1)
                            mm = sT.inc()

                            @p.on("scalar")
                            def _(s, k=k, oc=oc, bank=bank, mm=mm, war=mpj_free):
                                s.wait_ge(sT.h, mm)
                                if war is not None:
                                    s.wait_ge(sT.h, war)
                                s.activation(mpj[k][:, 512 * oc:512 * (oc + 1)],
                                             PX[bank][:, :],
                                             AF.Copy).then_inc(sA.h, 1)
                            px_free[bank] = sA.inc()
                            ptile_free[buf] = sT.n
                            ld += 1
                    hbp_free = sT.n
                    mpj_free = None
                    mix_all = sA.n

                    for pr in range(8):
                        par = (8 * tcb + pr) % 2

                        @p.on("tensor")
                        def _(t, pr=pr, par=par, ma=mix_all, war=pb_free[par]):
                            t.wait_ge(sA.h, ma)
                            if war is not None:
                                t.wait_ge(sA.h, war)
                            t.matmul(PB[par][:, :],
                                     s2pj[0][:, 128 * pr:128 * (pr + 1)],
                                     mpj[0][:, 512 * pr:512 * (pr + 1)],
                                     start=True, stop=False)
                            t.matmul(PB[par][:, :],
                                     s2pj[1][:, 128 * pr:128 * (pr + 1)],
                                     mpj[1][:, 512 * pr:512 * (pr + 1)],
                                     start=False, stop=True).then_inc(sT.h, 1)
                        bd2_mm = sT.inc()

                        @p.on("scalar")
                        def _(s, par=par, bd2_mm=bd2_mm, war=pjb_free[par]):
                            s.wait_ge(sT.h, bd2_mm)
                            if war is not None:
                                s.wait_ge(war[0].h, war[1])
                            s.activation(pjb[par][:, :], PB[par][:, :],
                                         AF.Copy).then_inc(sA.h, 1)
                        pb_free[par] = sA.inc()

                        sSs = bS.slot(8 * tcb + pr)

                        @p.on("gpsimd")
                        def _(g, pr=pr, tcb=tcb, par=par, ev=pb_free[par], sSs=sSs):
                            g.wait_ge(sA.h, ev)
                            g.dma_start(out=pj_out_fm[128 * pr:128 * (pr + 1),
                                                      512 * tcb:512 * (tcb + 1)],
                                        in_=pjb[par][:, :]).then_inc(sSs.h, 16)
                        pjb_free[par] = (sSs, sSs.inc(16))
                    mpj_free = sT.n

                @p.on("gpsimd")
                def _(g, t0=bS.s[0].n, t1=bS.s[1].n):
                    g.wait_ge(bS.s[0].h, t0)
                    g.wait_ge(bS.s[1].h, t1)
                p.run()

        def emit_linattn(xa_tm, mac_fm):
            with ExitStack() as c1:
                E1 = c1.enter_context
                S2, B2 = c1.enter_context(sem_scope())
                xv = E1(nc.sbuf_tensor("la_xv", [128, 8 * T], BF16))
                wqkT = E1(nc.sbuf_tensor("la_wt", [128, 8 * 2 * D], BF16))
                sms = E1(nc.sbuf_tensor("la_sm", [128, FCH], F32))
                ksums = E1(nc.sbuf_tensor("la_ks", [128, 32], F32))
                kwa = E1(nc.sbuf_tensor("la_ka", [128, 8], F32))
                kwb = E1(nc.sbuf_tensor("la_kb", [128, 8], F32))
                kme8 = E1(nc.sbuf_tensor("la_km", [128, 8], F32))
                kmbs = E1(nc.sbuf_tensor("la_kc", [128, 16], BF16))
                zb = [E1(nc.sbuf_tensor(f"la_z{i}", [128, 512], F32)) for i in range(2)]
                eb = [E1(nc.sbuf_tensor(f"la_e{i}", [128, 512], F32)) for i in range(2)]
                rb = [E1(nc.sbuf_tensor(f"la_r{i}", [128, 512], F32)) for i in range(2)]
                qb = [E1(nc.sbuf_tensor(f"la_q{i}", [128, 512], BF16)) for i in range(2)]
                swp = [E1(nc.sbuf_tensor(f"la_s{i}", [128, 512], BF16)) for i in range(2)]
                rw1 = [E1(nc.sbuf_tensor(f"la_1{i}", [128, 512], F32)) for i in range(2)]
                rw2 = [E1(nc.sbuf_tensor(f"la_2{i}", [128, 512], F32)) for i in range(2)]
                qrb = [E1(nc.sbuf_tensor(f"la_o{i}", [128, 512], BF16)) for i in range(2)]
                xat = [E1(nc.sbuf_tensor(f"la_x{i}", [128, D], BF16)) for i in range(2)]
                krt = [E1(nc.sbuf_tensor(f"la_k{i}", [128, D], BF16,
                                         is_transpose_dest=True)) for i in range(2)]
                kvs = E1(nc.sbuf_tensor("la_kv", [128, 512], BF16))
                kvacc = E1(nc.sbuf_tensor("la_kf", [128, 512], F32))
                zf = E1(nc.sbuf_tensor("la_zf", [2, 512], F32))
                zf2 = E1(nc.sbuf_tensor("la_z2", [2, 512], F32))
                zrb = E1(nc.sbuf_tensor("la_zr", [2, 512], BF16))
                peb = E1(nc.sbuf_tensor("la_pe", [128, T], F32))
                pebt = E1(nc.sbuf_tensor("la_pt", [128, T], F32))
                azf = [E1(nc.sbuf_tensor(f"la_a{i}", [128, 512], F32)) for i in range(2)]
                ob = [E1(nc.sbuf_tensor(f"la_b{i}", [128, 512], BF16)) for i in range(2)]
                PK = [E1(nc.psum_tensor(f"la_pk{i}", [128, 512], F32)) for i in range(2)]
                PKV = [E1(nc.psum_tensor(f"la_pkv{i}", [128, 512], F32))
                       for i in range(2)]
                PD = E1(nc.psum_tensor("la_pd", [2, 512], F32))
                PZ = E1(nc.psum_tensor("la_pz", [128, 512], F32))
                PAT = [E1(nc.psum_tensor(f"la_pa{i}", [128, 512], F32)) for i in range(2)]

                p = Ph(nc)
                sT = S2("la_t"); sA = S2("la_a"); sv = S2("la_v")
                bX = B2("la_x"); bK = B2("la_kr"); bL = B2("la_l"); bO = B2("la_o")

                pk_free = [None, None]
                zbrb_free = [None, None]
                qb_free = [None, None]
                qrb_war = [None, None]
                pd_free = [None]
                pz_free = [None]
                pat_free = [None, None]
                azf_free = [None, None]
                ob_free = [None, None]
                krb_sp = {}

                @p.on("vector")
                def _(v):
                    v.memset(cst[:, 3:4], 0.0)
                    v.tensor_scalar_mul(sms[0:32, :], sinm[0:32, :], -1.0)
                    v.tensor_copy(sms[32:64, :], sinm[32:64, :])
                    v.tensor_scalar_mul(sms[64:96, :], sinm[64:96, :], -1.0)
                    v.tensor_copy(sms[96:128, :], sinm[96:128, :]).then_inc(sv.h, 1)
                prep_done = sv.inc()

                @p.on("sync")
                def _(s):
                    for fc in range(8):
                        for tg in range(4):
                            s.dma_start(
                                out=xv[:, fc * T + 512 * tg:fc * T + 512 * (tg + 1)],
                                in_=xa_tm[512 * tg:512 * (tg + 1),
                                          128 * fc:128 * (fc + 1)],
                                transpose=True).then_inc(bX.s[0].h, 16)
                    for ic in range(8):
                        for og in range(4):
                            s.dma_start(
                                out=wqkT[:, ic * 2 * D + 512 * og:
                                         ic * 2 * D + 512 * (og + 1)],
                                in_=a_Wqk[512 * og:512 * (og + 1),
                                          128 * ic:128 * (ic + 1)],
                                transpose=True).then_inc(bX.s[0].h, 16)
                bX.s[0].inc(16 * 64)
                tp_done = bX.s[0].n

                def proj_mm(oc, tcb, bank, extra_wait):
                    @p.on("tensor")
                    def _(t, oc=oc, tcb=tcb, bank=bank, extra_wait=extra_wait,
                          war=pk_free[bank]):
                        for (sem, val) in extra_wait:
                            t.wait_ge(sem.h, val)
                        if war is not None:
                            t.wait_ge(sv.h, war)
                        for ic in range(8):
                            mmx = t.matmul(
                                PK[bank][:, :],
                                wqkT[:, ic * 2 * D + 128 * oc:
                                     ic * 2 * D + 128 * (oc + 1)],
                                xv[:, ic * T + 512 * tcb:ic * T + 512 * (tcb + 1)],
                                start=(ic == 0), stop=(ic == 7))
                        mmx.then_inc(sT.h, 1)
                    return sT.inc()

                def elu(oc, bank, buf, mm, accum_col, war_q):
                    @p.on("vector")
                    def _(v, oc=oc, bank=bank, buf=buf, mm=mm,
                          war=zbrb_free[buf], prep=prep_done):
                        v.wait_ge(sT.h, mm)
                        v.wait_ge(sv.h, prep)
                        if war is not None:
                            v.wait_ge(sA.h, war)
                        v.tensor_scalar(zb[buf][:, :], PK[bank][:, :],
                                        bqk_c[:, oc:oc + 1], cst[:, 3:4],
                                        ALU.add, ALU.min)
                        v.tensor_scalar(rb[buf][:, :], PK[bank][:, :],
                                        bqk_c[:, oc:oc + 1], cst[:, 3:4],
                                        ALU.add, ALU.max).then_inc(sv.h, 1)
                    zr_done = sv.inc()
                    pk_free[bank] = zr_done

                    @p.on("scalar")
                    def _(s, buf=buf, zr_done=zr_done):
                        s.wait_ge(sv.h, zr_done)
                        s.activation(eb[buf][:, :], zb[buf][:, :],
                                     AF.Exp).then_inc(sA.h, 1)
                    e_done = sA.inc()
                    zbrb_free[buf] = e_done

                    @p.on("vector")
                    def _(v, buf=buf, e_done=e_done, accum_col=accum_col,
                          war_q=war_q):
                        v.wait_ge(sA.h, e_done)
                        if war_q is not None:
                            v.wait_ge(sT.h, war_q)
                        kw = {}
                        if accum_col is not None:
                            kw = dict(accum_out=ksums[:, accum_col:accum_col + 1])
                        v.scalar_tensor_tensor(qb[buf][:, :], eb[buf][:, :], 1.0,
                                               rb[buf][:, :], ALU.mult, ALU.add,
                                               **kw).then_inc(sv.h, 1)
                    return sv.inc()

                def rope_la(col, buf, src_done, dst, war_d):
                    @p.on("vector")
                    def _(v, col=col, buf=buf, src_done=src_done, base=sv.n,
                          dst=dst, war_d=war_d):
                        v.wait_ge(sv.h, src_done)
                        v.tensor_copy(swp[buf][0:32, :], qb[buf][32:64, :])
                        v.tensor_copy(swp[buf][32:64, :], qb[buf][0:32, :])
                        v.tensor_copy(swp[buf][64:96, :], qb[buf][96:128, :])
                        v.tensor_copy(swp[buf][96:128, :],
                                      qb[buf][64:96, :]).then_inc(sv.h, 1)
                        v.wait_ge(sv.h, base + 1)
                        v.tensor_scalar(rw1[buf][:, :], qb[buf][:, :],
                                        cosm[:, col:col + 1], None, ALU.mult)
                        v.tensor_scalar(rw2[buf][:, :], swp[buf][:, :],
                                        sms[:, col:col + 1], None,
                                        ALU.mult).then_inc(sv.h, 1)
                        v.wait_ge(sv.h, base + 2)
                        if war_d is not None:
                            v.wait_ge(war_d[0].h, war_d[1])
                        v.tensor_tensor(dst[:, :], rw1[buf][:, :], rw2[buf][:, :],
                                        ALU.add).then_inc(sv.h, 1)
                    return sv.inc(3)

                cnt = 0
                for kc in range(8):
                    for tcb in range(4):
                        bank = cnt % 2
                        buf = cnt % 2
                        ew = [(bX.s[0], tp_done)] if cnt == 0 else []
                        mm = proj_mm(8 + kc, tcb, bank, ew)
                        q_done = elu(8 + kc, bank, buf, mm, cnt, None)
                        sps = bK.slot(cnt)
                        kr_done = rope_la(kc, buf, q_done, qrb[buf],
                                          krb_sp.get(cnt - 2))

                        @p.on("gpsimd")
                        def _(g, kc=kc, tcb=tcb, buf=buf, kr_done=kr_done, sps=sps):
                            g.wait_ge(sv.h, kr_done)
                            g.dma_start(out=sp_kr[128 * kc:128 * (kc + 1),
                                                  512 * tcb:512 * (tcb + 1)],
                                        in_=qrb[buf][:, :]).then_inc(sps.h, 16)
                        krb_sp[cnt] = (sps, sps.inc(16))
                        cnt += 1
                kr_sp_tot = (bK.s[0].n, bK.s[1].n)
                qrb_war[0] = krb_sp[30]
                qrb_war[1] = krb_sp[31]
                zrb_free = [None]

                @p.on("vector")
                def _(v, base=sv.n):
                    v.wait_ge(sv.h, base)
                    v.tensor_tensor(kwa[:, :], ksums[:, 0:32:4], ksums[:, 1:32:4],
                                    ALU.add)
                    v.tensor_tensor(kwb[:, :], ksums[:, 2:32:4], ksums[:, 3:32:4],
                                    ALU.add).then_inc(sv.h, 1)
                    v.wait_ge(sv.h, base + 1)
                    v.tensor_tensor(kme8[:, :], kwa[:, :], kwb[:, :],
                                    ALU.add).then_inc(sv.h, 1)
                    v.wait_ge(sv.h, base + 2)
                    v.tensor_scalar_mul(kme8[:, :], kme8[:, :],
                                        1.0 / T).then_inc(sv.h, 1)
                    v.wait_ge(sv.h, base + 3)
                    for kc in range(8):
                        v.tensor_scalar(kmbs[:, 2 * kc:2 * kc + 2], ones2[:, 0:2],
                                        kme8[:, kc:kc + 1], None, ALU.mult)
                    v.engine_nop().then_inc(sv.h, 1)
                km_done = sv.inc(4)

                @p.on("vector")
                def _(v):
                    v.memset(kvacc[:, :], 0.0).then_inc(sv.h, 1)
                kv0_done = sv.inc()

                ld_free = [None, None]
                pkv_free = [None, None]
                prev_add = kv0_done
                for tc in range(16):
                    buf = tc % 2
                    bank = tc % 2
                    sls = bL.slot(tc)

                    @p.on("sync")
                    def _(s, tc=tc, buf=buf, war=ld_free[buf], sls=sls,
                          first=(tc == 0), kt=kr_sp_tot):
                        if first:
                            s.wait_ge(bK.s[0].h, kt[0])
                            s.wait_ge(bK.s[1].h, kt[1])
                        if war is not None:
                            s.wait_ge(sT.h, war)
                        s.dma_start(out=xat[buf][:, :],
                                    in_=xa_tm[128 * tc:128 * (tc + 1), :]
                                    ).then_inc(sls.h, 16)
                        for fh in range(2):
                            s.dma_start(out=krt[buf][:, 512 * fh:512 * (fh + 1)],
                                        in_=sp_kr[512 * fh:512 * (fh + 1),
                                                  128 * tc:128 * (tc + 1)],
                                        transpose=True).then_inc(sls.h, 16)
                    wl = sls.inc(16 * 3)

                    @p.on("tensor")
                    def _(t, tc=tc, buf=buf, bank=bank, wl=wl, sls=sls,
                          war=pkv_free[bank]):
                        t.wait_ge(sls.h, wl)
                        if war is not None:
                            t.wait_ge(sv.h, war)
                        for h in range(16):
                            mmx = t.matmul(PKV[bank][64 * (h % 2):64 * (h % 2) + 64,
                                               64 * (h // 2):64 * (h // 2) + 64],
                                           krt[buf][:, 64 * h:64 * (h + 1)],
                                           xat[buf][:, 64 * h:64 * (h + 1)],
                                           start=True, stop=True,
                                           tile_position=(0, 64 * (h % 2)))
                        mmx.then_inc(sT.h, 1)
                    ld_free[buf] = sT.inc()

                    @p.on("vector")
                    def _(v, bank=bank, mm=ld_free[buf], pa=prev_add):
                        v.wait_ge(sT.h, mm)
                        v.wait_ge(sv.h, pa)
                        v.tensor_tensor(kvacc[:, :], kvacc[:, :],
                                        PKV[bank][:, :], ALU.add).then_inc(sv.h, 1)
                    prev_add = sv.inc()
                    pkv_free[bank] = prev_add
                kv_sum = prev_add

                @p.on("scalar")
                def _(s, kv_sum=kv_sum):
                    s.wait_ge(sv.h, kv_sum)
                    s.activation(kvs[:, :], kvacc[:, :], AF.Copy,
                                 scale=1.0 / T).then_inc(sA.h, 1)
                kv_done = sA.inc()

                last_o = None
                for rc in range(8):
                    @p.on("vector")
                    def _(v, rc=rc, base=sv.n, tp=tp_done, lo=last_o):
                        if rc == 0:
                            v.wait_ge(bX.s[0].h, tp)
                        if lo is not None:
                            v.wait_ge(sv.h, lo)
                        v.tensor_scalar(peb[:, :], xv[:, rc * T:(rc + 1) * T],
                                        lwc[1][:, rc:rc + 1], lbc[:, rc:rc + 1],
                                        ALU.mult, ALU.add)
                        v.memset(pebt[:, 0:1], 0.0)
                        v.tensor_scalar(pebt[:, 1:T], xv[:, rc * T:rc * T + T - 1],
                                        lwc[0][:, rc:rc + 1], None,
                                        ALU.mult).then_inc(sv.h, 1)
                        v.wait_ge(sv.h, base + 1)
                        v.tensor_tensor(peb[:, :], peb[:, :], pebt[:, :],
                                        ALU.add).then_inc(sv.h, 1)
                        v.wait_ge(sv.h, base + 2)
                        v.memset(pebt[:, T - 1:T], 0.0)
                        v.tensor_scalar(pebt[:, 0:T - 1],
                                        xv[:, rc * T + 1:rc * T + T],
                                        lwc[2][:, rc:rc + 1], None,
                                        ALU.mult).then_inc(sv.h, 1)
                        v.wait_ge(sv.h, base + 3)
                        v.tensor_tensor(peb[:, :], peb[:, :], pebt[:, :],
                                        ALU.add).then_inc(sv.h, 1)
                    pe_done = sv.inc(4)

                    for tcb in range(4):
                        bank = cnt % 2
                        buf = cnt % 2
                        mm = proj_mm(rc, tcb, bank, [])
                        q_done = elu(rc, bank, buf, mm, None, qb_free[buf])

                        @p.on("tensor")
                        def _(t, rc=rc, buf=buf, q_done=q_done, war=pd_free[0],
                              km=km_done, first=(cnt == 32)):
                            t.wait_ge(sv.h, q_done)
                            if first:
                                t.wait_ge(sv.h, km)
                            if war is not None:
                                t.wait_ge(sA.h, war)
                            t.matmul(PD[0:2, :], kmbs[:, 2 * rc:2 * rc + 2],
                                     qb[buf][:, :], start=True,
                                     stop=True).then_inc(sT.h, 1)
                        den_mm = sT.inc()
                        qb_free[buf] = den_mm

                        @p.on("scalar")
                        def _(s, den_mm=den_mm, base=sA.n, warz=zrb_free[0]):
                            s.wait_ge(sT.h, den_mm)
                            s.activation(zf[0:2, :], PD[0:2, :], AF.Ln,
                                         bias=cst[0:2, 2:3]).then_inc(sA.h, 1)
                            s.wait_ge(sA.h, base + 1)
                            if warz is not None:
                                s.wait_ge(sT.h, warz)
                            s.activation(zrb[0:2, :], zf[0:2, :], AF.Exp,
                                         scale=-1.0).then_inc(sA.h, 1)
                        z_done = sA.inc(2)
                        pd_free[0] = z_done - 1

                        @p.on("tensor")
                        def _(t, z_done=z_done, war=pz_free[0]):
                            t.wait_ge(sA.h, z_done)
                            if war is not None:
                                t.wait_ge(sv.h, war)
                            t.matmul(PZ[:, :], ind2[0:2, :], zrb[0:2, :],
                                     start=True, stop=True).then_inc(sT.h, 1)
                        zb_mm = sT.inc()
                        zrb_free[0] = zb_mm

                        qr_done = rope_la(rc, buf, q_done, qrb[buf],
                                          qrb_war[buf])

                        @p.on("tensor")
                        def _(t, rc=rc, buf=buf, bank2=buf, qr_done=qr_done,
                              kv=kv_done, first=(cnt == 32),
                              war=pat_free[buf]):
                            t.wait_ge(sv.h, qr_done)
                            if first:
                                t.wait_ge(sA.h, kv)
                            if war is not None:
                                t.wait_ge(sA.h, war)
                            for hh in range(2):
                                mmx = t.matmul(
                                    PAT[bank2][64 * hh:64 * hh + 64, :],
                                    kvs[64 * hh:64 * hh + 64,
                                        64 * rc:64 * rc + 64],
                                    qrb[buf][64 * hh:64 * hh + 64, :],
                                    start=True, stop=True,
                                    tile_position=(64 * hh, 64 * hh))
                            mmx.then_inc(sT.h, 1)
                        at_mm = sT.inc()

                        @p.on("scalar")
                        def _(s, buf=buf, bank2=buf, at_mm=at_mm,
                              war=azf_free[buf]):
                            s.wait_ge(sT.h, at_mm)
                            if war is not None:
                                s.wait_ge(sv.h, war)
                            s.activation(azf[buf][:, :], PAT[bank2][:, :],
                                         AF.Copy).then_inc(sA.h, 1)
                        ev_done = sA.inc()
                        pat_free[buf] = ev_done

                        sos = bO.slot(cnt - 32)

                        @p.on("vector")
                        def _(v, rc=rc, tcb=tcb, buf=buf, ev_done=ev_done,
                              zb_mm=zb_mm, base=sv.n, wo=ob_free[buf]):
                            v.wait_ge(sA.h, ev_done)
                            v.wait_ge(sT.h, zb_mm)
                            v.tensor_tensor(azf[buf][:, :], azf[buf][:, :],
                                            PZ[:, :], ALU.mult).then_inc(sv.h, 1)
                            v.wait_ge(sv.h, base + 1)
                            if wo is not None:
                                v.wait_ge(wo[0].h, wo[1])
                            v.tensor_tensor(ob[buf][:, :], azf[buf][:, :],
                                            peb[:, 512 * tcb:512 * (tcb + 1)],
                                            ALU.add).then_inc(sv.h, 1)
                        o_done = sv.inc(2)
                        last_o = o_done
                        pz_free[0] = o_done - 1
                        azf_free[buf] = o_done
                        qrb_war[buf] = (sT, at_mm)

                        @p.on("gpsimd")
                        def _(g, rc=rc, tcb=tcb, buf=buf, o_done=o_done, sos=sos):
                            g.wait_ge(sv.h, o_done)
                            g.dma_start(out=mac_fm[128 * rc:128 * (rc + 1),
                                                   512 * tcb:512 * (tcb + 1)],
                                        in_=ob[buf][:, :]).then_inc(sos.h, 16)
                        ob_free[buf] = (sos, sos.inc(16))
                        cnt += 1

                @p.on("gpsimd")
                def _(g, t0=bO.s[0].n, t1=bO.s[1].n):
                    g.wait_ge(bO.s[0].h, t0)
                    g.wait_ge(bO.s[1].h, t1)
                p.run()

        def emit_ffn_attention(src_tm, x1_fm):
            ft = mk_tag("fa")
            with ExitStack() as c1:
                E1 = c1.enter_context
                wqT = E1(nc.sbuf_tensor(f"{ft}wqT", [128, 8 * D], BF16))
                wkT = E1(nc.sbuf_tensor(f"{ft}wkT", [128, 8 * 256], BF16))
                wvT = E1(nc.sbuf_tensor(f"{ft}wvT", [128, 8 * 256], BF16))
                woT = E1(nc.sbuf_tensor(f"{ft}woT", [128, 8 * D], BF16))
                xf = E1(nc.sbuf_tensor(f"{ft}xf", [128, 8 * T], BF16))
                kr = E1(nc.sbuf_tensor(f"{ft}kr", [128, 2 * T], BF16))
                kr_view = [kr[:, 0:T], kr[:, T:2 * T]]
                vo = E1(nc.sbuf_tensor(f"{ft}vo", [128, 16 * 320], BF16, is_transpose_dest=True))
                qn = E1(nc.sbuf_tensor(f"{ft}qn", [128, T], BF16))
                qr = E1(nc.sbuf_tensor(f"{ft}qr", [128, T], BF16))
                kn = E1(nc.sbuf_tensor(f"{ft}kn", [128, T], BF16))
                sq = E1(nc.sbuf_tensor(f"{ft}sqb", [128, 512], BF16))
                swp = E1(nc.sbuf_tensor(f"{ft}swp", [128, T], BF16))
                krw2 = [E1(nc.sbuf_tensor(f"{ft}krw{i}", [128, T], BF16)) for i in range(2)]
                rw4 = E1(nc.sbuf_tensor(f"{ft}rw", [128, 4 * 1024], F32))
                rwb4 = E1(nc.sbuf_tensor(f"{ft}rwb", [2, 4 * 512], BF16))
                pt = [E1(nc.sbuf_tensor(f"{ft}pt{i}", [128, 512], BF16)) for i in range(5)]
                ysb = E1(nc.sbuf_tensor(f"{ft}ysb", [128, 8 * T], BF16))
                yt = E1(nc.sbuf_tensor(f"{ft}yt", [128, 512], F32))
                x1b = [E1(nc.sbuf_tensor(f"{ft}x1b{i}", [128, 512], BF16)) for i in range(3)]
                rlr = E1(nc.sbuf_tensor(f"{ft}rlr", [1, 512], F32))
                rlb = E1(nc.sbuf_tensor(f"{ft}rlb", [1, 512], BF16))
                vtmp = [E1(nc.sbuf_tensor(f"{ft}vtmp{i}", [128, 512], BF16)) for i in range(2)]
                PP = E1(nc.psum_tensor(f"{ft}PP", [128, 512], F32))
                SS = [E1(nc.psum_tensor(f"{ft}SS{i}", [128, 512], F32)) for i in range(4)]
                YY = [E1(nc.psum_tensor(f"{ft}YY{i}", [128, 512], F32)) for i in range(2)]
                PRL = E1(nc.psum_tensor(f"{ft}PRL", [128, 512], F32))
                pbanks = [PP] + SS
                pfree = {i: None for i in range(5)}

                S2, B2 = c1.enter_context(sem_scope())
                p = Ph(nc)
                sW = S2("fa_w"); sX = S2("fa_x"); sV = S2("fa_v"); sVb = S2("fa_vb")
                sT_ = S2("fa_t"); sA = S2("fa_a"); sD = S2("fa_d")
                sDbg = S2("fa_dbg")
                bO = B2("fa_o", 3)

                @p.on("sync")
                def _(s):
                    for ic in range(8):
                        for og in range(2):
                            s.dma_start(out=wqT[:, ic * D + 512 * og:ic * D + 512 * (og + 1)],
                                        in_=a_wq[512 * og:512 * (og + 1), 128 * ic:128 * (ic + 1)],
                                        transpose=True).then_inc(sW.h, 16)
                            s.dma_start(out=woT[:, ic * D + 512 * og:ic * D + 512 * (og + 1)],
                                        in_=a_wo[512 * og:512 * (og + 1), 128 * ic:128 * (ic + 1)],
                                        transpose=True).then_inc(sW.h, 16)
                        s.dma_start(out=wkT[:, ic * 256:(ic + 1) * 256],
                                    in_=a_wk[:, 128 * ic:128 * (ic + 1)],
                                    transpose=True).then_inc(sW.h, 16)
                        s.dma_start(out=wvT[:, ic * 256:(ic + 1) * 256],
                                    in_=a_wv[:, 128 * ic:128 * (ic + 1)],
                                    transpose=True).then_inc(sW.h, 16)
                    for fc in range(8):
                        for tg in range(4):
                            s.dma_start(out=xf[:, fc * T + 512 * tg:fc * T + 512 * (tg + 1)],
                                        in_=src_tm[512 * tg:512 * (tg + 1),
                                                   128 * fc:128 * (fc + 1)],
                                        transpose=True).then_inc(sX.h, 16)
                sW.inc(16 * 48)
                sX.inc(16 * 32)
                w_done = sW.n
                x_done = sX.n

                prcount = [0]
                rwb_free = [None] * 4
                rw_free = [None] * 4
                pl_free = [None]

                def proj_norm(kind, cc, tcb, wT, wT_stride, gc_t, dst, dst_off,
                              vtmp_war=None):
                    myid = prcount[0]; prcount[0] += 1
                    bi = myid % 5
                    bank = pbanks[bi]
                    rs = myid % 4
                    t0 = 512 * tcb

                    @p.on("tensor")
                    def _(t, cc=cc, t0=t0, wT=wT, wT_stride=wT_stride, bank=bank,
                          war=pfree[bi], first=(myid == 0)):
                        if first:
                            t.wait_ge(sW.h, w_done)
                            t.wait_ge(sX.h, x_done)
                        if war is not None:
                            t.wait_ge(war[0].h, war[1])
                        for ic in range(8):
                            mm = t.matmul(bank[:, :], wT[:, ic * wT_stride + 128 * cc:
                                                         ic * wT_stride + 128 * (cc + 1)],
                                          xf[:, ic * T + t0:ic * T + t0 + 512],
                                          start=(ic == 0), stop=(ic == 7))
                        mm.then_inc(sT_.h, 1)
                    mm_done = sT_.inc()

                    @p.on("scalar")
                    def _(s, mm_done=mm_done, dst=dst, dst_off=dst_off, kind=kind,
                          bank=bank, vtmp_war=vtmp_war):
                        s.wait_ge(sT_.h, mm_done)
                        if vtmp_war is not None:
                            wl = vtmp_war if isinstance(vtmp_war, list) else [vtmp_war]
                            for wsem, wval in wl:
                                s.wait_ge(wsem.h, wval)
                        if kind == "v":
                            s.activation(dst[:, dst_off:dst_off + 512], bank[:, :],
                                         AF.Copy).then_inc(sA.h, 2)
                        else:
                            s.activation(sq[:, :], bank[:, :], AF.Square).then_inc(sA.h, 1)
                            s.activation(dst[:, dst_off:dst_off + 512], bank[:, :],
                                         AF.Copy).then_inc(sA.h, 1)
                    ev_done = sA.inc(2)
                    if kind == "v":
                        pfree[bi] = (sA, ev_done)
                        return ev_done

                    @p.on("tensor")
                    def _(t, ev_done=ev_done, bank=bank):
                        t.wait_ge(sA.h, ev_done)
                        t.matmul(bank[0:2, :], ones2[:, 0:2], sq[:, :],
                                 start=True, stop=True).then_inc(sT_.h, 1)
                    st_done = sT_.inc()

                    @p.on("scalar")
                    def _(s, st_done=st_done, base=sA.n, bank=bank, rs=rs,
                          rww=rw_free[rs]):
                        s.wait_ge(sT_.h, st_done)
                        if rww is not None:
                            s.wait_ge(sD.h, rww)
                        s.activation(rw4[0:2, rs * 1024:rs * 1024 + 512], bank[0:2, :],
                                     AF.Ln, bias=cst[0:2, 2:3],
                                     scale=1.0 / 64.0).then_inc(sA.h, 1)
                        s.wait_ge(sA.h, base + 1)
                        s.activation(rw4[0:2, rs * 1024 + 512:rs * 1024 + 1024],
                                     rw4[0:2, rs * 1024:rs * 1024 + 512], AF.Exp,
                                     scale=-0.5).then_inc(sA.h, 1)
                    ln_done = sA.inc(2)
                    pfree[bi] = (sA, ln_done - 1)

                    @p.on("vector")
                    def _(v, ln_done=ln_done, gc_t=gc_t, cc=cc, rs=rs,
                          rbw=rwb_free[rs]):
                        v.wait_ge(sA.h, ln_done)
                        if rbw is not None:
                            v.wait_ge(sT_.h, rbw)
                        v.tensor_scalar(rwb4[0:2, rs * 512:rs * 512 + 512],
                                        rw4[0:2, rs * 1024 + 512:rs * 1024 + 1024],
                                        gc_t[0:2, cc:cc + 1], None,
                                        ALU.mult).then_inc(sD.h, 1)
                    g_done = sD.inc()
                    rw_free[rs] = g_done

                    @p.on("tensor")
                    def _(t, g_done=g_done, rs=rs, plw=pl_free[0]):
                        t.wait_ge(sD.h, g_done)
                        if plw is not None:
                            t.wait_ge(sD.h, plw)
                        t.matmul(PRL[:, :], ind2[0:2, :],
                                 rwb4[0:2, rs * 512:rs * 512 + 512],
                                 start=True, stop=True).then_inc(sT_.h, 1)
                    bc_done = sT_.inc()
                    rwb_free[rs] = bc_done
                    return bc_done

                def rope_chunk(srcb, dstb, wval, width=T, war_t=None, war_dbg=None):
                    @p.on("vector")
                    def _(v, srcb=srcb, dstb=dstb, wval=wval, base=sD.n, width=width,
                          war_t=war_t, war_dbg=war_dbg):
                        v.wait_ge(sD.h, wval)
                        if war_t is not None:
                            v.wait_ge(sT_.h, war_t)
                        if war_dbg is not None:
                            v.wait_ge(sDbg.h, war_dbg)
                        v.tensor_copy(swp[0:32, 0:width], srcb[32:64, 0:width])
                        v.tensor_copy(swp[32:64, 0:width], srcb[0:32, 0:width])
                        v.tensor_copy(swp[64:96, 0:width], srcb[96:128, 0:width])
                        v.tensor_copy(swp[96:128, 0:width], srcb[64:96, 0:width]).then_inc(sD.h, 1)
                        v.wait_ge(sD.h, base + 1)
                        v.tensor_tensor(krw2[0][:, 0:width], srcb[:, 0:width],
                                        cos2[:, 0:width], ALU.mult)
                        v.tensor_tensor(krw2[1][:, 0:width], swp[:, 0:width],
                                        sin2[:, 0:width], ALU.mult).then_inc(sD.h, 1)
                        v.wait_ge(sD.h, base + 2)
                        v.tensor_tensor(dstb[:, 0:width], krw2[0][:, 0:width],
                                        krw2[1][:, 0:width], ALU.add).then_inc(sD.h, 1)
                    return sD.inc(3)

                rope_done = None
                for kc in range(2):
                    for tcb in range(4):
                        kn_war = (sD, rope_done) if (kc == 1 and tcb == 0) else None
                        bc_done = proj_norm("k", kc, tcb, wkT, 256, kgc, kn, 512 * tcb,
                                            vtmp_war=kn_war)

                        @p.on("vector")
                        def _(v, bc_done=bc_done, tcb=tcb):
                            v.wait_ge(sT_.h, bc_done)
                            v.tensor_tensor(kn[:, 512 * tcb:512 * (tcb + 1)],
                                            kn[:, 512 * tcb:512 * (tcb + 1)], PRL[:, :],
                                            ALU.mult).then_inc(sD.h, 1)
                        nm_done = sD.inc()
                    rope_done = rope_chunk(kn, kr_view[kc], nm_done)
                vcnt = 0
                for kc in range(2):
                    for tcb in range(4):
                        vbuf = vcnt % 2
                        sVs = sV if vbuf == 0 else sVb
                        vtmp_war = (sVs, sVs.n) if vcnt >= 2 else None
                        ev_done = proj_norm("v", kc, tcb, wvT, 256, None,
                                            vtmp[vbuf], 0, vtmp_war=vtmp_war)

                        @p.on("gpsimd")
                        def _(g, kc=kc, tcb=tcb, ev_done=ev_done, vbuf=vbuf, sVs=sVs):
                            g.wait_ge(sA.h, ev_done)
                            g.dma_start(out=sp_fm2[128 * kc:128 * (kc + 1),
                                                   512 * tcb:512 * (tcb + 1)],
                                        in_=vtmp[vbuf][:, 0:512]).then_inc(sVs.h, 16)
                        sVs.inc(16)
                        vcnt += 1
                v_sp_done = sV.n
                v_sp_doneb = sVb.n

                @p.on("vector")
                def _(v):
                    v.memset(vo[:, :], 1.0).then_inc(sD.h, 1)
                vo_ones = sD.inc()

                sVo = S2("fa_vo")

                @p.on("sync")
                def _(s):
                    s.wait_ge(sV.h, v_sp_done)
                    s.wait_ge(sVb.h, v_sp_doneb)
                    s.wait_ge(sD.h, vo_ones)
                    for kt in range(16):
                        for kv in range(4):
                            s.dma_start(
                                out=vo[:, kt * 320 + kv * 80:kt * 320 + kv * 80 + 64],
                                in_=sp_fm2[64 * kv:64 * (kv + 1), 128 * kt:128 * (kt + 1)],
                                transpose=True).then_inc(sVo.h, 16)
                sVo.inc(16 * 64)
                vo_done = sVo.n

                sG2 = S2("fa_g")
                slab_free = [None] * 5
                yy_free = [None, None]
                rlb_free = [None]
                yt_free = [None]
                prev_q_ready = None
                prev_attn_t = None
                dbg_q_done = None
                pending_tail = [None]

                def flush_tail():
                    if pending_tail[0] is not None:
                        pending_tail[0]()
                        pending_tail[0] = None

                def emit_av(kt, hh, slab, wv, first, last, kvh):
                    @p.on("tensor")
                    def _(t, kt=kt, hh=hh, slab=slab, wv=wv, first=first,
                          last=last, kvh=kvh, off=max(0, 0),
                          yyw=(yy_free[hh] if first else None)):
                        t.wait_ge(wv[0].h, wv[1])
                        if yyw is not None:
                            t.wait_ge(sA.h, yyw[0])
                            t.wait_ge(sA.h, yyw[1])
                        t.matmul(YY[hh][0:65, wv[2]:512],
                                 vo[:, kt * 320 + kvh * 80:
                                    kt * 320 + kvh * 80 + 65],
                                 pt[slab][:, wv[2]:512],
                                 start=first, stop=last).then_inc(sT_.h, 1)
                    slab_free[slab] = sT_.inc()

                for c in range(8):
                    ha, hb = QPERM[2 * c], QPERM[2 * c + 1]
                    kvh2 = (ha // 4, hb // 4)
                    flush_tail()
                    for tcb in range(4):
                        qn_war = []
                        if tcb == 0 and prev_q_ready is not None:
                            qn_war.append((sD, prev_q_ready))
                        if tcb == 0 and dbg_q_done is not None:
                            qn_war.append((sDbg, dbg_q_done))
                        bc_done = proj_norm("q", c, tcb, wqT, D, qgc, qn, 512 * tcb,
                                            vtmp_war=qn_war or None)

                        @p.on("vector")
                        def _(v, bc_done=bc_done, tcb=tcb):
                            v.wait_ge(sT_.h, bc_done)
                            v.tensor_tensor(qn[:, 512 * tcb:512 * (tcb + 1)],
                                            qn[:, 512 * tcb:512 * (tcb + 1)],
                                            PRL[:, :], ALU.mult).then_inc(sD.h, 1)
                        nm_done = sD.inc()
                    rope_chunk(qn, qr, nm_done, war_t=prev_attn_t,
                               war_dbg=dbg_q_done)
                    dbg_q_done = None
                    q_ready = sD.n
                    prev_q_ready = q_ready
                    if debug and c == 0:
                        @p.on("gpsimd")
                        def _(g, q_ready=q_ready):
                            g.wait_ge(sD.h, q_ready)
                            g.dma_start(out=dbg["qr"][:, :], in_=qr[:, :]).then_inc(sDbg.h, 16)
                            g.dma_start(out=dbg["qn"][:, :], in_=qn[:, :]).then_inc(sDbg.h, 16)
                        sDbg.inc(32)
                        dbg_q_done = sDbg.n

                    for qc in range(4):
                        q0 = 512 * qc
                        ktmax = 4 * (qc + 1)
                        pend = []
                        ucnt = 0
                        for kt in range(ktmax):
                            k0 = 128 * kt
                            off = max(0, k0 - q0)
                            diag = (k0 >= q0)
                            for hh in range(2):
                                kvh = kvh2[hh]
                                if ucnt == 3:
                                    flush_tail()
                                slab = ucnt % 5
                                sbi = slab

                                @p.on("tensor")
                                def _(t, kvh=kvh, kt=kt, q0=q0, hh=hh, slab=slab,
                                      q_ready=q_ready, war=pfree[sbi],
                                      first=(kt == 0)):
                                    t.wait_ge(sD.h, q_ready)
                                    if first:
                                        t.wait_ge(sVo.h, vo_done)
                                    if war is not None:
                                        t.wait_ge(war[0].h, war[1])
                                    t.matmul(pbanks[slab][:, :],
                                             kr[64 * hh:64 * hh + 64,
                                                (kvh // 2) * T + 128 * kt:
                                                (kvh // 2) * T + 128 * (kt + 1)],
                                             qr[64 * hh:64 * hh + 64, q0:q0 + 512],
                                             start=True, stop=True,
                                             tile_position=(64 * hh, 0)).then_inc(sT_.h, 1)
                                sc_done = sT_.inc()

                                @p.on("scalar")
                                def _(s, sc_done=sc_done, hh=hh, c=c, off=off,
                                      slab=slab, war=slab_free[slab]):
                                    s.wait_ge(sT_.h, sc_done)
                                    if war is not None:
                                        s.wait_ge(sT_.h, war)
                                    s.activation(pt[slab][:, off:512],
                                                 pbanks[slab][:, off:512], AF.Exp,
                                                 bias=negc[:, 2 * c + hh:2 * c + hh + 1]
                                                 ).then_inc(sA.h, 1)
                                exp_done = sA.inc()
                                pfree[sbi] = (sA, exp_done)

                                if diag:
                                    @p.on("gpsimd")
                                    def _(g, exp_done=exp_done, slab=slab, off=off):
                                        g.wait_ge(sA.h, exp_done)
                                        g.tensor_tensor(pt[slab][:, off:off + 128],
                                                        pt[slab][:, off:off + 128],
                                                        tri[:, :], ALU.mult).then_inc(sG2.h, 1)
                                    wv = (sG2, sG2.inc(), off)
                                else:
                                    wv = (sA, exp_done, off)

                                pend.append((kt, hh, slab, wv, kt == 0,
                                             kt == ktmax - 1, kvh))
                                if len(pend) > 3:
                                    emit_av(*pend.pop(0))
                                ucnt += 1
                        if pending_tail[0] is not None:
                            flush_tail()
                        for x in pend:
                            emit_av(*x)
                        av_done = sT_.n

                        def mk_tail(av_done=av_done, c=c, qc=qc):
                            def tail():
                                for hh in range(2):
                                    @p.on("scalar")
                                    def _(s, av_done=av_done, hh=hh, base=sA.n,
                                          rbw=rlb_free[0]):
                                        s.wait_ge(sT_.h, av_done)
                                        if rbw is not None:
                                            s.wait_ge(sT_.h, rbw)
                                        s.activation(rlr[0:1, :], YY[hh][64:65, :],
                                                     AF.Ln).then_inc(sA.h, 1)
                                        s.wait_ge(sA.h, base + 1)
                                        s.activation(rlb[0:1, :], rlr[0:1, :],
                                                     AF.Exp, scale=-1.0).then_inc(sA.h, 1)
                                    rl_done = sA.inc(2)

                                    @p.on("tensor")
                                    def _(t, rl_done=rl_done, plw=pl_free[0]):
                                        t.wait_ge(sA.h, rl_done)
                                        if plw is not None:
                                            t.wait_ge(sD.h, plw)
                                        t.matmul(PRL[0:64, :], onesr_bf[0:1, 0:64],
                                                 rlb[0:1, :], start=True,
                                                 stop=True).then_inc(sT_.h, 1)
                                    plb_done = sT_.inc()
                                    rlb_free[0] = plb_done

                                    @p.on("scalar")
                                    def _(s, av_done=av_done, hh=hh, ytw=yt_free[0]):
                                        s.wait_ge(sT_.h, av_done)
                                        if ytw is not None:
                                            s.wait_ge(sD.h, ytw)
                                        s.activation(yt[0:64, :], YY[hh][0:64, :],
                                                     AF.Copy).then_inc(sA.h, 1)
                                    yev_done = sA.inc()

                                    @p.on("vector")
                                    def _(v, plb_done=plb_done, yev_done=yev_done,
                                          hh=hh, c=c, qc=qc):
                                        v.wait_ge(sT_.h, plb_done)
                                        v.wait_ge(sA.h, yev_done)
                                        v.tensor_tensor(
                                            ysb[64 * hh:64 * hh + 64,
                                                c * T + 512 * qc:
                                                c * T + 512 * (qc + 1)],
                                            yt[0:64, :], PRL[0:64, :],
                                            ALU.mult).then_inc(sD.h, 1)
                                    yfin = sD.inc()
                                    pl_free[0] = yfin
                                    yt_free[0] = yfin
                                    yy_free[hh] = (yev_done, rl_done)
                            return tail
                        pending_tail[0] = mk_tail()
                    prev_attn_t = sT_.n
                flush_tail()
                y_all_done = sD.n

                if debug:
                    @p.on("gpsimd")
                    def _(g, y_all=y_all_done):
                        g.wait_ge(sD.h, y_all)
                        g.dma_start(out=dbg["kr"][:, :], in_=kr[:, :]).then_inc(sDbg.h, 16)
                        g.dma_start(out=dbg["y"][:, :], in_=ysb[:, :]).then_inc(sDbg.h, 16)
                        g.dma_start(out=dbg["vo"][:, :], in_=vo[:, :]).then_inc(sDbg.h, 16)
                    sDbg.inc(48)

                x1cnt = [0]
                for oc in range(8):
                    for tcb in range(4):
                        myid = x1cnt[0]; x1cnt[0] += 1
                        buf = myid % 3
                        bi = myid % 5
                        bank = pbanks[bi]

                        @p.on("tensor")
                        def _(t, oc=oc, tcb=tcb, y_all=y_all_done, bank=bank,
                              warp=pfree[bi]):
                            t.wait_ge(sD.h, y_all)
                            if warp is not None:
                                t.wait_ge(warp[0].h, warp[1])
                            for ic in range(8):
                                mm = t.matmul(bank[:, :],
                                              woT[:, ic * D + 128 * oc:ic * D + 128 * (oc + 1)],
                                              ysb[:, ic * T + 512 * tcb:ic * T + 512 * (tcb + 1)],
                                              start=(ic == 0), stop=(ic == 7))
                            mm.then_inc(sT_.h, 1)
                        wo_done = sT_.inc()

                        sOs = bO.slot(myid)

                        @p.on("vector")
                        def _(v, wo_done=wo_done, buf=buf, sOs=sOs, bank=bank,
                              war=(sOs.n if myid >= 3 else None)):
                            v.wait_ge(sT_.h, wo_done)
                            if war is not None:
                                v.wait_ge(sOs.h, war)
                            v.tensor_copy(x1b[buf][:, :], bank[:, :]).then_inc(sD.h, 1)
                        x1_done = sD.inc()
                        pfree[bi] = (sD, x1_done)

                        @p.on("gpsimd")
                        def _(g, x1_done=x1_done, oc=oc, tcb=tcb, buf=buf, sOs=sOs):
                            g.wait_ge(sD.h, x1_done)
                            g.dma_start(out=x1_fm[128 * oc:128 * (oc + 1),
                                                  512 * tcb:512 * (tcb + 1)],
                                        in_=x1b[buf][:, :]).then_inc(sOs.h, 16)
                        sOs.inc(16)

                @p.on("gpsimd")
                def _(g, t0=bO.s[0].n, t1=bO.s[1].n, t2=bO.s[2].n, td=sDbg.n):
                    g.wait_ge(bO.s[0].h, t0)
                    g.wait_ge(bO.s[1].h, t1)
                    g.wait_ge(bO.s[2].h, t2)
                    if td:
                        g.wait_ge(sDbg.h, td)
                p.run()

        def emit_merge(res_in, term_fm, res_out, scale, xf_tm=None, hn_out=None,
                       ln_out=None, attin_out=None):
            mt = mk_tag("mg")
            with ExitStack() as c1:
                E1 = c1.enter_context
                xr = [E1(nc.sbuf_tensor(f"{mt}_x{i}", [128, D], F32)) for i in range(2)]
                tt_ = [E1(nc.sbuf_tensor(f"{mt}_t{i}", [128, D], BF16, is_transpose_dest=True))
                       for i in range(2)]
                xfb = [E1(nc.sbuf_tensor(f"{mt}_f{i}", [128, D], BF16)) for i in range(2)]
                x1f = E1(nc.sbuf_tensor(f"{mt}_1", [128, D], F32))
                sxn = [E1(nc.sbuf_tensor(f"{mt}_o{i}", [128, D], F32)) for i in range(2)]
                ob = [E1(nc.sbuf_tensor(f"{mt}_b{i}", [128, D], BF16)) for i in range(2)]
                ob2 = [E1(nc.sbuf_tensor(f"{mt}_c{i}", [128, D], BF16)) for i in range(2)]
                stt = [E1(nc.sbuf_tensor(f"{mt}_s{i}", [128, 12], F32)) for i in range(2)]
                sqs = E1(nc.sbuf_tensor(f"{mt}_q", [128, D], F32))

                S2, B2 = c1.enter_context(sem_scope())
                p = Ph(nc)
                sv = S2("mg_v"); sa = S2("mg_a")
                bl = B2("mg_l")
                bo = B2("mg_o")
                apply_done = []
                store_done = []
                n_store = (1 + (1 if hn_out else 0) + (1 if ln_out else 0)
                           + (1 if attin_out else 0))
                x1f_free = [None]
                for tc in range(TCH):
                    buf = tc % 2
                    war_load = apply_done[tc - 2] if tc >= 2 else None
                    sls = bl.slot(tc)
                    sos = bo.slot(tc)
                    war_store = sos.n if tc >= 2 else None

                    @p.on("sync")
                    def _(s, tc=tc, buf=buf, war=war_load, sls=sls):
                        if war is not None:
                            s.wait_ge(sv.h, war)
                        s.dma_start(out=xr[buf][:, :],
                                    in_=res_in[128 * tc:128 * (tc + 1), :]).then_inc(sls.h, 16)
                        for fg in range(2):
                            s.dma_start(out=tt_[buf][:, 512 * fg:512 * (fg + 1)],
                                        in_=term_fm[512 * fg:512 * (fg + 1),
                                                    128 * tc:128 * (tc + 1)],
                                        transpose=True).then_inc(sls.h, 16)
                        if xf_tm is not None:
                            s.dma_start(out=xfb[buf][:, :],
                                        in_=xf_tm[128 * tc:128 * (tc + 1), :]).then_inc(sls.h, 16)
                    wl = sls.inc(16 * (4 if xf_tm is not None else 3))

                    base = sv.n

                    @p.on("vector")
                    def _(v, buf=buf, wl=wl, base=base, war=war_store, sls=sls, sos=sos,
                          x1w=x1f_free[0]):
                        v.wait_ge(sls.h, wl)
                        if war is not None:
                            v.wait_ge(sos.h, war)
                        if xf_tm is not None:
                            if x1w is not None:
                                v.wait_ge(sv.h, x1w)
                            v.tensor_tensor(x1f[:, :], xfb[buf][:, :], tt_[buf][:, :],
                                            ALU.add).then_inc(sv.h, 1)
                            v.wait_ge(sv.h, base + 1)
                            v.scalar_tensor_tensor(sxn[buf][:, :], x1f[:, :], scale,
                                                   xr[buf][:, :], ALU.mult,
                                                   ALU.add).then_inc(sv.h, 1)
                            v.scalar_tensor_tensor(sqs[:, :], x1f[:, :], 1.0, x1f[:, :],
                                                   ALU.mult, ALU.mult,
                                                   accum_out=stt[buf][:, 0:1]).then_inc(sv.h, 1)
                        else:
                            v.scalar_tensor_tensor(sxn[buf][:, :], tt_[buf][:, :], scale,
                                                   xr[buf][:, :], ALU.mult,
                                                   ALU.add).then_inc(sv.h, 1)
                            if ln_out is not None:
                                v.wait_ge(sv.h, base + 1)
                                v.reduce_sum(stt[buf][:, 1:2], sxn[buf][:, :],
                                             AX.X).then_inc(sv.h, 1)
                                v.scalar_tensor_tensor(sqs[:, :], sxn[buf][:, :], 1.0,
                                                       sxn[buf][:, :], ALU.mult, ALU.mult,
                                                       accum_out=stt[buf][:, 0:1]
                                                       ).then_inc(sv.h, 1)
                    nv = 3 if (xf_tm is not None or ln_out is not None) else 1
                    sv.inc(nv)
                    base2 = sv.n

                    if xf_tm is not None:
                        @p.on("vector")
                        def _(v, buf=buf, base2=base2):
                            v.wait_ge(sv.h, base2)
                            v.tensor_scalar_mul(stt[buf][:, 2:3], stt[buf][:, 0:1],
                                                1.0 / D).then_inc(sv.h, 1)
                        sv.inc(1)
                        b3 = sv.n

                        @p.on("scalar")
                        def _(s, buf=buf, b3=b3, ba=sa.n):
                            s.wait_ge(sv.h, b3)
                            s.activation(stt[buf][:, 3:4], stt[buf][:, 2:3], AF.Ln,
                                         bias=cst[:, 2:3]).then_inc(sa.h, 1)
                            s.wait_ge(sa.h, ba + 1)
                            s.activation(stt[buf][:, 4:5], stt[buf][:, 3:4], AF.Exp,
                                         scale=-0.5).then_inc(sa.h, 1)
                        sa.inc(2)
                        b4 = sa.n

                        @p.on("vector")
                        def _(v, buf=buf, b4=b4):
                            v.wait_ge(sa.h, b4)
                            v.tensor_scalar(ob[buf][:, :], x1f[:, :], stt[buf][:, 4:5],
                                            None, ALU.mult).then_inc(sv.h, 1)
                        sv.inc(1)
                        x1f_free[0] = sv.n
                    elif ln_out is not None:
                        @p.on("vector")
                        def _(v, buf=buf, base2=base2):
                            v.wait_ge(sv.h, base2)
                            v.tensor_scalar_mul(stt[buf][:, 2:3], stt[buf][:, 1:2], 1.0 / D)
                            v.tensor_scalar_mul(stt[buf][:, 3:4], stt[buf][:, 0:1],
                                                1.0 / D).then_inc(sv.h, 1)
                            v.wait_ge(sv.h, base2 + 1)
                            v.scalar_tensor_tensor(stt[buf][:, 4:5], stt[buf][:, 2:3],
                                                   -1.0, stt[buf][:, 2:3], ALU.mult,
                                                   ALU.mult).then_inc(sv.h, 1)
                            v.wait_ge(sv.h, base2 + 2)
                            v.tensor_tensor(stt[buf][:, 4:5], stt[buf][:, 3:4],
                                            stt[buf][:, 4:5], ALU.add).then_inc(sv.h, 1)
                        sv.inc(3)
                        b3 = sv.n

                        @p.on("scalar")
                        def _(s, buf=buf, b3=b3, ba=sa.n, dual=(attin_out is not None)):
                            s.wait_ge(sv.h, b3)
                            s.activation(stt[buf][:, 5:6], stt[buf][:, 4:5], AF.Ln,
                                         bias=cst[:, 1:2]).then_inc(sa.h, 1)
                            s.wait_ge(sa.h, ba + 1)
                            s.activation(stt[buf][:, 6:7], stt[buf][:, 5:6], AF.Exp,
                                         scale=-0.5).then_inc(sa.h, 1)
                            if dual:
                                s.activation(stt[buf][:, 7:8], stt[buf][:, 5:6], AF.Exp,
                                             scale=-1.0).then_inc(sa.h, 1)
                        sa.inc(3 if attin_out is not None else 2)
                        b4 = sa.n

                        @p.on("vector")
                        def _(v, buf=buf, b4=b4, bb=sv.n, dual=(attin_out is not None)):
                            v.wait_ge(sa.h, b4)
                            v.scalar_tensor_tensor(stt[buf][:, 8:9], stt[buf][:, 2:3],
                                                   -1.0, stt[buf][:, 6:7], ALU.mult,
                                                   ALU.mult).then_inc(sv.h, 1)
                            v.wait_ge(sv.h, bb + 1)
                            v.tensor_scalar(ob[buf][:, :], sxn[buf][:, :], stt[buf][:, 6:7],
                                            stt[buf][:, 8:9], ALU.mult,
                                            ALU.add).then_inc(sv.h, 1)
                            if dual:
                                v.tensor_scalar(stt[buf][:, 9:10], stt[buf][:, 4:5],
                                                stt[buf][:, 7:8], cst[:, 2:3], ALU.mult,
                                                ALU.add).then_inc(sv.h, 1)
                        sv.inc(3 if attin_out is not None else 2)
                        b5 = sv.n

                        if attin_out is not None:
                            @p.on("scalar")
                            def _(s, buf=buf, b5=b5, ba=sa.n):
                                s.wait_ge(sv.h, b5)
                                s.activation(stt[buf][:, 10:11], stt[buf][:, 9:10], AF.Ln
                                             ).then_inc(sa.h, 1)
                                s.wait_ge(sa.h, ba + 1)
                                s.activation(stt[buf][:, 11:12], stt[buf][:, 10:11], AF.Exp,
                                             scale=-0.5).then_inc(sa.h, 1)
                            sa.inc(2)
                            b6 = sa.n

                            @p.on("vector")
                            def _(v, buf=buf, b6=b6, bb=sv.n):
                                v.wait_ge(sa.h, b6)
                                v.tensor_scalar(ob2[buf][:, :], ob[buf][:, :],
                                                stt[buf][:, 11:12], None,
                                                ALU.mult).then_inc(sv.h, 1)
                            sv.inc(1)

                    apply_done.append(sv.n)
                    wv2 = sv.n

                    @p.on("gpsimd")
                    def _(g, tc=tc, buf=buf, wv2=wv2, sos=sos):
                        g.wait_ge(sv.h, wv2)
                        g.dma_start(out=res_out[128 * tc:128 * (tc + 1), :],
                                    in_=sxn[buf][:, :]).then_inc(sos.h, 16)
                        if hn_out is not None:
                            g.dma_start(out=hn_out[128 * tc:128 * (tc + 1), :],
                                        in_=ob[buf][:, :]).then_inc(sos.h, 16)
                        if ln_out is not None:
                            g.dma_start(out=ln_out[128 * tc:128 * (tc + 1), :],
                                        in_=ob[buf][:, :]).then_inc(sos.h, 16)
                        if attin_out is not None:
                            g.dma_start(out=attin_out[128 * tc:128 * (tc + 1), :],
                                        in_=ob2[buf][:, :]).then_inc(sos.h, 16)
                    sos.inc(16 * n_store)

                @p.on("gpsimd")
                def _(g, t0=bo.s[0].n, t1=bo.s[1].n):
                    g.wait_ge(bo.s[0].h, t0)
                    g.wait_ge(bo.s[1].h, t1)
                p.run()

        def emit_ln_dual(src_f32, dst_ln, dst_attin):
            mt = mk_tag("ld")
            with ExitStack() as c1:
                E1 = c1.enter_context
                xc = [E1(nc.sbuf_tensor(f"{mt}_x{i}", [128, D], F32)) for i in range(2)]
                ob = [E1(nc.sbuf_tensor(f"{mt}_o{i}", [128, D], BF16)) for i in range(2)]
                ob2 = [E1(nc.sbuf_tensor(f"{mt}_c{i}", [128, D], BF16)) for i in range(2)]
                stt = [E1(nc.sbuf_tensor(f"{mt}_s{i}", [128, 12], F32)) for i in range(2)]
                sqs = E1(nc.sbuf_tensor(f"{mt}_q", [128, D], F32))

                S2, B2 = c1.enter_context(sem_scope())
                p = Ph(nc)
                sv = S2("ld_v"); sa = S2("ld_a")
                bl = B2("ld_l")
                bo = B2("ld_o")
                apply_done = []
                for tc in range(TCH):
                    buf = tc % 2
                    war_load = apply_done[tc - 2] if tc >= 2 else None
                    sls = bl.slot(tc)
                    sos = bo.slot(tc)
                    war_store = sos.n if tc >= 2 else None

                    @p.on("sync")
                    def _(s, tc=tc, buf=buf, war=war_load, sls=sls):
                        if war is not None:
                            s.wait_ge(sv.h, war)
                        s.dma_start(out=xc[buf][:, :],
                                    in_=src_f32[128 * tc:128 * (tc + 1), :]).then_inc(sls.h, 16)
                    wl = sls.inc(16)
                    base = sv.n

                    @p.on("vector")
                    def _(v, buf=buf, wl=wl, base=base, war=war_store, sls=sls, sos=sos):
                        v.wait_ge(sls.h, wl)
                        if war is not None:
                            v.wait_ge(sos.h, war)
                        v.reduce_sum(stt[buf][:, 1:2], xc[buf][:, :], AX.X).then_inc(sv.h, 1)
                        v.scalar_tensor_tensor(sqs[:, :], xc[buf][:, :], 1.0, xc[buf][:, :],
                                               ALU.mult, ALU.mult,
                                               accum_out=stt[buf][:, 0:1]).then_inc(sv.h, 1)
                        v.wait_ge(sv.h, base + 2)
                        v.tensor_scalar_mul(stt[buf][:, 2:3], stt[buf][:, 1:2], 1.0 / D)
                        v.tensor_scalar_mul(stt[buf][:, 3:4], stt[buf][:, 0:1],
                                            1.0 / D).then_inc(sv.h, 1)
                        v.wait_ge(sv.h, base + 3)
                        v.scalar_tensor_tensor(stt[buf][:, 4:5], stt[buf][:, 2:3], -1.0,
                                               stt[buf][:, 2:3], ALU.mult,
                                               ALU.mult).then_inc(sv.h, 1)
                        v.wait_ge(sv.h, base + 4)
                        v.tensor_tensor(stt[buf][:, 4:5], stt[buf][:, 3:4], stt[buf][:, 4:5],
                                        ALU.add).then_inc(sv.h, 1)
                    sv.inc(5)
                    b3 = sv.n

                    @p.on("scalar")
                    def _(s, buf=buf, b3=b3, ba=sa.n):
                        s.wait_ge(sv.h, b3)
                        s.activation(stt[buf][:, 5:6], stt[buf][:, 4:5], AF.Ln,
                                     bias=cst[:, 1:2]).then_inc(sa.h, 1)
                        s.wait_ge(sa.h, ba + 1)
                        s.activation(stt[buf][:, 6:7], stt[buf][:, 5:6], AF.Exp,
                                     scale=-0.5).then_inc(sa.h, 1)
                        s.activation(stt[buf][:, 7:8], stt[buf][:, 5:6], AF.Exp,
                                     scale=-1.0).then_inc(sa.h, 1)
                    sa.inc(3)
                    b4 = sa.n

                    @p.on("vector")
                    def _(v, buf=buf, b4=b4, bb=sv.n):
                        v.wait_ge(sa.h, b4)
                        v.scalar_tensor_tensor(stt[buf][:, 8:9], stt[buf][:, 2:3], -1.0,
                                               stt[buf][:, 6:7], ALU.mult,
                                               ALU.mult).then_inc(sv.h, 1)
                        v.tensor_scalar(stt[buf][:, 9:10], stt[buf][:, 4:5],
                                        stt[buf][:, 7:8], cst[:, 2:3], ALU.mult,
                                        ALU.add).then_inc(sv.h, 1)
                        v.wait_ge(sv.h, bb + 2)
                        v.tensor_scalar(ob[buf][:, :], xc[buf][:, :], stt[buf][:, 6:7],
                                        stt[buf][:, 8:9], ALU.mult, ALU.add).then_inc(sv.h, 1)
                    sv.inc(3)
                    b5 = sv.n

                    @p.on("scalar")
                    def _(s, buf=buf, b5=b5, ba=sa.n):
                        s.wait_ge(sv.h, b5)
                        s.activation(stt[buf][:, 10:11], stt[buf][:, 9:10],
                                     AF.Ln).then_inc(sa.h, 1)
                        s.wait_ge(sa.h, ba + 1)
                        s.activation(stt[buf][:, 11:12], stt[buf][:, 10:11], AF.Exp,
                                     scale=-0.5).then_inc(sa.h, 1)
                    sa.inc(2)
                    b6 = sa.n

                    @p.on("vector")
                    def _(v, buf=buf, b6=b6):
                        v.wait_ge(sa.h, b6)
                        v.tensor_scalar(ob2[buf][:, :], ob[buf][:, :], stt[buf][:, 11:12],
                                        None, ALU.mult).then_inc(sv.h, 1)
                    sv.inc(1)
                    apply_done.append(sv.n)
                    wv2 = sv.n

                    @p.on("gpsimd")
                    def _(g, tc=tc, buf=buf, wv2=wv2, sos=sos):
                        g.wait_ge(sv.h, wv2)
                        g.dma_start(out=dst_ln[128 * tc:128 * (tc + 1), :],
                                    in_=ob[buf][:, :]).then_inc(sos.h, 16)
                        g.dma_start(out=dst_attin[128 * tc:128 * (tc + 1), :],
                                    in_=ob2[buf][:, :]).then_inc(sos.h, 16)
                    sos.inc(32)

                @p.on("gpsimd")
                def _(g, t0=bo.s[0].n, t1=bo.s[1].n):
                    g.wait_ge(bo.s[0].h, t0)
                    g.wait_ge(bo.s[1].h, t1)
                p.run()

        full_chain = False
        if stop_after != "p0":
            emit_ln_dual(x_in, sp_tm2, sp_tm)
            if stop_after != "p1":
                emit_ffn_attention(sp_tm, sp_fm1)
                emit_merge(x_in, sp_fm1, s_x, 0.5, xf_tm=sp_tm2, hn_out=sp_hn)
                if stop_after != "p2":
                    emit_banks("b1", sp_hn, sp_fm2)
                    emit_merge(s_x, sp_fm2, s_x, 1.0, ln_out=sp_tm2)
                    if stop_after != "p3":
                        emit_linattn(sp_tm2, sp_fm1)
                        emit_merge(s_x, sp_fm1, s_x, 1.0, ln_out=sp_tm2,
                                   attin_out=sp_tm)
                        if stop_after != "p4":
                            emit_ffn_attention(sp_tm, sp_fm1)
                            emit_merge(s_x, sp_fm1, s_x, 0.5, xf_tm=sp_tm2,
                                       hn_out=sp_hn)
                            emit_banks("b2", sp_hn, sp_fm2)
                            emit_merge(s_x, sp_fm2, x_out, 1.0)
                            full_chain = True

        if debug:
            p = Ph(nc)
            sd = S("dbg")
            dbg_row = E(nc.sbuf_tensor("dbg_row", [1, 16], F32))

            @p.on("vector")
            def _(v):
                v.tensor_copy(dbg_row[0:1, 0:16], idxr[0:1, 0:16]).then_inc(sd.h, 1)

            @p.on("sync")
            def _(s):
                s.wait_ge(sd.h, 1)
                s.dma_start(out=dbg["xf"][:, :], in_=sp_tm[:, :]).then_inc(sd.h, 16)
                s.dma_start(out=dbg["bfc0"][:, :], in_=b_fc[0][:, 0:256]).then_inc(sd.h, 16)
                s.dma_start(out=dbg["bsel"][:, :], in_=dbg_row[0:1, :]).then_inc(sd.h, 16)
                s.dma_start(out=dbg["att"][:, :], in_=sp_fm1[:, :]).then_inc(sd.h, 16)
                s.dma_start(out=dbg["sx"][:, :], in_=s_x[:, :]).then_inc(sd.h, 16)
                s.dma_start(out=dbg["hn"][:, :], in_=sp_hn[:, :]).then_inc(sd.h, 16)
                s.dma_start(out=dbg["awq"][:, :], in_=a_wq[:, :]).then_inc(sd.h, 16)
                s.dma_start(out=dbg["awk"][:, :], in_=a_wk[:, :]).then_inc(sd.h, 16)
                s.dma_start(out=dbg["h2"][:, :], in_=sp_h2[0:128, :]).then_inc(sd.h, 16)
                s.dma_start(out=dbg["pj"][:, :], in_=sp_fm2[:, :]).then_inc(sd.h, 16)
                s.dma_start(out=dbg["bpj0"][:, :],
                            in_=b_pj[0][0:128, 0:512]).then_inc(sd.h, 16)
                s.dma_start(out=dbg["xa"][:, :], in_=sp_tm2[:, :]).then_inc(sd.h, 16)
                s.dma_start(out=dbg["mac"][:, :], in_=sp_fm1[:, :]).then_inc(sd.h, 16)
                s.wait_ge(sd.h, 209)
            p.run()

        if not full_chain:
            p = Ph(nc)
            sf = S("fin")

            @p.on("sync")
            def _(s):
                s.dma_start(out=x_out[:, :], in_=x_in[:, :]).then_inc(sf.h, 16)
                s.wait_ge(sf.h, 16)
            p.run()

    return nc


def _finish_stub(nc, ctx, S, x_out, x_in, dbg, sp_tm, b_fc, idxr, E, debug):
    p = Ph(nc)
    sf = S("stub")

    @p.on("sync")
    def _(s):
        s.dma_start(out=x_out[:, :], in_=x_in[:, :]).then_inc(sf.h, 16)
        s.wait_ge(sf.h, 16)
    p.run()
    return nc


def emit_norm_pass(nc, ctx, S, E, src_f32, dst_bf, epscol, kind="ln"):
    p = Ph(nc)
    sl = S(f"n_{id(p) % 9999}_l")
    sv = S(f"n_{id(p) % 9999}_v")
    sa = S(f"n_{id(p) % 9999}_a")
    so = S(f"n_{id(p) % 9999}_o")
    with ExitStack() as c0:
        E0 = c0.enter_context
        xc = [E0(nc.sbuf_tensor(f"nx{i}_{id(p) % 9999}", [128, D], F32)) for i in range(2)]
        ob = [E0(nc.sbuf_tensor(f"no{i}_{id(p) % 9999}", [128, D], BF16)) for i in range(2)]
        stt = [E0(nc.sbuf_tensor(f"ns{i}_{id(p) % 9999}", [128, 8], F32)) for i in range(2)]
        sq = E0(nc.sbuf_tensor(f"nq_{id(p) % 9999}", [128, D], F32))

        apply_done = []
        store_done = []

        for tc in range(TCH):
            buf = tc % 2
            war_load = apply_done[tc - 2] if tc >= 2 else None
            war_store = store_done[tc - 2] if tc >= 2 else None

            @p.on("sync")
            def _(s, tc=tc, buf=buf, war=war_load):
                if war is not None:
                    s.wait_ge(sv.h, war)
                s.dma_start(out=xc[buf][:, :],
                            in_=src_f32[128 * tc:128 * (tc + 1), :]).then_inc(sl.h, 16)
            sl.inc(16)
            wl = sl.n

            base = sv.n

            @p.on("vector")
            def _(v, tc=tc, buf=buf, wl=wl, base=base, war=war_store):
                v.wait_ge(sl.h, wl)
                if war is not None:
                    v.wait_ge(so.h, war)
                if kind == "ln":
                    v.reduce_sum(stt[buf][:, 0:1], xc[buf][:, :], AX.X).then_inc(sv.h, 1)
                v.scalar_tensor_tensor(sq[:, :], xc[buf][:, :], 1.0, xc[buf][:, :],
                                       ALU.mult, ALU.mult,
                                       accum_out=stt[buf][:, 1:2]).then_inc(sv.h, 1)
            nred = 2 if kind == "ln" else 1
            sv.inc(nred)
            base2 = sv.n

            @p.on("vector")
            def _(v, buf=buf, base2=base2):
                v.wait_ge(sv.h, base2)
                if kind == "ln":
                    v.tensor_scalar_mul(stt[buf][:, 2:3], stt[buf][:, 0:1], 1.0 / D)
                    v.tensor_scalar_mul(stt[buf][:, 3:4], stt[buf][:, 1:2], 1.0 / D).then_inc(sv.h, 1)
                else:
                    v.tensor_scalar_mul(stt[buf][:, 3:4], stt[buf][:, 1:2], 1.0 / D).then_inc(sv.h, 1)
            sv.inc(1)
            base3 = sv.n

            @p.on("vector")
            def _(v, buf=buf, base3=base3):
                v.wait_ge(sv.h, base3)
                if kind == "ln":
                    v.scalar_tensor_tensor(stt[buf][:, 4:5], stt[buf][:, 2:3], -1.0,
                                           stt[buf][:, 2:3], ALU.mult, ALU.mult).then_inc(sv.h, 1)
                else:
                    v.tensor_copy(stt[buf][:, 4:5], stt[buf][:, 3:4]).then_inc(sv.h, 1)
            sv.inc(1)
            base4 = sv.n

            @p.on("vector")
            def _(v, buf=buf, base4=base4, isln=(kind == "ln")):
                v.wait_ge(sv.h, base4)
                if isln:
                    v.tensor_tensor(stt[buf][:, 4:5], stt[buf][:, 3:4], stt[buf][:, 4:5],
                                    ALU.add).then_inc(sv.h, 1)
                else:
                    v.engine_nop().then_inc(sv.h, 1)
            sv.inc(1)
            base5 = sv.n

            @p.on("scalar")
            def _(s, buf=buf, base5=base5, basea=sa.n):
                s.wait_ge(sv.h, base5)
                s.activation(stt[buf][:, 5:6], stt[buf][:, 4:5], AF.Ln,
                             bias=epscol).then_inc(sa.h, 1)
                s.wait_ge(sa.h, basea + 1)
                s.activation(stt[buf][:, 6:7], stt[buf][:, 5:6], AF.Exp, scale=-0.5).then_inc(sa.h, 1)
            sa.inc(2)
            base6 = sa.n

            @p.on("vector")
            def _(v, buf=buf, base6=base6, bs=sv.n, isln=(kind == "ln")):
                v.wait_ge(sa.h, base6)
                if isln:
                    v.scalar_tensor_tensor(stt[buf][:, 7:8], stt[buf][:, 2:3], -1.0,
                                           stt[buf][:, 6:7], ALU.mult, ALU.mult).then_inc(sv.h, 1)
                    v.wait_ge(sv.h, bs + 1)
                    v.tensor_scalar(ob[buf][:, :], xc[buf][:, :], stt[buf][:, 6:7],
                                    stt[buf][:, 7:8], ALU.mult, ALU.add).then_inc(sv.h, 1)
                else:
                    v.engine_nop().then_inc(sv.h, 1)
                    v.tensor_scalar(ob[buf][:, :], xc[buf][:, :], stt[buf][:, 6:7], None,
                                    ALU.mult).then_inc(sv.h, 1)
            sv.inc(2)
            apply_done.append(sv.n)
            wv2 = sv.n

            @p.on("gpsimd")
            def _(g, tc=tc, buf=buf, wv2=wv2):
                g.wait_ge(sv.h, wv2)
                g.dma_start(out=dst_bf[128 * tc:128 * (tc + 1), :],
                            in_=ob[buf][:, :]).then_inc(so.h, 16)
            so.inc(16)
            store_done.append(so.n)

        @p.on("gpsimd")
        def _(g, tot=so.n):
            g.wait_ge(so.h, tot)
        p.run()


_CACHE = {}


def kernel(**inputs):
    if "nc" not in _CACHE:
        _CACHE["nc"] = build(debug=False)
    nc = _CACHE["nc"]
    x = np.ascontiguousarray(np.asarray(inputs["L0_attn_x"], dtype=np.float32))
    shared = {
        "W_qk": inputs["L0_attn_W_qk"], "b_qk": inputs["L0_attn_b_qk"],
        "lepe_w": inputs["L0_attn_lepe_w"], "lepe_b": inputs["L0_attn_lepe_b"],
        "wq": inputs["L0_ffn_wq"], "wk": inputs["L0_ffn_wk"], "wv": inputs["L0_ffn_wv"],
        "wo": inputs["L0_ffn_wo"], "q_gain": inputs["L0_ffn_q_gain"],
        "k_gain": inputs["L0_ffn_k_gain"],
        "fc_s1": inputs["L0_ffn_fc_s1"], "fc_s2": inputs["L0_ffn_fc_s2"],
        "fc_mix": inputs["L0_ffn_fc_mix"], "fc_logits": inputs["L0_ffn_fc_logits"],
        "pj_s1": inputs["L0_ffn_pj_s1"], "pj_s2": inputs["L0_ffn_pj_s2"],
        "pj_mix": inputs["L0_ffn_pj_mix"], "pj_logits": inputs["L0_ffn_pj_logits"],
    }
    shared = {k: np.ascontiguousarray(np.asarray(v, dtype=np.float32)) for k, v in shared.items()}
    in_maps = [dict(shared, x=np.ascontiguousarray(x[c])) for c in range(8)]
    res = run_bass_kernel_spmd(nc, in_maps, core_ids=list(range(8)))
    return np.stack([r["y"] for r in res.results], axis=0)
